# version 02
# baseline (speedup 1.0000x reference)
import numpy as np
from contextlib import ExitStack
import concourse.bass as bass
import concourse.mybir as mybir
from concourse.bass import IndirectOffsetOnAxis
from concourse.bass_utils import run_bass_kernel_spmd

F32 = mybir.dt.float32
U32 = mybir.dt.uint32
AF = mybir.ActivationFunctionType
ALU = mybir.AluOpType
AX = mybir.AxisListType

NC = 8
T = 16384
D = 1024
DIN = 2048
S = 16
DTR = 64
E = 8
CAP = 5120
TL = T // NC
DL = DIN // NC
NITER = 4
TT = 1024
NJ = T // TT
NB = TL // 128
NG = 10
SQD = float(np.sqrt(1024.0))
OOB = 1.0e9
BIG = 2097152.0


def build_program(mode="full"):
    nc = bass.Bass("TRN2", target_bir_lowering=False, debug=False)

    def ein(name, shape, dtype=F32):
        return nc.dram_tensor(name, list(shape), dtype, kind="ExternalInput").ap()

    X = ein("x_shard", (TL, D))
    WIN = ein("w_in_loc", (D, 2 * DL))
    CW = ein("conv_w_loc", (DL, 4))
    CB = ein("conv_b_loc", (DL, 1))
    WX = ein("w_x_loc", (DL, 96))
    WDT = ein("w_dt_loc", (DTR, DL))
    BDT = ein("b_dt_loc", (DL, 1))
    ALOG = ein("a_log_loc", (DL, S))
    DP = ein("dp_loc", (DL, 1))
    WOUT = ein("w_out", (DIN, D))
    WGT = ein("w_gT", (D, E))
    CFC = ein("c_fc_loc", (D, 4 * D))
    CPR = ein("c_proj_loc", (4 * D, D))
    TOKID = ein("tokid", (TL, 1))
    PREV = ein("prevmask", (NC, 1))
    RKOF = ein("rankoff", (1, 1))

    OUT = nc.dram_tensor("out", [TL, D], F32, kind="ExternalOutput").ap()
    if mode == "dbg":
        DBG_LN2 = nc.dram_tensor("dbg_ln2", [TL, D], F32, kind="ExternalOutput").ap()
        DBG_TOK = nc.dram_tensor("dbg_tok", [E * CAP, 1], F32, kind="ExternalOutput").ap()
        DBG_RANK = nc.dram_tensor("dbg_rank", [128, 32], F32, kind="ExternalOutput").ap()
        DBG_PROB = nc.dram_tensor("dbg_prob", [128, 32], F32, kind="ExternalOutput").ap()
        DBG_ESEL = nc.dram_tensor("dbg_esel", [128, 32], F32, kind="ExternalOutput").ap()
        DBG_VAL = nc.dram_tensor("dbg_val", [128, 32], F32, kind="ExternalOutput").ap()
        DBG_EOUT = nc.dram_tensor("dbg_eout", [CAP, D], F32, kind="ExternalOutput").ap()
        DBG_X2 = nc.dram_tensor("dbg_x2", [TL, D], F32, kind="ExternalOutput").ap()

    def idr(name, shape, dtype=F32):
        return nc.dram_tensor(name, list(shape), dtype, kind="Internal").ap()

    cur_dram = idr("cur_dram", (TL, D))
    x2_dram = idr("x2_dram", (TL, D))
    z_in = idr("z_in", (D, TL))
    z_ag = idr("z_ag", (NC * D, TL))
    dbl_in = idr("dbl_in", (96, T))
    dbl_out = idr("dbl_out", (96, T))
    u_dram = idr("u_dram", (DL, T))
    res_dram = idr("res_dram", (DL, T))
    y_in = idr("y_in", (DL, T))
    y_ag = idr("y_ag", (DIN, T))
    ln2_in = idr("ln2_in", (TL, D))
    ln2_ag = idr("ln2_ag", (T, D))
    cnt0_in = idr("cnt0_in", (1, E))
    cnt0_ag = idr("cnt0_ag", (NC, E))
    cnt1_in = idr("cnt1_in", (1, E))
    cnt1_ag = idr("cnt1_ag", (NC, E))
    tokmap_in = idr("tokmap_in", (E * CAP, 1))
    tokmap_ar = idr("tokmap_ar", (E * CAP, 1))
    eout_in = idr("eout_in", (CAP, D))
    eout_ag = idr("eout_ag", (E * CAP, D))

    rg = [list(range(NC))]

    ops = []
    RB = {}

    def O(eng, fn, inc=1):
        ops.append((eng, fn, inc))

    def DMA(fn):
        O("sync", fn, 16)

    def GDMA(fn):
        O("gpsimd", fn, 16)

    def CC(fn):
        O("gpsimd", fn, 1)

    with ExitStack() as ctx:
        def sb(name, shape, dtype=F32):
            return ctx.enter_context(nc.sbuf_tensor(name, shape, dtype))

        def psum(name, shape, dtype=F32):
            return ctx.enter_context(nc.psum_tensor(name, shape, dtype))

        big1 = sb("big1", [128, S * TT])
        big2 = sb("big2", [128, 8 * TT])
        big3 = sb("big3", [128, 4224])
        w_in_sb = sb("w_in_sb", [128, 8 * 512])
        w_x_sb = sb("w_x_sb", [128, 2 * 96])
        w_dt_sb = sb("w_dt_sb", [DTR, DL])
        conv_sb = sb("conv_sb", [128, 8])
        convb_sb = sb("convb_sb", [128, 2])
        bdt_sb = sb("bdt_sb", [128, 2])
        acol_sb = sb("acol_sb", [128, 2 * S])
        dp_sb = sb("dp_sb", [128, 2])
        wgt_sb = sb("wgt_sb", [128, 8 * E])
        ones_sb = sb("ones_sb", [1, 128])
        ident_sb = sb("ident_sb", [128, 128])
        iota8_sb = sb("iota8_sb", [128, 8])
        tokid_sb = sb("tokid_sb", [128, NB])
        prev_sb = sb("prev_sb", [NC, 1])
        ones8_sb = sb("ones8_sb", [NC, 1])
        rkof_sb = sb("rkof_sb", [1, 1])
        rank_col = sb("rank_col", [128, 1])
        ebase_col = sb("ebase_col", [128, 1])
        idx2_f = sb("idx2_f", [128, 1])
        idx2_u = sb("idx2_u", [128, 1], U32)
        idx_f = sb("idx_f", [128, 1])
        idx_u = sb("idx_u", [128, 1], U32)
        xc_b = sb("xc_b", [128, TT])
        a_buf = xc_b
        u_b = sb("u_b", [128, TT])
        res2_b = sb("res2_b", [128, TT])
        zdt_sb = sb("zdt_sb", [128, TT])
        del_sb = sb("del_sb", [128, TT])
        du_sb = sb("du_sb", [128, TT])
        tmp_sb = sb("tmp_sb", [128, TT])
        y_sb = sb("y_sb", [128, TT])
        dtv_sb = y_sb[0:DTR, :]
        stageB = sb("stageB", [1, TT])
        stageC = sb("stageC", [1, TT])
        carry = sb("carry", [128, S])
        xblk = sb("xblk", [128, D])
        sq_blk = sb("sq_blk", [128, D])
        zt_blk = sb("zt_blk", [128, D])
        stat = sb("stat", [128, 8])
        ztr_ev = sb("ztr_ev", [128, 128])
        ev_sb = sb("ev_sb", [128, 512])
        wo_sb = sb("wo_sb", [128, 128])
        lg_sb = sb("lg_sb", [128, E])
        mx_sb = sb("mx_sb", [128, 8])
        ix_sb = sb("ix_sb", [128, 8], U32)
        ixf_sb = sb("ixf_sb", [128, 8])
        ind_t = sb("ind_t", [128, E])
        ind_em = [big1[0:E, k * TL:(k + 1) * TL] for k in range(2)]
        cum_em = [big1[0:E, 2 * TL + k * TL:2 * TL + (k + 1) * TL]
                  for k in range(2)]
        zero_em = big1[0:E, 4 * TL:5 * TL]
        pk_em = big1[0:E, 5 * TL:5 * TL + 128]
        cnts_sb = sb("cnts_sb", [NC, E])
        base_sb = sb("base_sb", [E, 2])
        rank_sb = sb("rank_sb", [128, 2 * NB])
        valid_sb = sb("valid_sb", [128, 2 * NB])
        probs_sb = sb("probs_sb", [128, 2 * NB])
        esel_sb = sb("esel_sb", [128, 2 * NB])
        slot_sb = sb("slot_sb", [128, 2 * NB])
        fill_sb = sb("fill_sb", [128, 320])
        cfc_sb = sb("cfc_sb", [128, 128])
        cst_sb = sb("cst_sb", [128, 8])

        pmA = psum("pmA", [128, 512])
        pmB = psum("pmB", [128, 512])
        pmR1 = psum("pmR1", [128, TT])
        pmR2 = psum("pmR2", [128, TT])
        pmT1 = psum("pmT1", [128, 128])
        pmT2 = psum("pmT2", [128, 128])

        NSEG = 12
        ordv = [ctx.enter_context(nc.semaphore(f"ordv{i}")) for i in range(NSEG)]
        block = ctx.enter_context(nc.Block())

        CONSTS = [1e-5, 1e-30, 0.5, -0.5, -1.0, 1.0, 0.0]
        for ci, cv in enumerate(CONSTS):
            nc.const_aps.register(F32, cv, cst_sb[:, ci:ci + 1])
            O("vector", lambda e, ci=ci, cv=cv: e.memset(
                cst_sb[:, ci:ci + 1], cv))
        DMA(lambda e: e.dma_start(
            out=w_in_sb[:], in_=WIN.rearrange("(k p) c -> p k c", p=128)))
        DMA(lambda e: e.dma_start(
            out=w_x_sb[:], in_=WX.rearrange("(d p) c -> p d c", p=128)))
        DMA(lambda e: e.dma_start(out=w_dt_sb[:], in_=WDT))
        DMA(lambda e: e.dma_start(
            out=conv_sb[:], in_=CW.rearrange("(d p) c -> p d c", p=128)))
        DMA(lambda e: e.dma_start(
            out=convb_sb[:], allow_slow_non_contiguous=True,
            in_=CB.rearrange("(d p) c -> p d c", p=128)[:, :, 0]))
        DMA(lambda e: e.dma_start(
            out=bdt_sb[:], allow_slow_non_contiguous=True,
            in_=BDT.rearrange("(d p) c -> p d c", p=128)[:, :, 0]))
        DMA(lambda e: e.dma_start(
            out=acol_sb[:], in_=ALOG.rearrange("(d p) s -> p d s", p=128)))
        DMA(lambda e: e.dma_start(
            out=dp_sb[:], allow_slow_non_contiguous=True,
            in_=DP.rearrange("(d p) c -> p d c", p=128)[:, :, 0]))
        DMA(lambda e: e.dma_start(
            out=wgt_sb[:], in_=WGT.rearrange("(k p) c -> p k c", p=128)))
        DMA(lambda e: e.dma_start(
            out=tokid_sb[:], allow_slow_non_contiguous=True,
            in_=TOKID.rearrange("(b p) c -> p b c", p=128)[:, :, 0]))
        DMA(lambda e: e.dma_start(out=prev_sb[:], in_=PREV))
        DMA(lambda e: e.dma_start(out=rkof_sb[:], in_=RKOF))
        O("scalar", lambda e: e.activation(out=acol_sb[:], in_=acol_sb[:],
                                           func=AF.Exp))
        O("vector", lambda e: e.tensor_scalar(
            out=acol_sb[:], in0=acol_sb[:], scalar1=-1.0, scalar2=None,
            op0=ALU.mult))
        O("vector", lambda e: e.memset(ones_sb[:], 1.0))
        O("vector", lambda e: e.memset(ones8_sb[:], 1.0))
        O("gpsimd", lambda e: e.iota(out=ident_sb[:], pattern=[[1, 128]],
                                     base=0, channel_multiplier=-1,
                                     allow_small_or_imprecise_dtypes=True))
        O("vector", lambda e: e.tensor_scalar(
            out=ident_sb[:], in0=ident_sb[:], scalar1=0.0, scalar2=None,
            op0=ALU.is_equal))
        O("gpsimd", lambda e: e.iota(out=iota8_sb[:], pattern=[[1, 8]],
                                     base=0, channel_multiplier=0,
                                     allow_small_or_imprecise_dtypes=True))
        O("vector", lambda e: e.memset(fill_sb[:], OOB))
        O("tensor", lambda e: e.matmul(out=pmT2[:, 0:1], lhsT=ones_sb[:],
                                       rhs=rkof_sb[:], start=True, stop=True))
        O("gpsimd", lambda e: e.iota(out=rank_col[:], pattern=[[1, 1]],
                                     base=0, channel_multiplier=32,
                                     allow_small_or_imprecise_dtypes=True))
        O("vector", lambda e: e.scalar_tensor_tensor(
            out=rank_col[:], in0=pmT2[:, 0:1], scalar=4.0, in1=rank_col[:],
            op0=ALU.mult, op1=ALU.add))
        O("gpsimd", lambda e: e.iota(out=ebase_col[:], pattern=[[1, 1]],
                                     base=0, channel_multiplier=1,
                                     allow_small_or_imprecise_dtypes=True))
        O("vector", lambda e: e.scalar_tensor_tensor(
            out=ebase_col[:], in0=pmT2[:, 0:1], scalar=float(CAP),
            in1=ebase_col[:], op0=ALU.mult, op1=ALU.add))

        def rms_of(src_tile, dst_tile):
            O("vector", lambda e: e.tensor_mul(sq_blk[:], src_tile[:], src_tile[:]))
            O("vector", lambda e: e.tensor_reduce(
                out=stat[:, 0:1], in_=sq_blk[:], axis=AX.X, op=ALU.add))
            O("scalar", lambda e: e.activation(
                out=stat[:, 3:4], in_=stat[:, 0:1], func=AF.Ln, bias=1e-30))
            O("scalar", lambda e: e.activation(
                out=stat[:, 3:4], in_=stat[:, 3:4], func=AF.Exp, scale=0.5))
            O("vector", lambda e: e.tensor_scalar(
                out=stat[:, 3:4], in0=stat[:, 3:4], scalar1=1e-12,
                scalar2=None, op0=ALU.max))
            O("vector", lambda e: e.reciprocal(out=stat[:, 4:5],
                                               in_=stat[:, 3:4]))
            O("vector", lambda e: e.tensor_scalar(
                out=stat[:, 4:5], in0=stat[:, 4:5], scalar1=SQD,
                scalar2=None, op0=ALU.mult))
            O("vector", lambda e: e.tensor_scalar(
                out=dst_tile[:], in0=src_tile[:], scalar1=stat[:, 4:5],
                scalar2=None, op0=ALU.mult))

        def ln_of(src_tile, dst_tile):
            O("vector", lambda e: e.tensor_mul(sq_blk[:], src_tile[:], src_tile[:]))
            O("vector", lambda e: e.tensor_reduce(
                out=stat[:, 0:1], in_=sq_blk[:], axis=AX.X, op=ALU.add))
            O("vector", lambda e: e.tensor_reduce(
                out=stat[:, 1:2], in_=src_tile[:], axis=AX.X, op=ALU.add))
            O("vector", lambda e: e.tensor_scalar(
                out=stat[:, 1:2], in0=stat[:, 1:2], scalar1=1.0 / D,
                scalar2=None, op0=ALU.mult))
            O("vector", lambda e: e.tensor_scalar(
                out=stat[:, 0:1], in0=stat[:, 0:1], scalar1=1.0 / D,
                scalar2=None, op0=ALU.mult))
            O("vector", lambda e: e.tensor_mul(stat[:, 2:3], stat[:, 1:2],
                                               stat[:, 1:2]))
            O("vector", lambda e: e.tensor_sub(stat[:, 0:1], stat[:, 0:1],
                                               stat[:, 2:3]))
            O("scalar", lambda e: e.activation(
                out=stat[:, 3:4], in_=stat[:, 0:1], func=AF.Ln, bias=1e-5))
            O("scalar", lambda e: e.activation(
                out=stat[:, 3:4], in_=stat[:, 3:4], func=AF.Exp, scale=-0.5))
            O("vector", lambda e: e.tensor_mul(stat[:, 4:5], stat[:, 1:2],
                                               stat[:, 3:4]))
            O("vector", lambda e: e.tensor_scalar(
                out=stat[:, 4:5], in0=stat[:, 4:5], scalar1=-1.0,
                scalar2=None, op0=ALU.mult))
            O("vector", lambda e: e.tensor_scalar(
                out=dst_tile[:], in0=src_tile[:], scalar1=stat[:, 3:4],
                scalar2=stat[:, 4:5], op0=ALU.mult, op1=ALU.add))

        def x_in_t(dt):
            return big3[:, dt * 1056: dt * 1056 + TT + 32]

        def res_t(dt):
            return big3[:, 2112 + dt * 1056: 2112 + dt * 1056 + TT]

        for it in range(NITER):
            for b in range(NB):
                src = X if it == 0 else cur_dram
                DMA(lambda e, src=src, b=b: e.dma_start(
                    out=xblk[:], in_=src[b * 128:(b + 1) * 128, :]))
                if it == 0:
                    ln_of(xblk, zt_blk)
                    O("vector", lambda e: e.tensor_copy(out=xblk[:],
                                                        in_=zt_blk[:]))
                rms_of(xblk, zt_blk)
                for m in range(8):
                    O("tensor", lambda e, m=m: e.transpose(
                        out=pmT1[:], in_=zt_blk[:, m * 128:(m + 1) * 128],
                        identity=ident_sb[:]))
                    O("vector", lambda e: e.tensor_copy(out=ztr_ev[:],
                                                        in_=pmT1[:]))
                    DMA(lambda e, b=b, m=m: e.dma_start(
                        out=z_in[m * 128:(m + 1) * 128,
                                 b * 128:(b + 1) * 128],
                        in_=ztr_ev[:]))
            CC(lambda e: e.collective_compute(
                "AllGather", ALU.bypass, replica_groups=rg,
                ins=[z_in.opt()], outs=[z_ag.opt()]))

            for j in range(NJ):
                t0 = j * TT
                r = t0 // TL
                lc = t0 % TL
                DMA(lambda e, r=r, lc=lc: e.dma_start(
                    out=big2[:],
                    in_=z_ag[r * D:(r + 1) * D, lc:lc + TT]
                    .rearrange("(k p) t -> p k t", p=128)))
                for cti in range(4):
                    dt = cti % 2
                    is_res = cti >= 2
                    for ns in range(2):
                        pm = pmA if ns == 0 else pmB
                        for kt in range(8):
                            O("tensor", lambda e, cti=cti, ns=ns, kt=kt, pm=pm:
                              e.matmul(
                                  out=pm[:],
                                  lhsT=w_in_sb[:, kt * 512 + cti * 128:
                                               kt * 512 + (cti + 1) * 128],
                                  rhs=big2[:, kt * TT + ns * 512:
                                           kt * TT + (ns + 1) * 512],
                                  start=(kt == 0), stop=(kt == 7)))
                    for ns in range(2):
                        pm = pmA if ns == 0 else pmB
                        if is_res:
                            O("vector", lambda e, dt=dt, ns=ns, pm=pm:
                              e.tensor_copy(
                                  out=res_t(dt)[:, ns * 512:(ns + 1) * 512],
                                  in_=pm[:]))
                        else:
                            O("vector", lambda e, dt=dt, ns=ns, pm=pm:
                              e.tensor_copy(
                                  out=x_in_t(dt)[:, 4 + ns * 512:
                                                 4 + (ns + 1) * 512],
                                  in_=pm[:]))
                for dt in range(2):
                    if j == 0:
                        O("vector", lambda e, dt=dt: e.memset(
                            x_in_t(dt)[:, 0:4], 0.0))
                    O("vector", lambda e, dt=dt: e.tensor_scalar(
                        out=xc_b[:], in0=x_in_t(dt)[:, 1:1 + TT],
                        scalar1=conv_sb[:, dt * 4: dt * 4 + 1],
                        scalar2=convb_sb[:, dt:dt + 1],
                        op0=ALU.mult, op1=ALU.add))
                    for k in range(1, 4):
                        O("vector", lambda e, dt=dt, k=k:
                          e.scalar_tensor_tensor(
                              out=xc_b[:], in0=x_in_t(dt)[:, 1 + k:1 + k + TT],
                              scalar=conv_sb[:, dt * 4 + k: dt * 4 + k + 1],
                              in1=xc_b[:], op0=ALU.mult, op1=ALU.add))
                    O("scalar", lambda e: e.activation(
                        out=u_b[:], in_=xc_b[:], func=AF.Silu))
                    O("vector", lambda e, dt=dt: e.tensor_copy(
                        out=x_in_t(dt)[:, 0:4],
                        in_=x_in_t(dt)[:, TT:TT + 4]))
                    DMA(lambda e, dt=dt, t0=t0: e.dma_start(
                        out=u_dram[dt * 128:(dt + 1) * 128, t0:t0 + TT],
                        in_=u_b[:]))
                    DMA(lambda e, dt=dt, t0=t0: e.dma_start(
                        out=res_dram[dt * 128:(dt + 1) * 128, t0:t0 + TT],
                        in_=res_t(dt)[:]))
                    for ns in range(2):
                        pm = pmA if ns == 0 else pmB
                        O("tensor", lambda e, dt=dt, ns=ns, pm=pm: e.matmul(
                            out=pm[0:96, :],
                            lhsT=w_x_sb[:, dt * 96:(dt + 1) * 96],
                            rhs=u_b[:, ns * 512:(ns + 1) * 512],
                            start=(dt == 0), stop=(dt == 1)))
                        if dt == 1:
                            O("vector", lambda e, ns=ns, pm=pm: e.tensor_copy(
                                out=tmp_sb[0:96, ns * 512:(ns + 1) * 512],
                                in_=pm[0:96, :]))
                DMA(lambda e, t0=t0: e.dma_start(
                    out=dbl_in[:, t0:t0 + TT], in_=tmp_sb[0:96, :]))
            CC(lambda e: e.collective_compute(
                "AllReduce", ALU.add, replica_groups=rg,
                ins=[dbl_in.opt()], outs=[dbl_out.opt()]))

            for dt in range(2):
                O("vector", lambda e: e.memset(carry[:], 0.0))
                for j in range(NJ):
                    t0 = j * TT
                    DMA(lambda e, dt=dt, t0=t0: e.dma_start(
                        out=u_b[:], in_=u_dram[dt * 128:(dt + 1) * 128,
                                               t0:t0 + TT]))
                    DMA(lambda e, dt=dt, t0=t0: e.dma_start(
                        out=res2_b[:], in_=res_dram[dt * 128:(dt + 1) * 128,
                                                    t0:t0 + TT]))
                    DMA(lambda e, t0=t0: e.dma_start(
                        out=dtv_sb[:], in_=dbl_out[0:DTR, t0:t0 + TT]))
                    for ns in range(2):
                        pm = pmA if ns == 0 else pmB
                        O("tensor", lambda e, dt=dt, ns=ns, pm=pm: e.matmul(
                            out=pm[:],
                            lhsT=w_dt_sb[:, dt * 128:(dt + 1) * 128],
                            rhs=dtv_sb[:, ns * 512:(ns + 1) * 512],
                            start=True, stop=True))
                        O("vector", lambda e, dt=dt, ns=ns, pm=pm:
                          e.tensor_scalar(
                              out=zdt_sb[:, ns * 512:(ns + 1) * 512],
                              in0=pm[:], scalar1=bdt_sb[:, dt:dt + 1],
                              scalar2=None, op0=ALU.add))
                    O("vector", lambda e: e.scalar_tensor_tensor(
                        out=tmp_sb[:], in0=zdt_sb[:], scalar=-1.0,
                        in1=zdt_sb[:], op0=ALU.mult, op1=ALU.max))
                    O("scalar", lambda e: e.activation(
                        out=tmp_sb[:], in_=tmp_sb[:], func=AF.Exp, scale=-1.0))
                    O("scalar", lambda e: e.activation(
                        out=tmp_sb[:], in_=tmp_sb[:], func=AF.Ln, bias=1.0))
                    O("vector", lambda e: e.scalar_tensor_tensor(
                        out=del_sb[:], in0=zdt_sb[:], scalar=0.0,
                        in1=tmp_sb[:], op0=ALU.max, op1=ALU.add))
                    O("vector", lambda e: e.tensor_mul(du_sb[:], del_sb[:],
                                                       u_b[:]))
                    for s in range(S):
                        DMA(lambda e, s=s, t0=t0: e.dma_start(
                            out=stageB[:],
                            in_=dbl_out[64 + s:65 + s, t0:t0 + TT]))
                        DMA(lambda e, s=s, t0=t0: e.dma_start(
                            out=stageC[:],
                            in_=dbl_out[80 + s:81 + s, t0:t0 + TT]))
                        for h in range(2):
                            O("tensor", lambda e, h=h: e.matmul(
                                out=pmR1[:, h * 512:(h + 1) * 512],
                                lhsT=ones_sb[:],
                                rhs=stageB[:, h * 512:(h + 1) * 512],
                                start=True, stop=True))
                            O("tensor", lambda e, h=h: e.matmul(
                                out=pmR2[:, h * 512:(h + 1) * 512],
                                lhsT=ones_sb[:],
                                rhs=stageC[:, h * 512:(h + 1) * 512],
                                start=True, stop=True))
                        O("scalar", lambda e, dt=dt, s=s: e.activation(
                            out=a_buf[:], in_=del_sb[:], func=AF.Exp,
                            scale=acol_sb[:, dt * S + s: dt * S + s + 1]))
                        O("vector", lambda e: e.tensor_mul(
                            tmp_sb[:], du_sb[:], pmR1[:]))
                        O("vector", lambda e, s=s: e.tensor_tensor_scan(
                            out=big1[:, s * TT:(s + 1) * TT], data0=a_buf[:],
                            data1=tmp_sb[:], initial=carry[:, s:s + 1],
                            op0=ALU.mult, op1=ALU.add))
                        O("vector", lambda e, s=s: e.tensor_copy(
                            out=carry[:, s:s + 1],
                            in_=big1[:, (s + 1) * TT - 1:(s + 1) * TT]))
                        O("vector", lambda e, s=s: e.tensor_mul(
                            big1[:, s * TT:(s + 1) * TT],
                            big1[:, s * TT:(s + 1) * TT], pmR2[:]))
                    O("vector", lambda e: e.tensor_reduce(
                        out=y_sb[:],
                        in_=big1.ap().rearrange("p (s t) -> p t s", s=S),
                        axis=AX.X, op=ALU.add))
                    O("vector", lambda e, dt=dt: e.scalar_tensor_tensor(
                        out=y_sb[:], in0=u_b[:], scalar=dp_sb[:, dt:dt + 1],
                        in1=y_sb[:], op0=ALU.mult, op1=ALU.add))
                    O("scalar", lambda e: e.activation(
                        out=tmp_sb[:], in_=res2_b[:], func=AF.Silu))
                    O("vector", lambda e: e.tensor_mul(y_sb[:], y_sb[:],
                                                       tmp_sb[:]))
                    DMA(lambda e, dt=dt, t0=t0: e.dma_start(
                        out=y_in[dt * 128:(dt + 1) * 128, t0:t0 + TT],
                        in_=y_sb[:]))
            CC(lambda e: e.collective_compute(
                "AllGather", ALU.bypass, replica_groups=rg,
                ins=[y_in.opt()], outs=[y_ag.opt()]))

            last = (it == NITER - 1)
            y_view = y_ag.rearrange("d (q t) -> (d q) t", q=32)
            for sub in range(4):
                for kt in range(16):
                    cst = float(sub + kt * 4096)
                    O("vector", lambda e, cst=cst: e.tensor_scalar(
                        out=idx_f[:], in0=rank_col[:], scalar1=cst,
                        scalar2=None, op0=ALU.add))
                    O("vector", lambda e: e.tensor_copy(out=idx_u[:],
                                                        in_=idx_f[:]))
                    GDMA(lambda e, kt=kt: e.indirect_dma_start(
                        out=big2[:, kt * 512:(kt + 1) * 512],
                        out_offset=None, in_=y_view,
                        in_offset=IndirectOffsetOnAxis(idx_u[:, 0:1], axis=0),
                        bounds_check=RB['y'], oob_is_err=False))
                for m in range(8):
                    for kt in range(16):
                        DMA(lambda e, m=m, kt=kt: e.dma_start(
                            out=wo_sb[:],
                            in_=WOUT[kt * 128:(kt + 1) * 128,
                                     m * 128:(m + 1) * 128]))
                        O("tensor", lambda e, kt=kt: e.matmul(
                            out=pmA[:], lhsT=wo_sb[:],
                            rhs=big2[:, kt * 512:(kt + 1) * 512],
                            start=(kt == 0), stop=(kt == 15)))
                    O("vector", lambda e: e.tensor_copy(out=ev_sb[:],
                                                        in_=pmA[:]))
                    for tb in range(4):
                        O("tensor", lambda e, tb=tb: e.transpose(
                            out=pmT1[:], in_=ev_sb[:, tb * 128:(tb + 1) * 128],
                            identity=ident_sb[:]))
                        O("vector", lambda e, tb=tb, m=m: e.tensor_copy(
                            out=big3[:, tb * 1024 + m * 128:
                                     tb * 1024 + (m + 1) * 128],
                            in_=pmT1[:]))
                for tb in range(4):
                    row0 = sub * 512 + tb * 128
                    if last:
                        DMA(lambda e, row0=row0: e.dma_start(
                            out=xblk[:], in_=X[row0:row0 + 128, :]))
                        O("vector", lambda e, tb=tb: e.tensor_add(
                            big3[:, tb * 1024:(tb + 1) * 1024],
                            big3[:, tb * 1024:(tb + 1) * 1024], xblk[:]))
                        DMA(lambda e, row0=row0, tb=tb: e.dma_start(
                            out=x2_dram[row0:row0 + 128, :],
                            in_=big3[:, tb * 1024:(tb + 1) * 1024]))
                    else:
                        DMA(lambda e, row0=row0, tb=tb: e.dma_start(
                            out=cur_dram[row0:row0 + 128, :],
                            in_=big3[:, tb * 1024:(tb + 1) * 1024]))

        if mode == "attn":
            for b in range(NB):
                DMA(lambda e, b=b: e.dma_start(
                    out=xblk[:], in_=x2_dram[b * 128:(b + 1) * 128, :]))
                DMA(lambda e, b=b: e.dma_start(
                    out=OUT[b * 128:(b + 1) * 128, :], in_=xblk[:]))
        else:
            for b in range(NB):
                DMA(lambda e, b=b: e.dma_start(
                    out=xblk[:], in_=x2_dram[b * 128:(b + 1) * 128, :]))
                ln_of(xblk, zt_blk)
                DMA(lambda e, b=b: e.dma_start(
                    out=ln2_in[b * 128:(b + 1) * 128, :], in_=zt_blk[:]))
                for m in range(8):
                    O("tensor", lambda e, m=m: e.transpose(
                        out=pmT1[:], in_=zt_blk[:, m * 128:(m + 1) * 128],
                        identity=ident_sb[:]))
                    O("vector", lambda e: e.tensor_copy(out=ztr_ev[:],
                                                        in_=pmT1[:]))
                    O("tensor", lambda e, m=m: e.matmul(
                        out=pmT2[:, 0:8], lhsT=ztr_ev[:],
                        rhs=wgt_sb[:, m * 8:(m + 1) * 8],
                        start=(m == 0), stop=(m == 7)))
                O("vector", lambda e: e.tensor_copy(out=lg_sb[:],
                                                    in_=pmT2[:, 0:8]))
                O("vector", lambda e: e.max(out=mx_sb[:], in_=lg_sb[:]))
                O("vector", lambda e: e.max_index(
                    out=ix_sb[:], in_max=mx_sb[:], in_values=lg_sb[:]))
                O("vector", lambda e: e.tensor_copy(out=ixf_sb[:],
                                                    in_=ix_sb[:]))
                O("vector", lambda e: e.tensor_sub(
                    stat[:, 0:1], mx_sb[:, 1:2], mx_sb[:, 0:1]))
                O("scalar", lambda e: e.activation(
                    out=stat[:, 1:2], in_=stat[:, 0:1], func=AF.Exp))
                O("vector", lambda e: e.tensor_scalar(
                    out=stat[:, 2:3], in0=stat[:, 1:2], scalar1=1.0,
                    scalar2=None, op0=ALU.add))
                O("vector", lambda e: e.reciprocal(out=stat[:, 3:4],
                                                   in_=stat[:, 2:3]))
                O("vector", lambda e, b=b: e.tensor_copy(
                    out=probs_sb[:, b:b + 1], in_=stat[:, 3:4]))
                O("vector", lambda e, b=b: e.tensor_mul(
                    probs_sb[:, NB + b:NB + b + 1], stat[:, 1:2],
                    stat[:, 3:4]))
                for k in range(2):
                    O("vector", lambda e, k=k, b=b: e.tensor_copy(
                        out=esel_sb[:, k * NB + b:k * NB + b + 1],
                        in_=ixf_sb[:, k:k + 1]))
                    O("vector", lambda e, k=k: e.tensor_scalar(
                        out=ind_t[:], in0=iota8_sb[:],
                        scalar1=ixf_sb[:, k:k + 1], scalar2=None,
                        op0=ALU.is_equal))
                    O("tensor", lambda e: e.transpose(
                        out=pmT1[0:8, :], in_=ind_t[:], identity=ident_sb[:]))
                    O("vector", lambda e, k=k, b=b: e.tensor_copy(
                        out=ind_em[k][:, b * 128:(b + 1) * 128],
                        in_=pmT1[0:8, 0:128]))
            O("vector", lambda e: e.memset(zero_em, 0.0))
            for k in range(2):
                O("vector", lambda e, k=k: e.tensor_tensor_scan(
                    out=cum_em[k], data0=ind_em[k], data1=zero_em,
                    initial=0.0, op0=ALU.add, op1=ALU.add))
            DMA(lambda e: e.dma_start(out=cnt0_in.rearrange("a e -> e a"),
                                      in_=cum_em[0][:, TL - 1:TL]))
            DMA(lambda e: e.dma_start(out=cnt1_in.rearrange("a e -> e a"),
                                      in_=cum_em[1][:, TL - 1:TL]))
            CC(lambda e: e.collective_compute(
                "AllGather", ALU.bypass, replica_groups=rg,
                ins=[cnt0_in.opt()], outs=[cnt0_ag.opt()]))
            CC(lambda e: e.collective_compute(
                "AllGather", ALU.bypass, replica_groups=rg,
                ins=[cnt1_in.opt()], outs=[cnt1_ag.opt()]))
            DMA(lambda e: e.dma_start(out=cnts_sb[:], in_=cnt0_ag))
            O("tensor", lambda e: e.matmul(
                out=pmT2[0:8, 0:1], lhsT=cnts_sb[:], rhs=prev_sb[:],
                start=True, stop=True))
            O("tensor", lambda e: e.matmul(
                out=pmT2[0:8, 1:2], lhsT=cnts_sb[:], rhs=ones8_sb[:],
                start=True, stop=False))
            DMA(lambda e: e.dma_start(out=cnts_sb[:], in_=cnt1_ag))
            O("tensor", lambda e: e.matmul(
                out=pmT2[0:8, 1:2], lhsT=cnts_sb[:], rhs=prev_sb[:],
                start=False, stop=True))
            O("vector", lambda e: e.tensor_copy(out=base_sb[:],
                                                in_=pmT2[0:8, 0:2]))
            for k in range(2):
                for b in range(NB):
                    O("vector", lambda e, k=k, b=b: e.tensor_mul(
                        pk_em[:], ind_em[k][:, b * 128:(b + 1) * 128],
                        cum_em[k][:, b * 128:(b + 1) * 128]))
                    col = k * NB + b
                    O("tensor", lambda e, col=col: e.matmul(
                        out=pmT2[:, col:col + 1],
                        lhsT=pk_em[:], rhs=ones8_sb[:],
                        start=True, stop=False))
                    O("tensor", lambda e, k=k, b=b, col=col: e.matmul(
                        out=pmT2[:, col:col + 1],
                        lhsT=ind_em[k][:, b * 128:(b + 1) * 128],
                        rhs=base_sb[:, k:k + 1], start=False, stop=True))
            O("vector", lambda e: e.tensor_copy(out=rank_sb[:],
                                                in_=pmT2[:, 0:32]))
            O("vector", lambda e: e.tensor_scalar(
                out=valid_sb[:], in0=rank_sb[:], scalar1=float(CAP) + 0.5,
                scalar2=None, op0=ALU.is_lt))
            O("vector", lambda e: e.tensor_mul(probs_sb[:], probs_sb[:],
                                               valid_sb[:]))
            O("vector", lambda e: e.scalar_tensor_tensor(
                out=slot_sb[:], in0=esel_sb[:], scalar=float(CAP),
                in1=rank_sb[:], op0=ALU.mult, op1=ALU.add))
            O("vector", lambda e: e.tensor_scalar(
                out=slot_sb[:], in0=slot_sb[:], scalar1=BIG - 1.0,
                scalar2=None, op0=ALU.add))
            O("vector", lambda e: e.scalar_tensor_tensor(
                out=slot_sb[:], in0=valid_sb[:], scalar=-BIG,
                in1=slot_sb[:], op0=ALU.mult, op1=ALU.add))
            DMA(lambda e: e.dma_start(
                out=tokmap_in.rearrange("(a b) c -> a (b c)", a=128),
                in_=fill_sb[:]))
            for k in range(2):
                for b in range(NB):
                    col = k * NB + b
                    O("vector", lambda e, col=col: e.tensor_copy(
                        out=idx_f[:], in_=slot_sb[:, col:col + 1]))
                    O("vector", lambda e: e.tensor_copy(out=idx_u[:],
                                                        in_=idx_f[:]))
                    GDMA(lambda e, b=b: e.indirect_dma_start(
                        out=tokmap_in,
                        out_offset=IndirectOffsetOnAxis(idx_u[:, 0:1], axis=0),
                        in_=tokid_sb[:, b:b + 1], in_offset=None,
                        bounds_check=RB['tok'], oob_is_err=False))
            CC(lambda e: e.collective_compute(
                "AllReduce", ALU.min, replica_groups=rg,
                ins=[tokmap_in.opt()], outs=[tokmap_ar.opt()]))
            CC(lambda e: e.collective_compute(
                "AllGather", ALU.bypass, replica_groups=rg,
                ins=[ln2_in.opt()], outs=[ln2_ag.opt()]))
            for g in range(NG):
                for st in range(4):
                    srow = g * 512 + st * 128
                    O("vector", lambda e, srow=srow: e.tensor_scalar(
                        out=idx2_f[:], in0=ebase_col[:], scalar1=float(srow),
                        scalar2=None, op0=ALU.add))
                    O("vector", lambda e: e.tensor_copy(out=idx2_u[:],
                                                        in_=idx2_f[:]))
                    GDMA(lambda e: e.indirect_dma_start(
                        out=idx_f[:], out_offset=None, in_=tokmap_ar,
                        in_offset=IndirectOffsetOnAxis(idx2_u[:, 0:1], axis=0),
                        bounds_check=RB['tok'], oob_is_err=False))
                    O("vector", lambda e: e.tensor_copy(out=idx_u[:],
                                                        in_=idx_f[:]))
                    O("vector", lambda e, st=st: e.memset(
                        big2[:, 4096 + st * 1024: 4096 + (st + 1) * 1024],
                        0.0))
                    GDMA(lambda e, st=st: e.indirect_dma_start(
                        out=big2[:, 4096 + st * 1024: 4096 + (st + 1) * 1024],
                        out_offset=None, in_=ln2_ag,
                        in_offset=IndirectOffsetOnAxis(idx_u[:, 0:1], axis=0),
                        bounds_check=RB['ln'], oob_is_err=False))
                    for m in range(8):
                        O("tensor", lambda e, st=st, m=m: e.transpose(
                            out=pmT1[:],
                            in_=big2[:, 4096 + st * 1024 + m * 128:
                                     4096 + st * 1024 + (m + 1) * 128],
                            identity=ident_sb[:]))
                        O("vector", lambda e, st=st, m=m: e.tensor_copy(
                            out=big2[:, m * 512 + st * 128:
                                     m * 512 + (st + 1) * 128],
                            in_=pmT1[:]))
                for m in range(32):
                    for kt in range(8):
                        DMA(lambda e, m=m, kt=kt: e.dma_start(
                            out=cfc_sb[:],
                            in_=CFC[kt * 128:(kt + 1) * 128,
                                    m * 128:(m + 1) * 128]))
                        O("tensor", lambda e, kt=kt: e.matmul(
                            out=pmA[:], lhsT=cfc_sb[:],
                            rhs=big2[:, kt * 512:(kt + 1) * 512],
                            start=(kt == 0), stop=(kt == 7)))
                    O("vector", lambda e, m=m: e.tensor_scalar(
                        out=big1[:, m * 512:(m + 1) * 512], in0=pmA[:],
                        scalar1=0.0, scalar2=None, op0=ALU.max))
                    O("scalar", lambda e, m=m: e.activation(
                        out=big1[:, m * 512:(m + 1) * 512],
                        in_=big1[:, m * 512:(m + 1) * 512], func=AF.Square))
                for m in range(8):
                    for kt in range(32):
                        DMA(lambda e, m=m, kt=kt: e.dma_start(
                            out=cfc_sb[:],
                            in_=CPR[kt * 128:(kt + 1) * 128,
                                    m * 128:(m + 1) * 128]))
                        O("tensor", lambda e, kt=kt: e.matmul(
                            out=pmB[:], lhsT=cfc_sb[:],
                            rhs=big1[:, kt * 512:(kt + 1) * 512],
                            start=(kt == 0), stop=(kt == 31)))
                    O("vector", lambda e: e.tensor_copy(out=ev_sb[:],
                                                        in_=pmB[:]))
                    for st in range(4):
                        O("tensor", lambda e, st=st: e.transpose(
                            out=pmT1[:],
                            in_=ev_sb[:, st * 128:(st + 1) * 128],
                            identity=ident_sb[:]))
                        O("vector", lambda e, st=st, m=m: e.tensor_copy(
                            out=big3[:, st * 1024 + m * 128:
                                     st * 1024 + (m + 1) * 128],
                            in_=pmT1[:]))
                for st in range(4):
                    srow = g * 512 + st * 128
                    DMA(lambda e, srow=srow, st=st: e.dma_start(
                        out=eout_in[srow:srow + 128, :],
                        in_=big3[:, st * 1024:(st + 1) * 1024]))
            CC(lambda e: e.collective_compute(
                "AllGather", ALU.bypass, replica_groups=rg,
                ins=[eout_in.opt()], outs=[eout_ag.opt()]))
            for b in range(NB):
                for k in range(2):
                    col = k * NB + b
                    O("vector", lambda e, col=col: e.tensor_scalar(
                        out=idx_f[:], in0=rank_sb[:, col:col + 1],
                        scalar1=-1.0, scalar2=0.0, op0=ALU.add, op1=ALU.max))
                    O("vector", lambda e: e.tensor_scalar(
                        out=idx_f[:], in0=idx_f[:], scalar1=float(CAP - 1),
                        scalar2=None, op0=ALU.min))
                    O("vector", lambda e, col=col: e.scalar_tensor_tensor(
                        out=idx_f[:], in0=esel_sb[:, col:col + 1],
                        scalar=float(CAP), in1=idx_f[:],
                        op0=ALU.mult, op1=ALU.add))
                    O("vector", lambda e: e.tensor_copy(out=idx_u[:],
                                                        in_=idx_f[:]))
                    dstt = sq_blk if k == 0 else zt_blk
                    GDMA(lambda e, dstt=dstt: e.indirect_dma_start(
                        out=dstt[:], out_offset=None, in_=eout_ag,
                        in_offset=IndirectOffsetOnAxis(idx_u[:, 0:1], axis=0),
                        bounds_check=RB['tok'], oob_is_err=False))
                O("vector", lambda e, b=b: e.tensor_scalar(
                    out=xc_b[:], in0=sq_blk[:],
                    scalar1=probs_sb[:, b:b + 1], scalar2=None,
                    op0=ALU.mult))
                O("vector", lambda e, b=b: e.scalar_tensor_tensor(
                    out=xc_b[:], in0=zt_blk[:],
                    scalar=probs_sb[:, NB + b:NB + b + 1], in1=xc_b[:],
                    op0=ALU.mult, op1=ALU.add))
                DMA(lambda e, b=b: e.dma_start(
                    out=xblk[:], in_=x2_dram[b * 128:(b + 1) * 128, :]))
                O("vector", lambda e: e.tensor_add(xc_b[:, 0:D],
                                                   xc_b[:, 0:D], xblk[:]))
                DMA(lambda e, b=b: e.dma_start(
                    out=OUT[b * 128:(b + 1) * 128, :], in_=xc_b[:, 0:D]))

        if mode == "dbg":
            for b in range(NB):
                DMA(lambda e, b=b: e.dma_start(
                    out=xblk[:], in_=ln2_in[b * 128:(b + 1) * 128, :]))
                DMA(lambda e, b=b: e.dma_start(
                    out=DBG_LN2[b * 128:(b + 1) * 128, :], in_=xblk[:]))
                DMA(lambda e, b=b: e.dma_start(
                    out=xblk[:], in_=x2_dram[b * 128:(b + 1) * 128, :]))
                DMA(lambda e, b=b: e.dma_start(
                    out=DBG_X2[b * 128:(b + 1) * 128, :], in_=xblk[:]))
            DMA(lambda e: e.dma_start(
                out=fill_sb[:], in_=tokmap_ar.rearrange("(a b) c -> a (b c)", a=128)))
            DMA(lambda e: e.dma_start(
                out=DBG_TOK.rearrange("(a b) c -> a (b c)", a=128), in_=fill_sb[:]))
            DMA(lambda e: e.dma_start(out=DBG_RANK, in_=rank_sb[:]))
            DMA(lambda e: e.dma_start(out=DBG_PROB, in_=probs_sb[:]))
            DMA(lambda e: e.dma_start(out=DBG_ESEL, in_=esel_sb[:]))
            DMA(lambda e: e.dma_start(out=DBG_VAL, in_=valid_sb[:]))
            for q in range(CAP // 128):
                DMA(lambda e, q=q: e.dma_start(
                    out=xblk[:], in_=eout_in[q * 128:(q + 1) * 128, :]))
                DMA(lambda e, q=q: e.dma_start(
                    out=DBG_EOUT[q * 128:(q + 1) * 128, :], in_=xblk[:]))

        SEG = 30000
        total = 0
        pre = []
        for eng, fn, inc in ops:
            pre.append(total)
            total += inc
        seg_of = []
        seg_base = [0]
        for i in range(len(ops)):
            if pre[i] - seg_base[-1] >= SEG:
                seg_base.append(pre[i])
            seg_of.append(len(seg_base) - 1)
        nseg = len(seg_base)
        assert nseg <= NSEG, nseg
        seg_total = []
        for s in range(nseg):
            end = seg_base[s + 1] if s + 1 < nseg else total
            seg_total.append(end - seg_base[s])

        def run_engine(engine, name):
            cur_seg = 0
            for i, (eng, fn, inc) in enumerate(ops):
                if eng != name:
                    continue
                s = seg_of[i]
                while cur_seg < s:
                    engine.wait_ge(ordv[cur_seg], seg_total[cur_seg])
                    cur_seg += 1
                engine.wait_ge(ordv[s], pre[i] - seg_base[s])
                fn(engine).then_inc(ordv[s], inc)
            while cur_seg < nseg - 1:
                engine.wait_ge(ordv[cur_seg], seg_total[cur_seg])
                cur_seg += 1
            engine.wait_ge(ordv[nseg - 1], seg_total[nseg - 1])

        @block.sync
        def _(sync):
            run_engine(sync, "sync")

        @block.vector
        def _(vector):
            run_engine(vector, "vector")

        @block.scalar
        def _(scalar):
            run_engine(scalar, "scalar")

        @block.gpsimd
        def _(gpsimd):
            RB['y'] = gpsimd.alloc_register("rb_y")
            RB['tok'] = gpsimd.alloc_register("rb_tok")
            RB['ln'] = gpsimd.alloc_register("rb_ln")
            gpsimd.reg_mov(RB['y'], DIN * 32 - 1)
            gpsimd.reg_mov(RB['tok'], E * CAP - 1)
            gpsimd.reg_mov(RB['ln'], T - 1)
            run_engine(gpsimd, "gpsimd")

        @block.tensor
        def _(tensor):
            run_engine(tensor, "tensor")

    return nc


_cache = {}


def get_program(mode="full"):
    if mode not in _cache:
        _cache[mode] = build_program(mode)
    return _cache[mode]


def make_in_maps(inputs):
    x = np.ascontiguousarray(
        np.asarray(inputs["L0_attn_x"], dtype=np.float32).reshape(T, D))
    w_in = np.asarray(inputs["L0_attn_W_in"], dtype=np.float32)
    conv_w = np.asarray(inputs["L0_attn_conv_w"], dtype=np.float32)
    conv_b = np.asarray(inputs["L0_attn_conv_b"], dtype=np.float32)
    w_x = np.asarray(inputs["L0_attn_W_x"], dtype=np.float32)
    w_dt = np.asarray(inputs["L0_attn_W_dt"], dtype=np.float32)
    b_dt = np.asarray(inputs["L0_attn_b_dt"], dtype=np.float32)
    a_log = np.asarray(inputs["L0_attn_A_log"], dtype=np.float32)
    dp = np.asarray(inputs["L0_attn_Dp"], dtype=np.float32)
    w_out = np.ascontiguousarray(
        np.asarray(inputs["L0_attn_W_out"], dtype=np.float32))
    w_g = np.asarray(inputs["L0_ffn_w_g"], dtype=np.float32)
    c_fc = np.asarray(inputs["L0_ffn_c_fc"], dtype=np.float32)
    c_proj = np.asarray(inputs["L0_ffn_c_proj"], dtype=np.float32)

    in_maps = []
    for c in range(NC):
        dsl = slice(c * DL, (c + 1) * DL)
        m = {
            "x_shard": np.ascontiguousarray(x[c * TL:(c + 1) * TL, :]),
            "w_in_loc": np.ascontiguousarray(
                np.concatenate([w_in[:, dsl], w_in[:, DIN + c * DL:
                                                  DIN + (c + 1) * DL]],
                               axis=1)),
            "conv_w_loc": np.ascontiguousarray(conv_w[dsl, 0, :]),
            "conv_b_loc": np.ascontiguousarray(conv_b[dsl].reshape(DL, 1)),
            "w_x_loc": np.ascontiguousarray(w_x[dsl, :]),
            "w_dt_loc": np.ascontiguousarray(w_dt[:, dsl]),
            "b_dt_loc": np.ascontiguousarray(b_dt[dsl].reshape(DL, 1)),
            "a_log_loc": np.ascontiguousarray(a_log[dsl, :]),
            "dp_loc": np.ascontiguousarray(dp[dsl].reshape(DL, 1)),
            "w_out": w_out,
            "w_gT": np.ascontiguousarray(w_g.T),
            "c_fc_loc": np.ascontiguousarray(c_fc[c]),
            "c_proj_loc": np.ascontiguousarray(c_proj[c]),
            "tokid": np.ascontiguousarray(
                np.arange(c * TL, (c + 1) * TL, dtype=np.float32)
                .reshape(TL, 1)),
            "prevmask": np.ascontiguousarray(
                (np.arange(NC) < c).astype(np.float32).reshape(NC, 1)),
            "rankoff": np.array([[float(c)]], dtype=np.float32),
        }
        in_maps.append(m)
    return in_maps


def kernel(**inputs):
    nc = get_program("full")
    in_maps = make_in_maps(inputs)
    res = run_bass_kernel_spmd(nc, in_maps, core_ids=list(range(NC)))
    out = np.concatenate(
        [np.asarray(res.results[c]["out"]) for c in range(NC)], axis=0)
    return out.reshape(1, T, D).astype(np.float32)


# version 04
# speedup vs baseline: 1.4758x; 1.4758x over previous
import numpy as np
from contextlib import ExitStack
import concourse.bass as bass
import concourse.mybir as mybir
from concourse.bass import IndirectOffsetOnAxis
from concourse.bass_utils import run_bass_kernel_spmd

F32 = mybir.dt.float32
U32 = mybir.dt.uint32
AF = mybir.ActivationFunctionType
ALU = mybir.AluOpType
AX = mybir.AxisListType

NC = 8
T = 16384
D = 1024
DIN = 2048
S = 16
DTR = 64
E = 8
CAP = 5120
TL = T // NC
DL = DIN // NC
NITER = 4
TT = 1024
NJ = T // TT
NB = TL // 128
NG = 10
SQD = float(np.sqrt(1024.0))
OOB = 1.0e9
BIG = 2097152.0


def build_program(mode="full"):
    nc = bass.Bass("TRN2", target_bir_lowering=False, debug=False)

    def ein(name, shape, dtype=F32):
        return nc.dram_tensor(name, list(shape), dtype, kind="ExternalInput").ap()

    X = ein("x_shard", (TL, D))
    WIN = ein("w_in_loc", (D, 2 * DL))
    CW = ein("conv_w_loc", (DL, 4))
    CB = ein("conv_b_loc", (DL, 1))
    WX = ein("w_x_loc", (DL, 96))
    WDT = ein("w_dt_loc", (DTR, DL))
    BDT = ein("b_dt_loc", (DL, 1))
    ALOG = ein("a_log_loc", (DL, S))
    DP = ein("dp_loc", (DL, 1))
    WOUT = ein("w_out", (DIN, D))
    WGT = ein("w_gT", (D, E))
    CFC = ein("c_fc_loc", (D, 4 * D))
    CPR = ein("c_proj_loc", (4 * D, D))
    TOKID = ein("tokid", (TL, 1))
    PREV = ein("prevmask", (NC, 1))
    RKOF = ein("rankoff", (1, 1))

    OUT = nc.dram_tensor("out", [TL, D], F32, kind="ExternalOutput").ap()
    if mode == "dbg":
        DBG_LN2 = nc.dram_tensor("dbg_ln2", [TL, D], F32, kind="ExternalOutput").ap()
        DBG_TOK = nc.dram_tensor("dbg_tok", [E * CAP, 1], F32, kind="ExternalOutput").ap()
        DBG_RANK = nc.dram_tensor("dbg_rank", [128, 32], F32, kind="ExternalOutput").ap()
        DBG_PROB = nc.dram_tensor("dbg_prob", [128, 32], F32, kind="ExternalOutput").ap()
        DBG_ESEL = nc.dram_tensor("dbg_esel", [128, 32], F32, kind="ExternalOutput").ap()
        DBG_VAL = nc.dram_tensor("dbg_val", [128, 32], F32, kind="ExternalOutput").ap()
        DBG_EOUT = nc.dram_tensor("dbg_eout", [CAP, D], F32, kind="ExternalOutput").ap()
        DBG_X2 = nc.dram_tensor("dbg_x2", [TL, D], F32, kind="ExternalOutput").ap()

    def idr(name, shape, dtype=F32):
        return nc.dram_tensor(name, list(shape), dtype, kind="Internal").ap()

    cur_dram = idr("cur_dram", (TL, D))
    x2_dram = idr("x2_dram", (TL, D))
    z_in = idr("z_in", (D, TL))
    z_ag = idr("z_ag", (NC * D, TL))
    dbl_in = idr("dbl_in", (96, T))
    dbl_out = idr("dbl_out", (96, T))
    u_dram = idr("u_dram", (DL, T))
    res_dram = idr("res_dram", (DL, T))
    y_in = idr("y_in", (DL, T))
    y_ag = idr("y_ag", (DIN, T))
    ln2_in = idr("ln2_in", (TL, D))
    ln2_ag = idr("ln2_ag", (T, D))
    cnt0_in = idr("cnt0_in", (1, E))
    cnt0_ag = idr("cnt0_ag", (NC, E))
    cnt1_in = idr("cnt1_in", (1, E))
    cnt1_ag = idr("cnt1_ag", (NC, E))
    tokmap_in = idr("tokmap_in", (E * CAP, 1))
    tokmap_ar = idr("tokmap_ar", (E * CAP, 1))
    eout_in = idr("eout_in", (CAP, D))
    eout_ag = idr("eout_ag", (E * CAP, D))

    rg = [list(range(NC))]

    ops = []
    RB = {}

    def O(eng, fn, inc=1):
        ops.append((eng, fn, inc))

    def DMA(fn):
        O("sync", fn, 16)

    def GDMA(fn):
        O("gpsimd", fn, 16)

    def CC(fn):
        O("gpsimd", fn, 1)

    with ExitStack() as ctx:
        def sb(name, shape, dtype=F32):
            return ctx.enter_context(nc.sbuf_tensor(name, shape, dtype))

        def psum(name, shape, dtype=F32):
            return ctx.enter_context(nc.psum_tensor(name, shape, dtype))

        big1 = sb("big1", [128, S * TT])
        big2 = sb("big2", [128, 8 * TT])
        big3 = sb("big3", [128, 4224])
        w_in_sb = sb("w_in_sb", [128, 8 * 512])
        w_x_sb = sb("w_x_sb", [128, 2 * 96])
        w_dt_sb = sb("w_dt_sb", [DTR, DL])
        conv_sb = sb("conv_sb", [128, 8])
        convb_sb = sb("convb_sb", [128, 2])
        bdt_sb = sb("bdt_sb", [128, 2])
        acol_sb = sb("acol_sb", [128, 2 * S])
        dp_sb = sb("dp_sb", [128, 2])
        wgt_sb = sb("wgt_sb", [128, 8 * E])
        ones_sb = sb("ones_sb", [1, 128])
        ident_sb = sb("ident_sb", [128, 128])
        iota8_sb = sb("iota8_sb", [128, 8])
        tokid_sb = sb("tokid_sb", [128, NB])
        prev_sb = sb("prev_sb", [NC, 1])
        ones8_sb = sb("ones8_sb", [NC, 1])
        rkof_sb = sb("rkof_sb", [1, 1])
        rank_col = sb("rank_col", [128, 1])
        ebase_col = sb("ebase_col", [128, 1])
        idx2_f = sb("idx2_f", [128, 1])
        idx2_u = sb("idx2_u", [128, 1], U32)
        idx_f = sb("idx_f", [128, 1])
        idx_u = sb("idx_u", [128, 1], U32)
        xc_b = sb("xc_b", [128, TT])
        a_buf = xc_b
        u_b = sb("u_b", [128, TT])
        res2_b = sb("res2_b", [128, TT])
        zdt_sb = sb("zdt_sb", [128, TT])
        del_sb = sb("del_sb", [128, TT])
        du_sb = sb("du_sb", [128, TT])
        tmp_sb = sb("tmp_sb", [128, TT])
        y_sb = sb("y_sb", [128, TT])
        dtv_sb = y_sb[0:DTR, :]
        stageBC = sb("stageBC", [1, 2 * TT])
        stageB = stageBC[0:1, 0:TT]
        stageC = stageBC[0:1, TT:2 * TT]
        carry = sb("carry", [128, S])
        xblk = sb("xblk", [128, D])
        sq_blk = sb("sq_blk", [128, D])
        zt_blk = sb("zt_blk", [128, D])
        stat = sb("stat", [128, 8])
        ztr_ev = sb("ztr_ev", [128, 128])
        ev_sb = sb("ev_sb", [128, 512])

        sc1 = sb("sc1", [128, 4096])
        wo_sb = sc1[:, 0:2048]
        lg_sb = sb("lg_sb", [128, E])
        mx_sb = sb("mx_sb", [128, 8])
        ix_sb = sb("ix_sb", [128, 8], U32)
        ixf_sb = sb("ixf_sb", [128, 8])
        ind_t = sb("ind_t", [128, E])
        ind_em = [big1[0:E, k * TL:(k + 1) * TL] for k in range(2)]
        cum_em = [big1[0:E, 2 * TL + k * TL:2 * TL + (k + 1) * TL]
                  for k in range(2)]
        zero_em = big1[0:E, 4 * TL:5 * TL]
        pk_em = big1[0:E, 5 * TL:5 * TL + 128]
        cnts_sb = sb("cnts_sb", [NC, E])
        base_sb = sb("base_sb", [E, 2])
        rank_sb = sb("rank_sb", [128, 2 * NB])
        valid_sb = sb("valid_sb", [128, 2 * NB])
        probs_sb = sb("probs_sb", [128, 2 * NB])
        esel_sb = sb("esel_sb", [128, 2 * NB])
        slot_sb = sb("slot_sb", [128, 2 * NB])
        fill_sb = sb("fill_sb", [128, 320])
        cfc_sb = sc1
        ztr8 = sc1[:, 0:1024]
        cst_sb = sb("cst_sb", [128, 8])

        pmA = psum("pmA", [128, 512])
        pmB = psum("pmB", [128, 512])
        pmR1 = psum("pmR1", [128, TT])
        pmR2 = psum("pmR2", [128, TT])
        pmT1 = psum("pmT1", [128, 128])
        pmT2 = psum("pmT2", [128, 128])

        NSEG = 12
        ordv = [ctx.enter_context(nc.semaphore(f"ordv{i}")) for i in range(NSEG)]
        block = ctx.enter_context(nc.Block())

        CONSTS = [1e-5, 1e-30, 0.5, -0.5, -1.0, 1.0, 0.0]
        for ci, cv in enumerate(CONSTS):
            nc.const_aps.register(F32, cv, cst_sb[:, ci:ci + 1])
            O("vector", lambda e, ci=ci, cv=cv: e.memset(
                cst_sb[:, ci:ci + 1], cv))
        DMA(lambda e: e.dma_start(
            out=w_in_sb[:], in_=WIN.rearrange("(k p) c -> p k c", p=128)))
        DMA(lambda e: e.dma_start(
            out=w_x_sb[:], in_=WX.rearrange("(d p) c -> p d c", p=128)))
        DMA(lambda e: e.dma_start(out=w_dt_sb[:], in_=WDT))
        DMA(lambda e: e.dma_start(
            out=conv_sb[:], in_=CW.rearrange("(d p) c -> p d c", p=128)))
        DMA(lambda e: e.dma_start(
            out=convb_sb[:], allow_slow_non_contiguous=True,
            in_=CB.rearrange("(d p) c -> p d c", p=128)[:, :, 0]))
        DMA(lambda e: e.dma_start(
            out=bdt_sb[:], allow_slow_non_contiguous=True,
            in_=BDT.rearrange("(d p) c -> p d c", p=128)[:, :, 0]))
        DMA(lambda e: e.dma_start(
            out=acol_sb[:], in_=ALOG.rearrange("(d p) s -> p d s", p=128)))
        DMA(lambda e: e.dma_start(
            out=dp_sb[:], allow_slow_non_contiguous=True,
            in_=DP.rearrange("(d p) c -> p d c", p=128)[:, :, 0]))
        DMA(lambda e: e.dma_start(
            out=wgt_sb[:], in_=WGT.rearrange("(k p) c -> p k c", p=128)))
        DMA(lambda e: e.dma_start(
            out=tokid_sb[:], allow_slow_non_contiguous=True,
            in_=TOKID.rearrange("(b p) c -> p b c", p=128)[:, :, 0]))
        DMA(lambda e: e.dma_start(out=prev_sb[:], in_=PREV))
        DMA(lambda e: e.dma_start(out=rkof_sb[:], in_=RKOF))
        O("scalar", lambda e: e.activation(out=acol_sb[:], in_=acol_sb[:],
                                           func=AF.Exp))
        O("vector", lambda e: e.tensor_scalar(
            out=acol_sb[:], in0=acol_sb[:], scalar1=-1.0, scalar2=None,
            op0=ALU.mult))
        O("vector", lambda e: e.memset(ones_sb[:], 1.0))
        O("vector", lambda e: e.memset(ones8_sb[:], 1.0))
        O("gpsimd", lambda e: e.iota(out=ident_sb[:], pattern=[[1, 128]],
                                     base=0, channel_multiplier=-1,
                                     allow_small_or_imprecise_dtypes=True))
        O("vector", lambda e: e.tensor_scalar(
            out=ident_sb[:], in0=ident_sb[:], scalar1=0.0, scalar2=None,
            op0=ALU.is_equal))
        O("gpsimd", lambda e: e.iota(out=iota8_sb[:], pattern=[[1, 8]],
                                     base=0, channel_multiplier=0,
                                     allow_small_or_imprecise_dtypes=True))
        O("vector", lambda e: e.memset(fill_sb[:], OOB))
        O("tensor", lambda e: e.matmul(out=pmT2[:, 0:1], lhsT=ones_sb[:],
                                       rhs=rkof_sb[:], start=True, stop=True))
        O("gpsimd", lambda e: e.iota(out=rank_col[:], pattern=[[1, 1]],
                                     base=0, channel_multiplier=32,
                                     allow_small_or_imprecise_dtypes=True))
        O("vector", lambda e: e.scalar_tensor_tensor(
            out=rank_col[:], in0=pmT2[:, 0:1], scalar=4.0, in1=rank_col[:],
            op0=ALU.mult, op1=ALU.add))
        O("gpsimd", lambda e: e.iota(out=ebase_col[:], pattern=[[1, 1]],
                                     base=0, channel_multiplier=1,
                                     allow_small_or_imprecise_dtypes=True))
        O("vector", lambda e: e.scalar_tensor_tensor(
            out=ebase_col[:], in0=pmT2[:, 0:1], scalar=float(CAP),
            in1=ebase_col[:], op0=ALU.mult, op1=ALU.add))

        def rms_of(src_tile, dst_tile):
            O("vector", lambda e: e.tensor_mul(sq_blk[:], src_tile[:], src_tile[:]))
            O("vector", lambda e: e.tensor_reduce(
                out=stat[:, 0:1], in_=sq_blk[:], axis=AX.X, op=ALU.add))
            O("scalar", lambda e: e.activation(
                out=stat[:, 3:4], in_=stat[:, 0:1], func=AF.Ln, bias=1e-30))
            O("scalar", lambda e: e.activation(
                out=stat[:, 3:4], in_=stat[:, 3:4], func=AF.Exp, scale=0.5))
            O("vector", lambda e: e.tensor_scalar(
                out=stat[:, 3:4], in0=stat[:, 3:4], scalar1=1e-12,
                scalar2=None, op0=ALU.max))
            O("vector", lambda e: e.reciprocal(out=stat[:, 4:5],
                                               in_=stat[:, 3:4]))
            O("vector", lambda e: e.tensor_scalar(
                out=stat[:, 4:5], in0=stat[:, 4:5], scalar1=SQD,
                scalar2=None, op0=ALU.mult))
            O("vector", lambda e: e.tensor_scalar(
                out=dst_tile[:], in0=src_tile[:], scalar1=stat[:, 4:5],
                scalar2=None, op0=ALU.mult))

        def ln_of(src_tile, dst_tile):
            O("vector", lambda e: e.tensor_mul(sq_blk[:], src_tile[:], src_tile[:]))
            O("vector", lambda e: e.tensor_reduce(
                out=stat[:, 0:1], in_=sq_blk[:], axis=AX.X, op=ALU.add))
            O("vector", lambda e: e.tensor_reduce(
                out=stat[:, 1:2], in_=src_tile[:], axis=AX.X, op=ALU.add))
            O("vector", lambda e: e.tensor_scalar(
                out=stat[:, 1:2], in0=stat[:, 1:2], scalar1=1.0 / D,
                scalar2=None, op0=ALU.mult))
            O("vector", lambda e: e.tensor_scalar(
                out=stat[:, 0:1], in0=stat[:, 0:1], scalar1=1.0 / D,
                scalar2=None, op0=ALU.mult))
            O("vector", lambda e: e.tensor_mul(stat[:, 2:3], stat[:, 1:2],
                                               stat[:, 1:2]))
            O("vector", lambda e: e.tensor_sub(stat[:, 0:1], stat[:, 0:1],
                                               stat[:, 2:3]))
            O("scalar", lambda e: e.activation(
                out=stat[:, 3:4], in_=stat[:, 0:1], func=AF.Ln, bias=1e-5))
            O("scalar", lambda e: e.activation(
                out=stat[:, 3:4], in_=stat[:, 3:4], func=AF.Exp, scale=-0.5))
            O("vector", lambda e: e.tensor_mul(stat[:, 4:5], stat[:, 1:2],
                                               stat[:, 3:4]))
            O("vector", lambda e: e.tensor_scalar(
                out=stat[:, 4:5], in0=stat[:, 4:5], scalar1=-1.0,
                scalar2=None, op0=ALU.mult))
            O("vector", lambda e: e.tensor_scalar(
                out=dst_tile[:], in0=src_tile[:], scalar1=stat[:, 3:4],
                scalar2=stat[:, 4:5], op0=ALU.mult, op1=ALU.add))

        def x_in_t(dt):
            return big3[:, dt * 1056: dt * 1056 + TT + 32]

        def res_t(dt):
            return big3[:, 2112 + dt * 1056: 2112 + dt * 1056 + TT]

        for it in range(NITER):
            for b in range(NB):
                src = X if it == 0 else cur_dram
                DMA(lambda e, src=src, b=b: e.dma_start(
                    out=xblk[:], in_=src[b * 128:(b + 1) * 128, :]))
                if it == 0:
                    ln_of(xblk, zt_blk)
                    O("vector", lambda e: e.tensor_copy(out=xblk[:],
                                                        in_=zt_blk[:]))
                rms_of(xblk, zt_blk)
                for m in range(8):
                    O("tensor", lambda e, m=m: e.transpose(
                        out=pmT1[:], in_=zt_blk[:, m * 128:(m + 1) * 128],
                        identity=ident_sb[:]))
                    O("vector", lambda e, m=m: e.tensor_copy(
                        out=ztr8[:, m * 128:(m + 1) * 128], in_=pmT1[:]))
                DMA(lambda e, b=b: e.dma_start(
                    out=z_in[:, b * 128:(b + 1) * 128]
                    .rearrange("(m p) t -> p m t", p=128),
                    in_=ztr8.rearrange("p (m t) -> p m t", t=128)))
            CC(lambda e: e.collective_compute(
                "AllGather", ALU.bypass, replica_groups=rg,
                ins=[z_in.opt()], outs=[z_ag.opt()]))

            for j in range(NJ):
                t0 = j * TT
                r = t0 // TL
                lc = t0 % TL
                DMA(lambda e, r=r, lc=lc: e.dma_start(
                    out=big2[:],
                    in_=z_ag[r * D:(r + 1) * D, lc:lc + TT]
                    .rearrange("(k p) t -> p k t", p=128)))
                for cti in range(4):
                    dt = cti % 2
                    is_res = cti >= 2
                    for ns in range(2):
                        pm = pmA if ns == 0 else pmB
                        for kt in range(8):
                            O("tensor", lambda e, cti=cti, ns=ns, kt=kt, pm=pm:
                              e.matmul(
                                  out=pm[:],
                                  lhsT=w_in_sb[:, kt * 512 + cti * 128:
                                               kt * 512 + (cti + 1) * 128],
                                  rhs=big2[:, kt * TT + ns * 512:
                                           kt * TT + (ns + 1) * 512],
                                  start=(kt == 0), stop=(kt == 7)))
                    for ns in range(2):
                        pm = pmA if ns == 0 else pmB
                        if is_res:
                            O("vector", lambda e, dt=dt, ns=ns, pm=pm:
                              e.tensor_copy(
                                  out=res_t(dt)[:, ns * 512:(ns + 1) * 512],
                                  in_=pm[:]))
                        else:
                            O("vector", lambda e, dt=dt, ns=ns, pm=pm:
                              e.tensor_copy(
                                  out=x_in_t(dt)[:, 4 + ns * 512:
                                                 4 + (ns + 1) * 512],
                                  in_=pm[:]))
                for dt in range(2):
                    if j == 0:
                        O("vector", lambda e, dt=dt: e.memset(
                            x_in_t(dt)[:, 0:4], 0.0))
                    O("vector", lambda e, dt=dt: e.tensor_scalar(
                        out=xc_b[:], in0=x_in_t(dt)[:, 1:1 + TT],
                        scalar1=conv_sb[:, dt * 4: dt * 4 + 1],
                        scalar2=convb_sb[:, dt:dt + 1],
                        op0=ALU.mult, op1=ALU.add))
                    for k in range(1, 4):
                        O("vector", lambda e, dt=dt, k=k:
                          e.scalar_tensor_tensor(
                              out=xc_b[:], in0=x_in_t(dt)[:, 1 + k:1 + k + TT],
                              scalar=conv_sb[:, dt * 4 + k: dt * 4 + k + 1],
                              in1=xc_b[:], op0=ALU.mult, op1=ALU.add))
                    O("scalar", lambda e: e.activation(
                        out=u_b[:], in_=xc_b[:], func=AF.Silu))
                    O("vector", lambda e, dt=dt: e.tensor_copy(
                        out=x_in_t(dt)[:, 0:4],
                        in_=x_in_t(dt)[:, TT:TT + 4]))
                    DMA(lambda e, dt=dt, t0=t0: e.dma_start(
                        out=u_dram[dt * 128:(dt + 1) * 128, t0:t0 + TT],
                        in_=u_b[:]))
                    DMA(lambda e, dt=dt, t0=t0: e.dma_start(
                        out=res_dram[dt * 128:(dt + 1) * 128, t0:t0 + TT],
                        in_=res_t(dt)[:]))
                    for ns in range(2):
                        pm = pmA if ns == 0 else pmB
                        O("tensor", lambda e, dt=dt, ns=ns, pm=pm: e.matmul(
                            out=pm[0:96, :],
                            lhsT=w_x_sb[:, dt * 96:(dt + 1) * 96],
                            rhs=u_b[:, ns * 512:(ns + 1) * 512],
                            start=(dt == 0), stop=(dt == 1)))
                        if dt == 1:
                            O("vector", lambda e, ns=ns, pm=pm: e.tensor_copy(
                                out=tmp_sb[0:96, ns * 512:(ns + 1) * 512],
                                in_=pm[0:96, :]))
                DMA(lambda e, t0=t0: e.dma_start(
                    out=dbl_in[:, t0:t0 + TT], in_=tmp_sb[0:96, :]))
            CC(lambda e: e.collective_compute(
                "AllReduce", ALU.add, replica_groups=rg,
                ins=[dbl_in.opt()], outs=[dbl_out.opt()]))

            for dt in range(2):
                O("vector", lambda e: e.memset(carry[:], 0.0))
                for j in range(NJ):
                    t0 = j * TT
                    DMA(lambda e, dt=dt, t0=t0: e.dma_start(
                        out=u_b[:], in_=u_dram[dt * 128:(dt + 1) * 128,
                                               t0:t0 + TT]))
                    DMA(lambda e, dt=dt, t0=t0: e.dma_start(
                        out=res2_b[:], in_=res_dram[dt * 128:(dt + 1) * 128,
                                                    t0:t0 + TT]))
                    DMA(lambda e, t0=t0: e.dma_start(
                        out=dtv_sb[:], in_=dbl_out[0:DTR, t0:t0 + TT]))
                    for ns in range(2):
                        pm = pmA if ns == 0 else pmB
                        O("tensor", lambda e, dt=dt, ns=ns, pm=pm: e.matmul(
                            out=pm[:],
                            lhsT=w_dt_sb[:, dt * 128:(dt + 1) * 128],
                            rhs=dtv_sb[:, ns * 512:(ns + 1) * 512],
                            start=True, stop=True))
                        O("vector", lambda e, dt=dt, ns=ns, pm=pm:
                          e.tensor_scalar(
                              out=zdt_sb[:, ns * 512:(ns + 1) * 512],
                              in0=pm[:], scalar1=bdt_sb[:, dt:dt + 1],
                              scalar2=None, op0=ALU.add))
                    O("vector", lambda e: e.scalar_tensor_tensor(
                        out=tmp_sb[:], in0=zdt_sb[:], scalar=-1.0,
                        in1=zdt_sb[:], op0=ALU.mult, op1=ALU.max))
                    O("scalar", lambda e: e.activation(
                        out=tmp_sb[:], in_=tmp_sb[:], func=AF.Exp, scale=-1.0))
                    O("scalar", lambda e: e.activation(
                        out=tmp_sb[:], in_=tmp_sb[:], func=AF.Ln, bias=1.0))
                    O("vector", lambda e: e.scalar_tensor_tensor(
                        out=del_sb[:], in0=zdt_sb[:], scalar=0.0,
                        in1=tmp_sb[:], op0=ALU.max, op1=ALU.add))
                    O("vector", lambda e: e.tensor_mul(du_sb[:], del_sb[:],
                                                       u_b[:]))
                    for s in range(S):
                        DMA(lambda e, s=s, t0=t0: e.dma_start(
                            out=stageBC.ap().rearrange("a (r t) -> a r t", r=2),
                            in_=dbl_out[64 + s:81 + s:16, t0:t0 + TT]))
                        for h in range(2):
                            O("tensor", lambda e, h=h: e.matmul(
                                out=pmR1[:, h * 512:(h + 1) * 512],
                                lhsT=ones_sb[:],
                                rhs=stageB[:, h * 512:(h + 1) * 512],
                                start=True, stop=True))
                            O("tensor", lambda e, h=h: e.matmul(
                                out=pmR2[:, h * 512:(h + 1) * 512],
                                lhsT=ones_sb[:],
                                rhs=stageC[:, h * 512:(h + 1) * 512],
                                start=True, stop=True))
                        O("scalar", lambda e, dt=dt, s=s: e.activation(
                            out=a_buf[:], in_=del_sb[:], func=AF.Exp,
                            scale=acol_sb[:, dt * S + s: dt * S + s + 1]))
                        O("vector", lambda e: e.tensor_mul(
                            tmp_sb[:], du_sb[:], pmR1[:]))
                        O("vector", lambda e, s=s: e.tensor_tensor_scan(
                            out=big1[:, s * TT:(s + 1) * TT], data0=a_buf[:],
                            data1=tmp_sb[:], initial=carry[:, s:s + 1],
                            op0=ALU.mult, op1=ALU.add))
                        O("vector", lambda e, s=s: e.tensor_copy(
                            out=carry[:, s:s + 1],
                            in_=big1[:, (s + 1) * TT - 1:(s + 1) * TT]))
                        O("vector", lambda e, s=s: e.tensor_mul(
                            big1[:, s * TT:(s + 1) * TT],
                            big1[:, s * TT:(s + 1) * TT], pmR2[:]))
                    O("vector", lambda e: e.tensor_reduce(
                        out=y_sb[:],
                        in_=big1.ap().rearrange("p (s t) -> p t s", s=S),
                        axis=AX.X, op=ALU.add))
                    O("vector", lambda e, dt=dt: e.scalar_tensor_tensor(
                        out=y_sb[:], in0=u_b[:], scalar=dp_sb[:, dt:dt + 1],
                        in1=y_sb[:], op0=ALU.mult, op1=ALU.add))
                    O("scalar", lambda e: e.activation(
                        out=tmp_sb[:], in_=res2_b[:], func=AF.Silu))
                    O("vector", lambda e: e.tensor_mul(y_sb[:], y_sb[:],
                                                       tmp_sb[:]))
                    DMA(lambda e, dt=dt, t0=t0: e.dma_start(
                        out=y_in[dt * 128:(dt + 1) * 128, t0:t0 + TT],
                        in_=y_sb[:]))
            CC(lambda e: e.collective_compute(
                "AllGather", ALU.bypass, replica_groups=rg,
                ins=[y_in.opt()], outs=[y_ag.opt()]))

            last = (it == NITER - 1)
            y_view = y_ag.rearrange("d (q t) -> (d q) t", q=32)
            for sub in range(4):
                for kt in range(16):
                    cst = float(sub + kt * 4096)
                    O("vector", lambda e, cst=cst: e.tensor_scalar(
                        out=idx_f[:], in0=rank_col[:], scalar1=cst,
                        scalar2=None, op0=ALU.add))
                    O("vector", lambda e: e.tensor_copy(out=idx_u[:],
                                                        in_=idx_f[:]))
                    GDMA(lambda e, kt=kt: e.indirect_dma_start(
                        out=big2[:, kt * 512:(kt + 1) * 512],
                        out_offset=None, in_=y_view,
                        in_offset=IndirectOffsetOnAxis(idx_u[:, 0:1], axis=0),
                        bounds_check=RB['y'], oob_is_err=False))
                for m in range(8):
                    DMA(lambda e, m=m: e.dma_start(
                        out=wo_sb[:],
                        in_=WOUT[:, m * 128:(m + 1) * 128]
                        .rearrange("(k p) c -> p k c", p=128)))
                    for kt in range(16):
                        O("tensor", lambda e, kt=kt: e.matmul(
                            out=pmA[:],
                            lhsT=wo_sb[:, kt * 128:(kt + 1) * 128],
                            rhs=big2[:, kt * 512:(kt + 1) * 512],
                            start=(kt == 0), stop=(kt == 15)))
                    O("vector", lambda e: e.tensor_copy(out=ev_sb[:],
                                                        in_=pmA[:]))
                    for tb in range(4):
                        O("tensor", lambda e, tb=tb: e.transpose(
                            out=pmT1[:], in_=ev_sb[:, tb * 128:(tb + 1) * 128],
                            identity=ident_sb[:]))
                        O("vector", lambda e, tb=tb, m=m: e.tensor_copy(
                            out=big3[:, tb * 1024 + m * 128:
                                     tb * 1024 + (m + 1) * 128],
                            in_=pmT1[:]))
                for tb in range(4):
                    row0 = sub * 512 + tb * 128
                    if last:
                        DMA(lambda e, row0=row0: e.dma_start(
                            out=xblk[:], in_=X[row0:row0 + 128, :]))
                        O("vector", lambda e, tb=tb: e.tensor_add(
                            big3[:, tb * 1024:(tb + 1) * 1024],
                            big3[:, tb * 1024:(tb + 1) * 1024], xblk[:]))
                        DMA(lambda e, row0=row0, tb=tb: e.dma_start(
                            out=x2_dram[row0:row0 + 128, :],
                            in_=big3[:, tb * 1024:(tb + 1) * 1024]))
                    else:
                        DMA(lambda e, row0=row0, tb=tb: e.dma_start(
                            out=cur_dram[row0:row0 + 128, :],
                            in_=big3[:, tb * 1024:(tb + 1) * 1024]))

        if mode == "attn":
            for b in range(NB):
                DMA(lambda e, b=b: e.dma_start(
                    out=xblk[:], in_=x2_dram[b * 128:(b + 1) * 128, :]))
                DMA(lambda e, b=b: e.dma_start(
                    out=OUT[b * 128:(b + 1) * 128, :], in_=xblk[:]))
        else:
            for b in range(NB):
                DMA(lambda e, b=b: e.dma_start(
                    out=xblk[:], in_=x2_dram[b * 128:(b + 1) * 128, :]))
                ln_of(xblk, zt_blk)
                DMA(lambda e, b=b: e.dma_start(
                    out=ln2_in[b * 128:(b + 1) * 128, :], in_=zt_blk[:]))
                for m in range(8):
                    O("tensor", lambda e, m=m: e.transpose(
                        out=pmT1[:], in_=zt_blk[:, m * 128:(m + 1) * 128],
                        identity=ident_sb[:]))
                    O("vector", lambda e: e.tensor_copy(out=ztr_ev[:],
                                                        in_=pmT1[:]))
                    O("tensor", lambda e, m=m: e.matmul(
                        out=pmT2[:, 0:8], lhsT=ztr_ev[:],
                        rhs=wgt_sb[:, m * 8:(m + 1) * 8],
                        start=(m == 0), stop=(m == 7)))
                O("vector", lambda e: e.tensor_copy(out=lg_sb[:],
                                                    in_=pmT2[:, 0:8]))
                O("vector", lambda e: e.max(out=mx_sb[:], in_=lg_sb[:]))
                O("vector", lambda e: e.max_index(
                    out=ix_sb[:], in_max=mx_sb[:], in_values=lg_sb[:]))
                O("vector", lambda e: e.tensor_copy(out=ixf_sb[:],
                                                    in_=ix_sb[:]))
                O("vector", lambda e: e.tensor_sub(
                    stat[:, 0:1], mx_sb[:, 1:2], mx_sb[:, 0:1]))
                O("scalar", lambda e: e.activation(
                    out=stat[:, 1:2], in_=stat[:, 0:1], func=AF.Exp))
                O("vector", lambda e: e.tensor_scalar(
                    out=stat[:, 2:3], in0=stat[:, 1:2], scalar1=1.0,
                    scalar2=None, op0=ALU.add))
                O("vector", lambda e: e.reciprocal(out=stat[:, 3:4],
                                                   in_=stat[:, 2:3]))
                O("vector", lambda e, b=b: e.tensor_copy(
                    out=probs_sb[:, b:b + 1], in_=stat[:, 3:4]))
                O("vector", lambda e, b=b: e.tensor_mul(
                    probs_sb[:, NB + b:NB + b + 1], stat[:, 1:2],
                    stat[:, 3:4]))
                for k in range(2):
                    O("vector", lambda e, k=k, b=b: e.tensor_copy(
                        out=esel_sb[:, k * NB + b:k * NB + b + 1],
                        in_=ixf_sb[:, k:k + 1]))
                    O("vector", lambda e, k=k: e.tensor_scalar(
                        out=ind_t[:], in0=iota8_sb[:],
                        scalar1=ixf_sb[:, k:k + 1], scalar2=None,
                        op0=ALU.is_equal))
                    O("tensor", lambda e: e.transpose(
                        out=pmT1[0:8, :], in_=ind_t[:], identity=ident_sb[:]))
                    O("vector", lambda e, k=k, b=b: e.tensor_copy(
                        out=ind_em[k][:, b * 128:(b + 1) * 128],
                        in_=pmT1[0:8, 0:128]))
            O("vector", lambda e: e.memset(zero_em, 0.0))
            for k in range(2):
                O("vector", lambda e, k=k: e.tensor_tensor_scan(
                    out=cum_em[k], data0=ind_em[k], data1=zero_em,
                    initial=0.0, op0=ALU.add, op1=ALU.add))
            DMA(lambda e: e.dma_start(out=cnt0_in.rearrange("a e -> e a"),
                                      in_=cum_em[0][:, TL - 1:TL]))
            DMA(lambda e: e.dma_start(out=cnt1_in.rearrange("a e -> e a"),
                                      in_=cum_em[1][:, TL - 1:TL]))
            CC(lambda e: e.collective_compute(
                "AllGather", ALU.bypass, replica_groups=rg,
                ins=[cnt0_in.opt()], outs=[cnt0_ag.opt()]))
            CC(lambda e: e.collective_compute(
                "AllGather", ALU.bypass, replica_groups=rg,
                ins=[cnt1_in.opt()], outs=[cnt1_ag.opt()]))
            DMA(lambda e: e.dma_start(out=cnts_sb[:], in_=cnt0_ag))
            O("tensor", lambda e: e.matmul(
                out=pmT2[0:8, 0:1], lhsT=cnts_sb[:], rhs=prev_sb[:],
                start=True, stop=True))
            O("tensor", lambda e: e.matmul(
                out=pmT2[0:8, 1:2], lhsT=cnts_sb[:], rhs=ones8_sb[:],
                start=True, stop=False))
            DMA(lambda e: e.dma_start(out=cnts_sb[:], in_=cnt1_ag))
            O("tensor", lambda e: e.matmul(
                out=pmT2[0:8, 1:2], lhsT=cnts_sb[:], rhs=prev_sb[:],
                start=False, stop=True))
            O("vector", lambda e: e.tensor_copy(out=base_sb[:],
                                                in_=pmT2[0:8, 0:2]))
            for k in range(2):
                for b in range(NB):
                    O("vector", lambda e, k=k, b=b: e.tensor_mul(
                        pk_em[:], ind_em[k][:, b * 128:(b + 1) * 128],
                        cum_em[k][:, b * 128:(b + 1) * 128]))
                    col = k * NB + b
                    O("tensor", lambda e, col=col: e.matmul(
                        out=pmT2[:, col:col + 1],
                        lhsT=pk_em[:], rhs=ones8_sb[:],
                        start=True, stop=False))
                    O("tensor", lambda e, k=k, b=b, col=col: e.matmul(
                        out=pmT2[:, col:col + 1],
                        lhsT=ind_em[k][:, b * 128:(b + 1) * 128],
                        rhs=base_sb[:, k:k + 1], start=False, stop=True))
            O("vector", lambda e: e.tensor_copy(out=rank_sb[:],
                                                in_=pmT2[:, 0:32]))
            O("vector", lambda e: e.tensor_scalar(
                out=valid_sb[:], in0=rank_sb[:], scalar1=float(CAP) + 0.5,
                scalar2=None, op0=ALU.is_lt))
            O("vector", lambda e: e.tensor_mul(probs_sb[:], probs_sb[:],
                                               valid_sb[:]))
            O("vector", lambda e: e.scalar_tensor_tensor(
                out=slot_sb[:], in0=esel_sb[:], scalar=float(CAP),
                in1=rank_sb[:], op0=ALU.mult, op1=ALU.add))
            O("vector", lambda e: e.tensor_scalar(
                out=slot_sb[:], in0=slot_sb[:], scalar1=BIG - 1.0,
                scalar2=None, op0=ALU.add))
            O("vector", lambda e: e.scalar_tensor_tensor(
                out=slot_sb[:], in0=valid_sb[:], scalar=-BIG,
                in1=slot_sb[:], op0=ALU.mult, op1=ALU.add))
            DMA(lambda e: e.dma_start(
                out=tokmap_in.rearrange("(a b) c -> a (b c)", a=128),
                in_=fill_sb[:]))
            for k in range(2):
                for b in range(NB):
                    col = k * NB + b
                    O("vector", lambda e, col=col: e.tensor_copy(
                        out=idx_f[:], in_=slot_sb[:, col:col + 1]))
                    O("vector", lambda e: e.tensor_copy(out=idx_u[:],
                                                        in_=idx_f[:]))
                    GDMA(lambda e, b=b: e.indirect_dma_start(
                        out=tokmap_in,
                        out_offset=IndirectOffsetOnAxis(idx_u[:, 0:1], axis=0),
                        in_=tokid_sb[:, b:b + 1], in_offset=None,
                        bounds_check=RB['tok'], oob_is_err=False))
            CC(lambda e: e.collective_compute(
                "AllReduce", ALU.min, replica_groups=rg,
                ins=[tokmap_in.opt()], outs=[tokmap_ar.opt()]))
            CC(lambda e: e.collective_compute(
                "AllGather", ALU.bypass, replica_groups=rg,
                ins=[ln2_in.opt()], outs=[ln2_ag.opt()]))
            for g in range(NG):
                for st in range(4):
                    srow = g * 512 + st * 128
                    O("vector", lambda e, srow=srow: e.tensor_scalar(
                        out=idx2_f[:], in0=ebase_col[:], scalar1=float(srow),
                        scalar2=None, op0=ALU.add))
                    O("vector", lambda e: e.tensor_copy(out=idx2_u[:],
                                                        in_=idx2_f[:]))
                    GDMA(lambda e: e.indirect_dma_start(
                        out=idx_f[:], out_offset=None, in_=tokmap_ar,
                        in_offset=IndirectOffsetOnAxis(idx2_u[:, 0:1], axis=0),
                        bounds_check=RB['tok'], oob_is_err=False))
                    O("vector", lambda e: e.tensor_copy(out=idx_u[:],
                                                        in_=idx_f[:]))
                    O("vector", lambda e, st=st: e.memset(
                        big2[:, 4096 + st * 1024: 4096 + (st + 1) * 1024],
                        0.0))
                    GDMA(lambda e, st=st: e.indirect_dma_start(
                        out=big2[:, 4096 + st * 1024: 4096 + (st + 1) * 1024],
                        out_offset=None, in_=ln2_ag,
                        in_offset=IndirectOffsetOnAxis(idx_u[:, 0:1], axis=0),
                        bounds_check=RB['ln'], oob_is_err=False))
                    for m in range(8):
                        O("tensor", lambda e, st=st, m=m: e.transpose(
                            out=pmT1[:],
                            in_=big2[:, 4096 + st * 1024 + m * 128:
                                     4096 + st * 1024 + (m + 1) * 128],
                            identity=ident_sb[:]))
                        O("vector", lambda e, st=st, m=m: e.tensor_copy(
                            out=big2[:, m * 512 + st * 128:
                                     m * 512 + (st + 1) * 128],
                            in_=pmT1[:]))
                for m in range(32):
                    DMA(lambda e, m=m: e.dma_start(
                        out=cfc_sb[:, 0:1024],
                        in_=CFC[:, m * 128:(m + 1) * 128]
                        .rearrange("(k p) c -> p k c", p=128)))
                    for kt in range(8):
                        O("tensor", lambda e, kt=kt: e.matmul(
                            out=pmA[:],
                            lhsT=cfc_sb[:, kt * 128:(kt + 1) * 128],
                            rhs=big2[:, kt * 512:(kt + 1) * 512],
                            start=(kt == 0), stop=(kt == 7)))
                    O("vector", lambda e, m=m: e.tensor_scalar(
                        out=big1[:, m * 512:(m + 1) * 512], in0=pmA[:],
                        scalar1=0.0, scalar2=None, op0=ALU.max))
                    O("scalar", lambda e, m=m: e.activation(
                        out=big1[:, m * 512:(m + 1) * 512],
                        in_=big1[:, m * 512:(m + 1) * 512], func=AF.Square))
                for m in range(8):
                    DMA(lambda e, m=m: e.dma_start(
                        out=cfc_sb[:],
                        in_=CPR[:, m * 128:(m + 1) * 128]
                        .rearrange("(k p) c -> p k c", p=128)))
                    for kt in range(32):
                        O("tensor", lambda e, kt=kt: e.matmul(
                            out=pmB[:],
                            lhsT=cfc_sb[:, kt * 128:(kt + 1) * 128],
                            rhs=big1[:, kt * 512:(kt + 1) * 512],
                            start=(kt == 0), stop=(kt == 31)))
                    O("vector", lambda e: e.tensor_copy(out=ev_sb[:],
                                                        in_=pmB[:]))
                    for st in range(4):
                        O("tensor", lambda e, st=st: e.transpose(
                            out=pmT1[:],
                            in_=ev_sb[:, st * 128:(st + 1) * 128],
                            identity=ident_sb[:]))
                        O("vector", lambda e, st=st, m=m: e.tensor_copy(
                            out=big3[:, st * 1024 + m * 128:
                                     st * 1024 + (m + 1) * 128],
                            in_=pmT1[:]))
                for st in range(4):
                    srow = g * 512 + st * 128
                    DMA(lambda e, srow=srow, st=st: e.dma_start(
                        out=eout_in[srow:srow + 128, :],
                        in_=big3[:, st * 1024:(st + 1) * 1024]))
            CC(lambda e: e.collective_compute(
                "AllGather", ALU.bypass, replica_groups=rg,
                ins=[eout_in.opt()], outs=[eout_ag.opt()]))
            for b in range(NB):
                for k in range(2):
                    col = k * NB + b
                    O("vector", lambda e, col=col: e.tensor_scalar(
                        out=idx_f[:], in0=rank_sb[:, col:col + 1],
                        scalar1=-1.0, scalar2=0.0, op0=ALU.add, op1=ALU.max))
                    O("vector", lambda e: e.tensor_scalar(
                        out=idx_f[:], in0=idx_f[:], scalar1=float(CAP - 1),
                        scalar2=None, op0=ALU.min))
                    O("vector", lambda e, col=col: e.scalar_tensor_tensor(
                        out=idx_f[:], in0=esel_sb[:, col:col + 1],
                        scalar=float(CAP), in1=idx_f[:],
                        op0=ALU.mult, op1=ALU.add))
                    O("vector", lambda e: e.tensor_copy(out=idx_u[:],
                                                        in_=idx_f[:]))
                    dstt = sq_blk if k == 0 else zt_blk
                    GDMA(lambda e, dstt=dstt: e.indirect_dma_start(
                        out=dstt[:], out_offset=None, in_=eout_ag,
                        in_offset=IndirectOffsetOnAxis(idx_u[:, 0:1], axis=0),
                        bounds_check=RB['tok'], oob_is_err=False))
                O("vector", lambda e, b=b: e.tensor_scalar(
                    out=xc_b[:], in0=sq_blk[:],
                    scalar1=probs_sb[:, b:b + 1], scalar2=None,
                    op0=ALU.mult))
                O("vector", lambda e, b=b: e.scalar_tensor_tensor(
                    out=xc_b[:], in0=zt_blk[:],
                    scalar=probs_sb[:, NB + b:NB + b + 1], in1=xc_b[:],
                    op0=ALU.mult, op1=ALU.add))
                DMA(lambda e, b=b: e.dma_start(
                    out=xblk[:], in_=x2_dram[b * 128:(b + 1) * 128, :]))
                O("vector", lambda e: e.tensor_add(xc_b[:, 0:D],
                                                   xc_b[:, 0:D], xblk[:]))
                DMA(lambda e, b=b: e.dma_start(
                    out=OUT[b * 128:(b + 1) * 128, :], in_=xc_b[:, 0:D]))

        if mode == "dbg":
            for b in range(NB):
                DMA(lambda e, b=b: e.dma_start(
                    out=xblk[:], in_=ln2_in[b * 128:(b + 1) * 128, :]))
                DMA(lambda e, b=b: e.dma_start(
                    out=DBG_LN2[b * 128:(b + 1) * 128, :], in_=xblk[:]))
                DMA(lambda e, b=b: e.dma_start(
                    out=xblk[:], in_=x2_dram[b * 128:(b + 1) * 128, :]))
                DMA(lambda e, b=b: e.dma_start(
                    out=DBG_X2[b * 128:(b + 1) * 128, :], in_=xblk[:]))
            DMA(lambda e: e.dma_start(
                out=fill_sb[:], in_=tokmap_ar.rearrange("(a b) c -> a (b c)", a=128)))
            DMA(lambda e: e.dma_start(
                out=DBG_TOK.rearrange("(a b) c -> a (b c)", a=128), in_=fill_sb[:]))
            DMA(lambda e: e.dma_start(out=DBG_RANK, in_=rank_sb[:]))
            DMA(lambda e: e.dma_start(out=DBG_PROB, in_=probs_sb[:]))
            DMA(lambda e: e.dma_start(out=DBG_ESEL, in_=esel_sb[:]))
            DMA(lambda e: e.dma_start(out=DBG_VAL, in_=valid_sb[:]))
            for q in range(CAP // 128):
                DMA(lambda e, q=q: e.dma_start(
                    out=xblk[:], in_=eout_in[q * 128:(q + 1) * 128, :]))
                DMA(lambda e, q=q: e.dma_start(
                    out=DBG_EOUT[q * 128:(q + 1) * 128, :], in_=xblk[:]))

        SEG = 30000
        total = 0
        pre = []
        for eng, fn, inc in ops:
            pre.append(total)
            total += inc
        seg_of = []
        seg_base = [0]
        for i in range(len(ops)):
            if pre[i] - seg_base[-1] >= SEG:
                seg_base.append(pre[i])
            seg_of.append(len(seg_base) - 1)
        nseg = len(seg_base)
        assert nseg <= NSEG, nseg
        seg_total = []
        for s in range(nseg):
            end = seg_base[s + 1] if s + 1 < nseg else total
            seg_total.append(end - seg_base[s])

        def run_engine(engine, name):
            cur_seg = 0
            for i, (eng, fn, inc) in enumerate(ops):
                if eng != name:
                    continue
                s = seg_of[i]
                while cur_seg < s:
                    engine.wait_ge(ordv[cur_seg], seg_total[cur_seg])
                    cur_seg += 1
                engine.wait_ge(ordv[s], pre[i] - seg_base[s])
                fn(engine).then_inc(ordv[s], inc)
            while cur_seg < nseg - 1:
                engine.wait_ge(ordv[cur_seg], seg_total[cur_seg])
                cur_seg += 1
            engine.wait_ge(ordv[nseg - 1], seg_total[nseg - 1])

        @block.sync
        def _(sync):
            run_engine(sync, "sync")

        @block.vector
        def _(vector):
            run_engine(vector, "vector")

        @block.scalar
        def _(scalar):
            run_engine(scalar, "scalar")

        @block.gpsimd
        def _(gpsimd):
            RB['y'] = gpsimd.alloc_register("rb_y")
            RB['tok'] = gpsimd.alloc_register("rb_tok")
            RB['ln'] = gpsimd.alloc_register("rb_ln")
            gpsimd.reg_mov(RB['y'], DIN * 32 - 1)
            gpsimd.reg_mov(RB['tok'], E * CAP - 1)
            gpsimd.reg_mov(RB['ln'], T - 1)
            run_engine(gpsimd, "gpsimd")

        @block.tensor
        def _(tensor):
            run_engine(tensor, "tensor")

    return nc


_cache = {}


def get_program(mode="full"):
    if mode not in _cache:
        _cache[mode] = build_program(mode)
    return _cache[mode]


def make_in_maps(inputs):
    x = np.ascontiguousarray(
        np.asarray(inputs["L0_attn_x"], dtype=np.float32).reshape(T, D))
    w_in = np.asarray(inputs["L0_attn_W_in"], dtype=np.float32)
    conv_w = np.asarray(inputs["L0_attn_conv_w"], dtype=np.float32)
    conv_b = np.asarray(inputs["L0_attn_conv_b"], dtype=np.float32)
    w_x = np.asarray(inputs["L0_attn_W_x"], dtype=np.float32)
    w_dt = np.asarray(inputs["L0_attn_W_dt"], dtype=np.float32)
    b_dt = np.asarray(inputs["L0_attn_b_dt"], dtype=np.float32)
    a_log = np.asarray(inputs["L0_attn_A_log"], dtype=np.float32)
    dp = np.asarray(inputs["L0_attn_Dp"], dtype=np.float32)
    w_out = np.ascontiguousarray(
        np.asarray(inputs["L0_attn_W_out"], dtype=np.float32))
    w_g = np.asarray(inputs["L0_ffn_w_g"], dtype=np.float32)
    c_fc = np.asarray(inputs["L0_ffn_c_fc"], dtype=np.float32)
    c_proj = np.asarray(inputs["L0_ffn_c_proj"], dtype=np.float32)

    in_maps = []
    for c in range(NC):
        dsl = slice(c * DL, (c + 1) * DL)
        m = {
            "x_shard": np.ascontiguousarray(x[c * TL:(c + 1) * TL, :]),
            "w_in_loc": np.ascontiguousarray(
                np.concatenate([w_in[:, dsl], w_in[:, DIN + c * DL:
                                                  DIN + (c + 1) * DL]],
                               axis=1)),
            "conv_w_loc": np.ascontiguousarray(conv_w[dsl, 0, :]),
            "conv_b_loc": np.ascontiguousarray(conv_b[dsl].reshape(DL, 1)),
            "w_x_loc": np.ascontiguousarray(w_x[dsl, :]),
            "w_dt_loc": np.ascontiguousarray(w_dt[:, dsl]),
            "b_dt_loc": np.ascontiguousarray(b_dt[dsl].reshape(DL, 1)),
            "a_log_loc": np.ascontiguousarray(a_log[dsl, :]),
            "dp_loc": np.ascontiguousarray(dp[dsl].reshape(DL, 1)),
            "w_out": w_out,
            "w_gT": np.ascontiguousarray(w_g.T),
            "c_fc_loc": np.ascontiguousarray(c_fc[c]),
            "c_proj_loc": np.ascontiguousarray(c_proj[c]),
            "tokid": np.ascontiguousarray(
                np.arange(c * TL, (c + 1) * TL, dtype=np.float32)
                .reshape(TL, 1)),
            "prevmask": np.ascontiguousarray(
                (np.arange(NC) < c).astype(np.float32).reshape(NC, 1)),
            "rankoff": np.array([[float(c)]], dtype=np.float32),
        }
        in_maps.append(m)
    return in_maps


def kernel(**inputs):
    nc = get_program("full")
    in_maps = make_in_maps(inputs)
    res = run_bass_kernel_spmd(nc, in_maps, core_ids=list(range(NC)))
    out = np.concatenate(
        [np.asarray(res.results[c]["out"]) for c in range(NC)], axis=0)
    return out.reshape(1, T, D).astype(np.float32)


# version 05
# speedup vs baseline: 1.9029x; 1.2894x over previous
import numpy as np
from contextlib import ExitStack
import concourse.bass as bass
import concourse.mybir as mybir
from concourse.bass import IndirectOffsetOnAxis
from concourse.bass_utils import run_bass_kernel_spmd

F32 = mybir.dt.float32
BF16 = mybir.dt.bfloat16
U32 = mybir.dt.uint32
AF = mybir.ActivationFunctionType
ALU = mybir.AluOpType
AX = mybir.AxisListType

NC = 8
T = 16384
D = 1024
DIN = 2048
S = 16
DTR = 64
E = 8
CAP = 5120
TL = T // NC
DL = DIN // NC
NITER = 4
TT = 1024
NJ = T // TT
NB = TL // 128
GSL = 1024
NG = CAP // GSL
SQD = float(np.sqrt(1024.0))
OOB = 1.0e9
BIG = 2097152.0


def build_program(mode="full"):
    nc = bass.Bass("TRN2", target_bir_lowering=False, debug=False)

    def ein(name, shape, dtype=F32):
        return nc.dram_tensor(name, list(shape), dtype, kind="ExternalInput").ap()

    X = ein("x_shard", (TL, D))
    WIN = ein("w_in_loc", (D, 2 * DL))
    CW = ein("conv_w_loc", (DL, 4))
    CB = ein("conv_b_loc", (DL, 1))
    WX = ein("w_x_loc", (DL, 96))
    WDT = ein("w_dt_loc", (DTR, DL))
    BDT = ein("b_dt_loc", (DL, 1))
    ALOG = ein("a_log_loc", (DL, S))
    DP = ein("dp_loc", (DL, 1))
    WOUT = ein("w_out", (DIN, D))
    WGT = ein("w_gT", (D, E))
    CFC = ein("c_fc_loc", (D, 4 * D))
    CPR = ein("c_proj_loc", (4 * D, D))
    TOKID = ein("tokid", (TL, 1))
    PREV = ein("prevmask", (NC, 1))
    RKOF = ein("rankoff", (1, 1))

    OUT = nc.dram_tensor("out", [TL, D], F32, kind="ExternalOutput").ap()
    if mode == "dbg":
        DBG_X2 = nc.dram_tensor("dbg_x2", [TL, D], F32,
                                kind="ExternalOutput").ap()

    def idr(name, shape, dtype=F32):
        return nc.dram_tensor(name, list(shape), dtype, kind="Internal").ap()

    cur_dram = idr("cur_dram", (TL, D))
    x2_dram = idr("x2_dram", (TL, D))
    z_in = idr("z_in", (D, TL), BF16)
    z_ag = idr("z_ag", (NC * D, TL), BF16)
    dbl_in = idr("dbl_in", (96 * NJ, TT))
    dbl_out = idr("dbl_out", (96 * NJ, TT))
    y_in = idr("y_in", (DL, T), BF16)
    y_ag = idr("y_ag", (DIN, T), BF16)
    ln2_in = idr("ln2_in", (TL, D), BF16)
    ln2_ag = idr("ln2_ag", (T, D), BF16)
    cnt0_in = idr("cnt0_in", (1, E))
    cnt0_ag = idr("cnt0_ag", (NC, E))
    cnt1_in = idr("cnt1_in", (1, E))
    cnt1_ag = idr("cnt1_ag", (NC, E))
    tokmap_in = idr("tokmap_in", (E * CAP, 1))
    tokmap_ar = idr("tokmap_ar", (E * CAP, 1))
    eout_in = idr("eout_in", (CAP, D))
    eout_ag = idr("eout_ag", (E * CAP, D))

    rg = [list(range(NC))]

    ops = []
    RB = {}

    def O(eng, fn, inc=1):
        ops.append((eng, fn, inc))

    def DMA(fn):
        O("sync", fn, 16)

    def GDMA(fn):
        O("gpsimd", fn, 16)

    def CC(fn):
        O("gpsimd", fn, 1)

    with ExitStack() as ctx:
        def sb(name, shape, dtype=F32):
            return ctx.enter_context(nc.sbuf_tensor(name, shape, dtype))

        def psum(name, shape, dtype=F32):
            return ctx.enter_context(nc.psum_tensor(name, shape, dtype))

        big1 = sb("big1", [128, S * TT])
        big1b = big1[:].bitcast(BF16)
        big2 = sb("big2", [128, 8 * TT])
        big2b = big2[:].bitcast(BF16)
        big3 = sb("big3", [128, 4352])
        sc1 = sb("sc1", [128, 4096])
        sc1b = sc1[:].bitcast(BF16)

        w_in_sb = sb("w_in_sb", [128, 8 * 512], BF16)
        w_x_sb = sb("w_x_sb", [128, 2 * 96], BF16)
        w_dt_sb = sb("w_dt_sb", [DTR, DL], BF16)
        wgt_sb = sb("wgt_sb", [128, 8 * E], BF16)
        conv_sb = sb("conv_sb", [128, 8])
        convb_sb = sb("convb_sb", [128, 2])
        bdt_sb = sb("bdt_sb", [128, 2])
        acol_sb = sb("acol_sb", [128, 2 * S])
        dp_sb = sb("dp_sb", [128, 2])
        ones_sb = sb("ones_sb", [1, 128])
        ident_sb = sb("ident_sb", [128, 128])
        identb_sb = sb("identb_sb", [128, 128], BF16)
        iota8_sb = sb("iota8_sb", [128, 8])
        tokid_sb = sb("tokid_sb", [128, NB])
        prev_sb = sb("prev_sb", [NC, 1])
        ones8_sb = sb("ones8_sb", [NC, 1])
        rkof_sb = sb("rkof_sb", [1, 1])
        rank_col = sb("rank_col", [128, 1])
        ebase_col = sb("ebase_col", [128, 1])
        idx2_f = sb("idx2_f", [128, 1])
        idx2_u = sb("idx2_u", [128, 1], U32)
        idx_f = sb("idx_f", [128, 1])
        idx_u = sb("idx_u", [128, 1], U32)
        cst_sb = sb("cst_sb", [128, 8])

        xc_b = sb("xc_b", [128, TT])
        a_buf = xc_b
        del_b = [sb(f"del_{d}", [128, TT]) for d in range(2)]
        du_b = [sb(f"du_{d}", [128, TT]) for d in range(2)]
        h_b = sb("h_b", [128, TT])
        ydt_b = [sb(f"ydt_{d}", [128, TT]) for d in range(2)]
        tmp_sb = sb("tmp_sb", [128, TT])
        zdt_sb = sb("zdt_sb", [128, TT])
        dtv_sb = h_b[0:DTR, :]
        dtb_sb = ydt_b[0][0:DTR, :].bitcast(BF16)[0:DTR, 0:TT]
        ubf_sb = zdt_sb[:].bitcast(BF16)[:, 0:TT]
        ybf_sb = xc_b[:].bitcast(BF16)[:, 0:TT]
        stageBC = sb("stageBC", [1, 2 * TT])
        stageB = stageBC[0:1, 0:TT]
        stageC = stageBC[0:1, TT:2 * TT]
        carry = sb("carry", [128, 2 * S])

        xblk = sb("xblk", [128, D])
        sq_blk = sb("sq_blk", [128, D])
        zt_blk = sb("zt_blk", [128, D])
        ztb_blk = sb("ztb_blk", [128, D], BF16)
        stat = sb("stat", [128, 8])
        ztr_ev = sb("ztr_ev", [128, 128])
        ev_sb = sb("ev_sb", [128, 512])

        lg_sb = sb("lg_sb", [128, E])
        mx_sb = sb("mx_sb", [128, 8])
        ix_sb = sb("ix_sb", [128, 8], U32)
        ixf_sb = sb("ixf_sb", [128, 8])
        ind_t = sb("ind_t", [128, E])
        ind_em = [big1[0:E, k * TL:(k + 1) * TL] for k in range(2)]
        cum_em = [big1[0:E, 2 * TL + k * TL:2 * TL + (k + 1) * TL]
                  for k in range(2)]
        zero_em = big1[0:E, 4 * TL:5 * TL]
        pk_em = big1[0:E, 5 * TL:5 * TL + 128]
        cnts_sb = sb("cnts_sb", [NC, E])
        base_sb = sb("base_sb", [E, 2])
        rank_sb = sb("rank_sb", [128, 2 * NB])
        valid_sb = sb("valid_sb", [128, 2 * NB])
        probs_sb = sb("probs_sb", [128, 2 * NB])
        esel_sb = sb("esel_sb", [128, 2 * NB])
        slot_sb = sb("slot_sb", [128, 2 * NB])
        fill_sb = sb("fill_sb", [128, 320])

        wout_bf = big1b[:, 0:16384]
        hmoe_bf = big1b[:, 0:32768]
        zsb_bf = [big2b[:, q * 8192:(q + 1) * 8192] for q in range(2)]
        ykq_bf = big2b[:, 0:8192]
        bufT_bf = big2b[:, 0:8192]
        bufrow_bf = big2b[:, 8192:16384]
        cfc_bf = sc1b[:, 0:8192]
        ztr8_bf = sc1b[:, 0:1024]

        def x_in_t(dt):
            return big3[:, dt * 1088: dt * 1088 + TT + 32]

        def u_t(dt):
            return x_in_t(dt)[:, 4:4 + TT]

        def res_t(dt):
            return big3[:, 2176 + dt * 1088: 2176 + dt * 1088 + TT]

        pmA = psum("pmA", [128, 512])
        pmB = psum("pmB", [128, 512])
        pmR1 = psum("pmR1", [128, TT])
        pmR2 = psum("pmR2", [128, TT])
        pmT1 = psum("pmT1", [128, 128])
        pmT2 = psum("pmT2", [128, 128])
        pmT1b = pmT1[:].bitcast(BF16)

        NSEG = 12
        ordv = [ctx.enter_context(nc.semaphore(f"ordv{i}"))
                for i in range(NSEG)]
        block = ctx.enter_context(nc.Block())

        CONSTS = [1e-5, 1e-30, 0.5, -0.5, -1.0, 1.0, 0.0]
        for ci, cv in enumerate(CONSTS):
            nc.const_aps.register(F32, cv, cst_sb[:, ci:ci + 1])
            O("vector", lambda e, ci=ci, cv=cv: e.memset(
                cst_sb[:, ci:ci + 1], cv))
        GDMA(lambda e: e.dma_start(
            out=w_in_sb[:], in_=WIN.rearrange("(k p) c -> p k c", p=128)))
        GDMA(lambda e: e.dma_start(
            out=w_x_sb[:], in_=WX.rearrange("(d p) c -> p d c", p=128)))
        GDMA(lambda e: e.dma_start(out=w_dt_sb[:], in_=WDT))
        GDMA(lambda e: e.dma_start(
            out=wgt_sb[:], in_=WGT.rearrange("(k p) c -> p k c", p=128)))
        DMA(lambda e: e.dma_start(
            out=conv_sb[:], in_=CW.rearrange("(d p) c -> p d c", p=128)))
        DMA(lambda e: e.dma_start(
            out=convb_sb[:], allow_slow_non_contiguous=True,
            in_=CB.rearrange("(d p) c -> p d c", p=128)[:, :, 0]))
        DMA(lambda e: e.dma_start(
            out=bdt_sb[:], allow_slow_non_contiguous=True,
            in_=BDT.rearrange("(d p) c -> p d c", p=128)[:, :, 0]))
        DMA(lambda e: e.dma_start(
            out=acol_sb[:], in_=ALOG.rearrange("(d p) s -> p d s", p=128)))
        DMA(lambda e: e.dma_start(
            out=dp_sb[:], allow_slow_non_contiguous=True,
            in_=DP.rearrange("(d p) c -> p d c", p=128)[:, :, 0]))
        DMA(lambda e: e.dma_start(
            out=tokid_sb[:], allow_slow_non_contiguous=True,
            in_=TOKID.rearrange("(b p) c -> p b c", p=128)[:, :, 0]))
        DMA(lambda e: e.dma_start(out=prev_sb[:], in_=PREV))
        DMA(lambda e: e.dma_start(out=rkof_sb[:], in_=RKOF))
        O("scalar", lambda e: e.activation(out=acol_sb[:], in_=acol_sb[:],
                                           func=AF.Exp))
        O("vector", lambda e: e.tensor_scalar(
            out=acol_sb[:], in0=acol_sb[:], scalar1=-1.0, scalar2=None,
            op0=ALU.mult))
        O("vector", lambda e: e.memset(ones_sb[:], 1.0))
        O("vector", lambda e: e.memset(ones8_sb[:], 1.0))
        O("gpsimd", lambda e: e.iota(out=ident_sb[:], pattern=[[1, 128]],
                                     base=0, channel_multiplier=-1,
                                     allow_small_or_imprecise_dtypes=True))
        O("vector", lambda e: e.tensor_scalar(
            out=ident_sb[:], in0=ident_sb[:], scalar1=0.0, scalar2=None,
            op0=ALU.is_equal))
        O("vector", lambda e: e.tensor_copy(out=identb_sb[:], in_=ident_sb[:]))
        O("gpsimd", lambda e: e.iota(out=iota8_sb[:], pattern=[[1, 8]],
                                     base=0, channel_multiplier=0,
                                     allow_small_or_imprecise_dtypes=True))
        O("vector", lambda e: e.memset(fill_sb[:], OOB))
        O("tensor", lambda e: e.matmul(out=pmT2[:, 0:1], lhsT=ones_sb[:],
                                       rhs=rkof_sb[:], start=True, stop=True))
        O("gpsimd", lambda e: e.iota(out=rank_col[:], pattern=[[1, 1]],
                                     base=0, channel_multiplier=32,
                                     allow_small_or_imprecise_dtypes=True))
        O("vector", lambda e: e.scalar_tensor_tensor(
            out=rank_col[:], in0=pmT2[:, 0:1], scalar=4.0, in1=rank_col[:],
            op0=ALU.mult, op1=ALU.add))
        O("gpsimd", lambda e: e.iota(out=ebase_col[:], pattern=[[1, 1]],
                                     base=0, channel_multiplier=1,
                                     allow_small_or_imprecise_dtypes=True))
        O("vector", lambda e: e.scalar_tensor_tensor(
            out=ebase_col[:], in0=pmT2[:, 0:1], scalar=float(CAP),
            in1=ebase_col[:], op0=ALU.mult, op1=ALU.add))

        def rms_of(src_tile, dst_tile):
            O("vector", lambda e: e.tensor_mul(sq_blk[:], src_tile[:],
                                               src_tile[:]))
            O("vector", lambda e: e.tensor_reduce(
                out=stat[:, 0:1], in_=sq_blk[:], axis=AX.X, op=ALU.add))
            O("scalar", lambda e: e.activation(
                out=stat[:, 3:4], in_=stat[:, 0:1], func=AF.Ln, bias=1e-30))
            O("scalar", lambda e: e.activation(
                out=stat[:, 3:4], in_=stat[:, 3:4], func=AF.Exp, scale=0.5))
            O("vector", lambda e: e.tensor_scalar(
                out=stat[:, 3:4], in0=stat[:, 3:4], scalar1=1e-12,
                scalar2=None, op0=ALU.max))
            O("vector", lambda e: e.reciprocal(out=stat[:, 4:5],
                                               in_=stat[:, 3:4]))
            O("vector", lambda e: e.tensor_scalar(
                out=stat[:, 4:5], in0=stat[:, 4:5], scalar1=SQD,
                scalar2=None, op0=ALU.mult))
            O("vector", lambda e: e.tensor_scalar(
                out=dst_tile[:], in0=src_tile[:], scalar1=stat[:, 4:5],
                scalar2=None, op0=ALU.mult))

        def ln_of(src_tile, dst_tile):
            O("vector", lambda e: e.tensor_mul(sq_blk[:], src_tile[:],
                                               src_tile[:]))
            O("vector", lambda e: e.tensor_reduce(
                out=stat[:, 0:1], in_=sq_blk[:], axis=AX.X, op=ALU.add))
            O("vector", lambda e: e.tensor_reduce(
                out=stat[:, 1:2], in_=src_tile[:], axis=AX.X, op=ALU.add))
            O("vector", lambda e: e.tensor_scalar(
                out=stat[:, 1:2], in0=stat[:, 1:2], scalar1=1.0 / D,
                scalar2=None, op0=ALU.mult))
            O("vector", lambda e: e.tensor_scalar(
                out=stat[:, 0:1], in0=stat[:, 0:1], scalar1=1.0 / D,
                scalar2=None, op0=ALU.mult))
            O("vector", lambda e: e.tensor_mul(stat[:, 2:3], stat[:, 1:2],
                                               stat[:, 1:2]))
            O("vector", lambda e: e.tensor_sub(stat[:, 0:1], stat[:, 0:1],
                                               stat[:, 2:3]))
            O("scalar", lambda e: e.activation(
                out=stat[:, 3:4], in_=stat[:, 0:1], func=AF.Ln, bias=1e-5))
            O("scalar", lambda e: e.activation(
                out=stat[:, 3:4], in_=stat[:, 3:4], func=AF.Exp, scale=-0.5))
            O("vector", lambda e: e.tensor_mul(stat[:, 4:5], stat[:, 1:2],
                                               stat[:, 3:4]))
            O("vector", lambda e: e.tensor_scalar(
                out=stat[:, 4:5], in0=stat[:, 4:5], scalar1=-1.0,
                scalar2=None, op0=ALU.mult))
            O("vector", lambda e: e.tensor_scalar(
                out=dst_tile[:], in0=src_tile[:], scalar1=stat[:, 3:4],
                scalar2=stat[:, 4:5], op0=ALU.mult, op1=ALU.add))

        for it in range(NITER):
            for b in range(NB):
                src = X if it == 0 else cur_dram
                DMA(lambda e, src=src, b=b: e.dma_start(
                    out=xblk[:], in_=src[b * 128:(b + 1) * 128, :]))
                if it == 0:
                    ln_of(xblk, zt_blk)
                    O("vector", lambda e: e.tensor_copy(out=xblk[:],
                                                        in_=zt_blk[:]))
                rms_of(xblk, ztb_blk)
                for m in range(8):
                    O("tensor", lambda e, m=m: e.transpose(
                        out=pmT1b[:, 0:128],
                        in_=ztb_blk[:, m * 128:(m + 1) * 128],
                        identity=identb_sb[:]))
                    O("vector", lambda e, m=m: e.tensor_copy(
                        out=ztr8_bf[:, m * 128:(m + 1) * 128],
                        in_=pmT1b[:, 0:128]))
                DMA(lambda e, b=b: e.dma_start(
                    out=z_in[:, b * 128:(b + 1) * 128]
                    .rearrange("(m p) t -> p m t", p=128),
                    in_=ztr8_bf.rearrange("p (m t) -> p m t", t=128)))
            CC(lambda e: e.collective_compute(
                "AllGather", ALU.bypass, replica_groups=rg,
                ins=[z_in.opt()], outs=[z_ag.opt()]))

            O("vector", lambda e: e.memset(carry[:], 0.0))
            for j in range(NJ):
                t0 = j * TT
                r = t0 // TL
                lc = t0 % TL
                zq = j % 2
                DMA(lambda e, r=r, lc=lc, zq=zq: e.dma_start(
                    out=zsb_bf[zq],
                    in_=z_ag[r * D:(r + 1) * D, lc:lc + TT]
                    .rearrange("(k p) t -> p k t", p=128)))
                for cti in range(4):
                    dt = cti % 2
                    is_res = cti >= 2
                    for ns in range(2):
                        pm = pmA if ns == 0 else pmB
                        for kt in range(8):
                            O("tensor", lambda e, cti=cti, ns=ns, kt=kt,
                              pm=pm, zq=zq: e.matmul(
                                  out=pm[:],
                                  lhsT=w_in_sb[:, kt * 512 + cti * 128:
                                               kt * 512 + (cti + 1) * 128],
                                  rhs=zsb_bf[zq][:, kt * TT + ns * 512:
                                                 kt * TT + (ns + 1) * 512],
                                  start=(kt == 0), stop=(kt == 7)))
                    for ns in range(2):
                        pm = pmA if ns == 0 else pmB
                        dst = (res_t(dt) if is_res else
                               x_in_t(dt)[:, 4:4 + TT])
                        O("vector", lambda e, dst=dst, ns=ns, pm=pm:
                          e.tensor_copy(
                              out=dst[:, ns * 512:(ns + 1) * 512],
                              in_=pm[:]))
                for dt in range(2):
                    if j == 0:
                        O("vector", lambda e, dt=dt: e.memset(
                            x_in_t(dt)[:, 0:4], 0.0))
                    O("vector", lambda e, dt=dt: e.tensor_scalar(
                        out=xc_b[:], in0=x_in_t(dt)[:, 1:1 + TT],
                        scalar1=conv_sb[:, dt * 4: dt * 4 + 1],
                        scalar2=convb_sb[:, dt:dt + 1],
                        op0=ALU.mult, op1=ALU.add))
                    for k in range(1, 4):
                        O("vector", lambda e, dt=dt, k=k:
                          e.scalar_tensor_tensor(
                              out=xc_b[:],
                              in0=x_in_t(dt)[:, 1 + k:1 + k + TT],
                              scalar=conv_sb[:, dt * 4 + k: dt * 4 + k + 1],
                              in1=xc_b[:], op0=ALU.mult, op1=ALU.add))
                    O("vector", lambda e, dt=dt: e.tensor_copy(
                        out=x_in_t(dt)[:, 0:4],
                        in_=x_in_t(dt)[:, TT:TT + 4]))
                    O("scalar", lambda e, dt=dt: e.activation(
                        out=u_t(dt)[:], in_=xc_b[:], func=AF.Silu))
                    O("vector", lambda e, dt=dt: e.tensor_copy(
                        out=ubf_sb[:], in_=u_t(dt)[:]))
                    for ns in range(2):
                        pm = pmA if ns == 0 else pmB
                        O("tensor", lambda e, dt=dt, ns=ns, pm=pm: e.matmul(
                            out=pm[0:96, :],
                            lhsT=w_x_sb[:, dt * 96:(dt + 1) * 96],
                            rhs=ubf_sb[:, ns * 512:(ns + 1) * 512],
                            start=(dt == 0), stop=(dt == 1)))
                        if dt == 1:
                            O("vector", lambda e, ns=ns, pm=pm: e.tensor_copy(
                                out=tmp_sb[0:96, ns * 512:(ns + 1) * 512],
                                in_=pm[0:96, :]))
                DMA(lambda e, j=j: e.dma_start(
                    out=dbl_in[96 * j:96 * (j + 1), :], in_=tmp_sb[0:96, :]))
                CC(lambda e, j=j: e.collective_compute(
                    "AllReduce", ALU.add, replica_groups=rg,
                    ins=[dbl_in[96 * j:96 * (j + 1), :].opt()],
                    outs=[dbl_out[96 * j:96 * (j + 1), :].opt()]))
                DMA(lambda e, j=j: e.dma_start(
                    out=dtv_sb[:], in_=dbl_out[96 * j:96 * j + DTR, :]))
                O("vector", lambda e: e.tensor_copy(out=dtb_sb[:],
                                                    in_=dtv_sb[:]))
                for dt in range(2):
                    for ns in range(2):
                        pm = pmA if ns == 0 else pmB
                        O("tensor", lambda e, dt=dt, ns=ns, pm=pm: e.matmul(
                            out=pm[:],
                            lhsT=w_dt_sb[:, dt * 128:(dt + 1) * 128],
                            rhs=dtb_sb[:, ns * 512:(ns + 1) * 512],
                            start=True, stop=True))
                        O("vector", lambda e, dt=dt, ns=ns, pm=pm:
                          e.tensor_scalar(
                              out=zdt_sb[:, ns * 512:(ns + 1) * 512],
                              in0=pm[:], scalar1=bdt_sb[:, dt:dt + 1],
                              scalar2=None, op0=ALU.add))
                    O("vector", lambda e: e.scalar_tensor_tensor(
                        out=tmp_sb[:], in0=zdt_sb[:], scalar=-1.0,
                        in1=zdt_sb[:], op0=ALU.mult, op1=ALU.max))
                    O("scalar", lambda e: e.activation(
                        out=tmp_sb[:], in_=tmp_sb[:], func=AF.Exp,
                        scale=-1.0))
                    O("scalar", lambda e: e.activation(
                        out=tmp_sb[:], in_=tmp_sb[:], func=AF.Ln, bias=1.0))
                    O("vector", lambda e, dt=dt: e.scalar_tensor_tensor(
                        out=del_b[dt][:], in0=zdt_sb[:], scalar=0.0,
                        in1=tmp_sb[:], op0=ALU.max, op1=ALU.add))
                    O("vector", lambda e, dt=dt: e.tensor_mul(
                        du_b[dt][:], del_b[dt][:], u_t(dt)[:]))
                for s in range(S):
                    DMA(lambda e, s=s, j=j: e.dma_start(
                        out=stageBC.ap().rearrange("a (r t) -> a r t", r=2),
                        in_=dbl_out[96 * j + 64 + s:96 * j + 81 + s:16, :]))
                    for h in range(2):
                        O("tensor", lambda e, h=h: e.matmul(
                            out=pmR1[:, h * 512:(h + 1) * 512],
                            lhsT=ones_sb[:],
                            rhs=stageB[:, h * 512:(h + 1) * 512],
                            start=True, stop=True))
                        O("tensor", lambda e, h=h: e.matmul(
                            out=pmR2[:, h * 512:(h + 1) * 512],
                            lhsT=ones_sb[:],
                            rhs=stageC[:, h * 512:(h + 1) * 512],
                            start=True, stop=True))
                    for dt in range(2):
                        cidx = dt * S + s
                        O("scalar", lambda e, dt=dt, s=s: e.activation(
                            out=a_buf[:], in_=del_b[dt][:], func=AF.Exp,
                            scale=acol_sb[:, dt * S + s: dt * S + s + 1]))
                        O("vector", lambda e, dt=dt: e.tensor_mul(
                            tmp_sb[:], du_b[dt][:], pmR1[:]))
                        O("vector", lambda e, cidx=cidx: e.tensor_tensor_scan(
                            out=h_b[:], data0=a_buf[:], data1=tmp_sb[:],
                            initial=carry[:, cidx:cidx + 1],
                            op0=ALU.mult, op1=ALU.add))
                        O("vector", lambda e, cidx=cidx: e.tensor_copy(
                            out=carry[:, cidx:cidx + 1],
                            in_=h_b[:, TT - 1:TT]))
                        if s == 0:
                            O("vector", lambda e, dt=dt: e.tensor_mul(
                                ydt_b[dt][:], h_b[:], pmR2[:]))
                        else:
                            O("vector", lambda e: e.tensor_mul(
                                tmp_sb[:], h_b[:], pmR2[:]))
                            O("vector", lambda e, dt=dt: e.tensor_add(
                                ydt_b[dt][:], ydt_b[dt][:], tmp_sb[:]))
                for dt in range(2):
                    O("vector", lambda e, dt=dt: e.scalar_tensor_tensor(
                        out=ydt_b[dt][:], in0=u_t(dt)[:],
                        scalar=dp_sb[:, dt:dt + 1], in1=ydt_b[dt][:],
                        op0=ALU.mult, op1=ALU.add))
                    O("scalar", lambda e, dt=dt: e.activation(
                        out=tmp_sb[:], in_=res_t(dt)[:], func=AF.Silu))
                    O("vector", lambda e, dt=dt: e.tensor_mul(
                        ybf_sb[:], ydt_b[dt][:], tmp_sb[:]))
                    DMA(lambda e, dt=dt, t0=t0: e.dma_start(
                        out=y_in[dt * 128:(dt + 1) * 128, t0:t0 + TT],
                        in_=ybf_sb[:]))
            CC(lambda e: e.collective_compute(
                "AllGather", ALU.bypass, replica_groups=rg,
                ins=[y_in.opt()], outs=[y_ag.opt()]))

            last = (it == NITER - 1)
            GDMA(lambda e: e.dma_start(
                out=wout_bf, in_=WOUT.rearrange("(k p) c -> p k c", p=128)))
            y_view = y_ag.rearrange("d (q t) -> (d q) t", q=32)
            for sub in range(4):
                for kt in range(16):
                    cst = float(sub + kt * 4096)
                    O("vector", lambda e, cst=cst: e.tensor_scalar(
                        out=idx_f[:], in0=rank_col[:], scalar1=cst,
                        scalar2=None, op0=ALU.add))
                    O("vector", lambda e: e.tensor_copy(out=idx_u[:],
                                                        in_=idx_f[:]))
                    GDMA(lambda e, kt=kt: e.indirect_dma_start(
                        out=ykq_bf[:, kt * 512:(kt + 1) * 512],
                        out_offset=None, in_=y_view,
                        in_offset=IndirectOffsetOnAxis(idx_u[:, 0:1], axis=0),
                        bounds_check=RB['y'], oob_is_err=False))
                for m in range(8):
                    for kt in range(16):
                        O("tensor", lambda e, m=m, kt=kt: e.matmul(
                            out=pmA[:],
                            lhsT=wout_bf[:, kt * 1024 + m * 128:
                                         kt * 1024 + (m + 1) * 128],
                            rhs=ykq_bf[:, kt * 512:(kt + 1) * 512],
                            start=(kt == 0), stop=(kt == 15)))
                    O("vector", lambda e: e.tensor_copy(out=ev_sb[:],
                                                        in_=pmA[:]))
                    for tb in range(4):
                        O("tensor", lambda e, tb=tb: e.transpose(
                            out=pmT1[:],
                            in_=ev_sb[:, tb * 128:(tb + 1) * 128],
                            identity=ident_sb[:]))
                        O("vector", lambda e, tb=tb, m=m: e.tensor_copy(
                            out=big3[:, tb * 1024 + m * 128:
                                     tb * 1024 + (m + 1) * 128],
                            in_=pmT1[:]))
                for tb in range(4):
                    row0 = sub * 512 + tb * 128
                    if last:
                        DMA(lambda e, row0=row0: e.dma_start(
                            out=xblk[:], in_=X[row0:row0 + 128, :]))
                        O("vector", lambda e, tb=tb: e.tensor_add(
                            big3[:, tb * 1024:(tb + 1) * 1024],
                            big3[:, tb * 1024:(tb + 1) * 1024], xblk[:]))
                        DMA(lambda e, row0=row0, tb=tb: e.dma_start(
                            out=x2_dram[row0:row0 + 128, :],
                            in_=big3[:, tb * 1024:(tb + 1) * 1024]))
                    else:
                        DMA(lambda e, row0=row0, tb=tb: e.dma_start(
                            out=cur_dram[row0:row0 + 128, :],
                            in_=big3[:, tb * 1024:(tb + 1) * 1024]))

        if mode == "attn":
            for b in range(NB):
                DMA(lambda e, b=b: e.dma_start(
                    out=xblk[:], in_=x2_dram[b * 128:(b + 1) * 128, :]))
                DMA(lambda e, b=b: e.dma_start(
                    out=OUT[b * 128:(b + 1) * 128, :], in_=xblk[:]))
        else:
            for b in range(NB):
                DMA(lambda e, b=b: e.dma_start(
                    out=xblk[:], in_=x2_dram[b * 128:(b + 1) * 128, :]))
                ln_of(xblk, zt_blk)
                O("vector", lambda e: e.tensor_copy(out=ztb_blk[:],
                                                    in_=zt_blk[:]))
                DMA(lambda e, b=b: e.dma_start(
                    out=ln2_in[b * 128:(b + 1) * 128, :], in_=ztb_blk[:]))
                for m in range(8):
                    O("tensor", lambda e, m=m: e.transpose(
                        out=pmT1b[:, 0:128],
                        in_=ztb_blk[:, m * 128:(m + 1) * 128],
                        identity=identb_sb[:]))
                    O("vector", lambda e: e.tensor_copy(
                        out=ztr_ev[:].bitcast(BF16)[:, 0:128],
                        in_=pmT1b[:, 0:128]))
                    O("tensor", lambda e, m=m: e.matmul(
                        out=pmT2[:, 0:8],
                        lhsT=ztr_ev[:].bitcast(BF16)[:, 0:128],
                        rhs=wgt_sb[:, m * 8:(m + 1) * 8],
                        start=(m == 0), stop=(m == 7)))
                O("vector", lambda e: e.tensor_copy(out=lg_sb[:],
                                                    in_=pmT2[:, 0:8]))
                O("vector", lambda e: e.max(out=mx_sb[:], in_=lg_sb[:]))
                O("vector", lambda e: e.max_index(
                    out=ix_sb[:], in_max=mx_sb[:], in_values=lg_sb[:]))
                O("vector", lambda e: e.tensor_copy(out=ixf_sb[:],
                                                    in_=ix_sb[:]))
                O("vector", lambda e: e.tensor_sub(
                    stat[:, 0:1], mx_sb[:, 1:2], mx_sb[:, 0:1]))
                O("scalar", lambda e: e.activation(
                    out=stat[:, 1:2], in_=stat[:, 0:1], func=AF.Exp))
                O("vector", lambda e: e.tensor_scalar(
                    out=stat[:, 2:3], in0=stat[:, 1:2], scalar1=1.0,
                    scalar2=None, op0=ALU.add))
                O("vector", lambda e: e.reciprocal(out=stat[:, 3:4],
                                                   in_=stat[:, 2:3]))
                O("vector", lambda e, b=b: e.tensor_copy(
                    out=probs_sb[:, b:b + 1], in_=stat[:, 3:4]))
                O("vector", lambda e, b=b: e.tensor_mul(
                    probs_sb[:, NB + b:NB + b + 1], stat[:, 1:2],
                    stat[:, 3:4]))
                for k in range(2):
                    O("vector", lambda e, k=k, b=b: e.tensor_copy(
                        out=esel_sb[:, k * NB + b:k * NB + b + 1],
                        in_=ixf_sb[:, k:k + 1]))
                    O("vector", lambda e, k=k: e.tensor_scalar(
                        out=ind_t[:], in0=iota8_sb[:],
                        scalar1=ixf_sb[:, k:k + 1], scalar2=None,
                        op0=ALU.is_equal))
                    O("tensor", lambda e: e.transpose(
                        out=pmT1[0:8, :], in_=ind_t[:],
                        identity=ident_sb[:]))
                    O("vector", lambda e, k=k, b=b: e.tensor_copy(
                        out=ind_em[k][:, b * 128:(b + 1) * 128],
                        in_=pmT1[0:8, 0:128]))
            O("vector", lambda e: e.memset(zero_em, 0.0))
            for k in range(2):
                O("vector", lambda e, k=k: e.tensor_tensor_scan(
                    out=cum_em[k], data0=ind_em[k], data1=zero_em,
                    initial=0.0, op0=ALU.add, op1=ALU.add))
            DMA(lambda e: e.dma_start(out=cnt0_in.rearrange("a e -> e a"),
                                      in_=cum_em[0][:, TL - 1:TL]))
            DMA(lambda e: e.dma_start(out=cnt1_in.rearrange("a e -> e a"),
                                      in_=cum_em[1][:, TL - 1:TL]))
            CC(lambda e: e.collective_compute(
                "AllGather", ALU.bypass, replica_groups=rg,
                ins=[cnt0_in.opt()], outs=[cnt0_ag.opt()]))
            CC(lambda e: e.collective_compute(
                "AllGather", ALU.bypass, replica_groups=rg,
                ins=[cnt1_in.opt()], outs=[cnt1_ag.opt()]))
            DMA(lambda e: e.dma_start(out=cnts_sb[:], in_=cnt0_ag))
            O("tensor", lambda e: e.matmul(
                out=pmT2[0:8, 0:1], lhsT=cnts_sb[:], rhs=prev_sb[:],
                start=True, stop=True))
            O("tensor", lambda e: e.matmul(
                out=pmT2[0:8, 1:2], lhsT=cnts_sb[:], rhs=ones8_sb[:],
                start=True, stop=False))
            DMA(lambda e: e.dma_start(out=cnts_sb[:], in_=cnt1_ag))
            O("tensor", lambda e: e.matmul(
                out=pmT2[0:8, 1:2], lhsT=cnts_sb[:], rhs=prev_sb[:],
                start=False, stop=True))
            O("vector", lambda e: e.tensor_copy(out=base_sb[:],
                                                in_=pmT2[0:8, 0:2]))
            for k in range(2):
                for b in range(NB):
                    col = k * NB + b
                    O("vector", lambda e, k=k, b=b: e.tensor_mul(
                        pk_em, ind_em[k][:, b * 128:(b + 1) * 128],
                        cum_em[k][:, b * 128:(b + 1) * 128]))
                    O("tensor", lambda e, col=col: e.matmul(
                        out=pmT2[:, col:col + 1],
                        lhsT=pk_em, rhs=ones8_sb[:],
                        start=True, stop=False))
                    O("tensor", lambda e, k=k, b=b, col=col: e.matmul(
                        out=pmT2[:, col:col + 1],
                        lhsT=ind_em[k][:, b * 128:(b + 1) * 128],
                        rhs=base_sb[:, k:k + 1], start=False, stop=True))
            O("vector", lambda e: e.tensor_copy(out=rank_sb[:],
                                                in_=pmT2[:, 0:32]))
            O("vector", lambda e: e.tensor_scalar(
                out=valid_sb[:], in0=rank_sb[:], scalar1=float(CAP) + 0.5,
                scalar2=None, op0=ALU.is_lt))
            O("vector", lambda e: e.tensor_mul(probs_sb[:], probs_sb[:],
                                               valid_sb[:]))
            O("vector", lambda e: e.scalar_tensor_tensor(
                out=slot_sb[:], in0=esel_sb[:], scalar=float(CAP),
                in1=rank_sb[:], op0=ALU.mult, op1=ALU.add))
            O("vector", lambda e: e.tensor_scalar(
                out=slot_sb[:], in0=slot_sb[:], scalar1=BIG - 1.0,
                scalar2=None, op0=ALU.add))
            O("vector", lambda e: e.scalar_tensor_tensor(
                out=slot_sb[:], in0=valid_sb[:], scalar=-BIG,
                in1=slot_sb[:], op0=ALU.mult, op1=ALU.add))
            DMA(lambda e: e.dma_start(
                out=tokmap_in.rearrange("(a b) c -> a (b c)", a=128),
                in_=fill_sb[:]))
            for k in range(2):
                for b in range(NB):
                    col = k * NB + b
                    O("vector", lambda e, col=col: e.tensor_copy(
                        out=idx_f[:], in_=slot_sb[:, col:col + 1]))
                    O("vector", lambda e: e.tensor_copy(out=idx_u[:],
                                                        in_=idx_f[:]))
                    GDMA(lambda e, b=b: e.indirect_dma_start(
                        out=tokmap_in,
                        out_offset=IndirectOffsetOnAxis(idx_u[:, 0:1],
                                                        axis=0),
                        in_=tokid_sb[:, b:b + 1], in_offset=None,
                        bounds_check=RB['tok'], oob_is_err=False))
            CC(lambda e: e.collective_compute(
                "AllReduce", ALU.min, replica_groups=rg,
                ins=[tokmap_in.opt()], outs=[tokmap_ar.opt()]))
            CC(lambda e: e.collective_compute(
                "AllGather", ALU.bypass, replica_groups=rg,
                ins=[ln2_in.opt()], outs=[ln2_ag.opt()]))
            for g in range(NG):
                for st in range(8):
                    srow = g * GSL + st * 128
                    O("vector", lambda e, srow=srow: e.tensor_scalar(
                        out=idx2_f[:], in0=ebase_col[:],
                        scalar1=float(srow), scalar2=None, op0=ALU.add))
                    O("vector", lambda e: e.tensor_copy(out=idx2_u[:],
                                                        in_=idx2_f[:]))
                    GDMA(lambda e: e.indirect_dma_start(
                        out=idx_f[:], out_offset=None, in_=tokmap_ar,
                        in_offset=IndirectOffsetOnAxis(idx2_u[:, 0:1],
                                                       axis=0),
                        bounds_check=RB['tok'], oob_is_err=False))
                    O("vector", lambda e: e.tensor_copy(out=idx_u[:],
                                                        in_=idx_f[:]))
                    O("vector", lambda e, st=st: e.memset(
                        bufrow_bf[:, st * D:(st + 1) * D], 0.0))
                    GDMA(lambda e, st=st: e.indirect_dma_start(
                        out=bufrow_bf[:, st * D:(st + 1) * D],
                        out_offset=None, in_=ln2_ag,
                        in_offset=IndirectOffsetOnAxis(idx_u[:, 0:1],
                                                       axis=0),
                        bounds_check=RB['ln'], oob_is_err=False))
                    for m in range(8):
                        O("tensor", lambda e, st=st, m=m: e.transpose(
                            out=pmT1b[:, 0:128],
                            in_=bufrow_bf[:, st * D + m * 128:
                                          st * D + (m + 1) * 128],
                            identity=identb_sb[:]))
                        O("vector", lambda e, st=st, m=m: e.tensor_copy(
                            out=bufT_bf[:, m * GSL + st * 128:
                                        m * GSL + (st + 1) * 128],
                            in_=pmT1b[:, 0:128]))
                for m in range(32):
                    GDMA(lambda e, m=m: e.dma_start(
                        out=cfc_bf[:, 0:1024],
                        in_=CFC[:, m * 128:(m + 1) * 128]
                        .rearrange("(k p) c -> p k c", p=128)))
                    for nsb in range(2):
                        pm = pmA if nsb == 0 else pmB
                        for kt in range(8):
                            O("tensor", lambda e, kt=kt, nsb=nsb, pm=pm:
                              e.matmul(
                                  out=pm[:],
                                  lhsT=cfc_bf[:, kt * 128:(kt + 1) * 128],
                                  rhs=bufT_bf[:, kt * GSL + nsb * 512:
                                              kt * GSL + (nsb + 1) * 512],
                                  start=(kt == 0), stop=(kt == 7)))
                        O("vector", lambda e, pm=pm: e.tensor_scalar(
                            out=ev_sb[:], in0=pm[:],
                            scalar1=0.0, scalar2=None, op0=ALU.max))
                        O("scalar", lambda e, m=m, nsb=nsb: e.activation(
                            out=hmoe_bf[:, m * GSL + nsb * 512:
                                        m * GSL + (nsb + 1) * 512],
                            in_=ev_sb[:], func=AF.Square))
                for nsb in range(2):
                    for m in range(8):
                        GDMA(lambda e, m=m: e.dma_start(
                            out=cfc_bf[:, 0:4096],
                            in_=CPR[:, m * 128:(m + 1) * 128]
                            .rearrange("(k p) c -> p k c", p=128)))
                        for kt in range(32):
                            O("tensor", lambda e, kt=kt, nsb=nsb: e.matmul(
                                out=pmB[:],
                                lhsT=cfc_bf[:, kt * 128:(kt + 1) * 128],
                                rhs=hmoe_bf[:, kt * GSL + nsb * 512:
                                            kt * GSL + (nsb + 1) * 512],
                                start=(kt == 0), stop=(kt == 31)))
                        O("vector", lambda e: e.tensor_copy(out=ev_sb[:],
                                                            in_=pmB[:]))
                        for st in range(4):
                            O("tensor", lambda e, st=st: e.transpose(
                                out=pmT1[:],
                                in_=ev_sb[:, st * 128:(st + 1) * 128],
                                identity=ident_sb[:]))
                            O("vector", lambda e, st=st, m=m: e.tensor_copy(
                                out=big3[:, st * 1024 + m * 128:
                                         st * 1024 + (m + 1) * 128],
                                in_=pmT1[:]))
                    for st in range(4):
                        srow = g * GSL + nsb * 512 + st * 128
                        DMA(lambda e, srow=srow, st=st: e.dma_start(
                            out=eout_in[srow:srow + 128, :],
                            in_=big3[:, st * 1024:(st + 1) * 1024]))
            CC(lambda e: e.collective_compute(
                "AllGather", ALU.bypass, replica_groups=rg,
                ins=[eout_in.opt()], outs=[eout_ag.opt()]))
            for b in range(NB):
                for k in range(2):
                    col = k * NB + b
                    O("vector", lambda e, col=col: e.tensor_scalar(
                        out=idx_f[:], in0=rank_sb[:, col:col + 1],
                        scalar1=-1.0, scalar2=0.0, op0=ALU.add,
                        op1=ALU.max))
                    O("vector", lambda e: e.tensor_scalar(
                        out=idx_f[:], in0=idx_f[:],
                        scalar1=float(CAP - 1), scalar2=None, op0=ALU.min))
                    O("vector", lambda e, col=col: e.scalar_tensor_tensor(
                        out=idx_f[:], in0=esel_sb[:, col:col + 1],
                        scalar=float(CAP), in1=idx_f[:],
                        op0=ALU.mult, op1=ALU.add))
                    O("vector", lambda e: e.tensor_copy(out=idx_u[:],
                                                        in_=idx_f[:]))
                    dstt = sq_blk if k == 0 else zt_blk
                    GDMA(lambda e, dstt=dstt: e.indirect_dma_start(
                        out=dstt[:], out_offset=None, in_=eout_ag,
                        in_offset=IndirectOffsetOnAxis(idx_u[:, 0:1],
                                                       axis=0),
                        bounds_check=RB['tok'], oob_is_err=False))
                O("vector", lambda e, b=b: e.tensor_scalar(
                    out=xc_b[:, 0:D], in0=sq_blk[:],
                    scalar1=probs_sb[:, b:b + 1], scalar2=None,
                    op0=ALU.mult))
                O("vector", lambda e, b=b: e.scalar_tensor_tensor(
                    out=xc_b[:, 0:D], in0=zt_blk[:],
                    scalar=probs_sb[:, NB + b:NB + b + 1],
                    in1=xc_b[:, 0:D], op0=ALU.mult, op1=ALU.add))
                DMA(lambda e, b=b: e.dma_start(
                    out=xblk[:], in_=x2_dram[b * 128:(b + 1) * 128, :]))
                O("vector", lambda e: e.tensor_add(xc_b[:, 0:D],
                                                   xc_b[:, 0:D], xblk[:]))
                DMA(lambda e, b=b: e.dma_start(
                    out=OUT[b * 128:(b + 1) * 128, :], in_=xc_b[:, 0:D]))
            if mode == "dbg":
                for b in range(NB):
                    DMA(lambda e, b=b: e.dma_start(
                        out=xblk[:], in_=x2_dram[b * 128:(b + 1) * 128, :]))
                    DMA(lambda e, b=b: e.dma_start(
                        out=DBG_X2[b * 128:(b + 1) * 128, :], in_=xblk[:]))

        SEG = 30000
        total = 0
        pre = []
        for eng, fn, inc in ops:
            pre.append(total)
            total += inc
        seg_of = []
        seg_base = [0]
        for i in range(len(ops)):
            if pre[i] - seg_base[-1] >= SEG:
                seg_base.append(pre[i])
            seg_of.append(len(seg_base) - 1)
        nseg = len(seg_base)
        assert nseg <= NSEG, nseg
        seg_total = []
        for sgi in range(nseg):
            end = seg_base[sgi + 1] if sgi + 1 < nseg else total
            seg_total.append(end - seg_base[sgi])

        def run_engine(engine, name):
            cur_seg = 0
            for i, (eng, fn, inc) in enumerate(ops):
                if eng != name:
                    continue
                sgi = seg_of[i]
                while cur_seg < sgi:
                    engine.wait_ge(ordv[cur_seg], seg_total[cur_seg])
                    cur_seg += 1
                engine.wait_ge(ordv[sgi], pre[i] - seg_base[sgi])
                fn(engine).then_inc(ordv[sgi], inc)
            while cur_seg < nseg - 1:
                engine.wait_ge(ordv[cur_seg], seg_total[cur_seg])
                cur_seg += 1
            engine.wait_ge(ordv[nseg - 1], seg_total[nseg - 1])

        @block.sync
        def _(sync):
            run_engine(sync, "sync")

        @block.vector
        def _(vector):
            run_engine(vector, "vector")

        @block.scalar
        def _(scalar):
            run_engine(scalar, "scalar")

        @block.gpsimd
        def _(gpsimd):
            RB['y'] = gpsimd.alloc_register("rb_y")
            RB['tok'] = gpsimd.alloc_register("rb_tok")
            RB['ln'] = gpsimd.alloc_register("rb_ln")
            gpsimd.reg_mov(RB['y'], DIN * 32 - 1)
            gpsimd.reg_mov(RB['tok'], E * CAP - 1)
            gpsimd.reg_mov(RB['ln'], T - 1)
            run_engine(gpsimd, "gpsimd")

        @block.tensor
        def _(tensor):
            run_engine(tensor, "tensor")

    return nc


_cache = {}


def get_program(mode="full"):
    if mode not in _cache:
        _cache[mode] = build_program(mode)
    return _cache[mode]


def make_in_maps(inputs):
    x = np.ascontiguousarray(
        np.asarray(inputs["L0_attn_x"], dtype=np.float32).reshape(T, D))
    w_in = np.asarray(inputs["L0_attn_W_in"], dtype=np.float32)
    conv_w = np.asarray(inputs["L0_attn_conv_w"], dtype=np.float32)
    conv_b = np.asarray(inputs["L0_attn_conv_b"], dtype=np.float32)
    w_x = np.asarray(inputs["L0_attn_W_x"], dtype=np.float32)
    w_dt = np.asarray(inputs["L0_attn_W_dt"], dtype=np.float32)
    b_dt = np.asarray(inputs["L0_attn_b_dt"], dtype=np.float32)
    a_log = np.asarray(inputs["L0_attn_A_log"], dtype=np.float32)
    dp = np.asarray(inputs["L0_attn_Dp"], dtype=np.float32)
    w_out = np.ascontiguousarray(
        np.asarray(inputs["L0_attn_W_out"], dtype=np.float32))
    w_g = np.asarray(inputs["L0_ffn_w_g"], dtype=np.float32)
    c_fc = np.asarray(inputs["L0_ffn_c_fc"], dtype=np.float32)
    c_proj = np.asarray(inputs["L0_ffn_c_proj"], dtype=np.float32)

    in_maps = []
    for c in range(NC):
        dsl = slice(c * DL, (c + 1) * DL)
        m = {
            "x_shard": np.ascontiguousarray(x[c * TL:(c + 1) * TL, :]),
            "w_in_loc": np.ascontiguousarray(
                np.concatenate([w_in[:, dsl], w_in[:, DIN + c * DL:
                                                  DIN + (c + 1) * DL]],
                               axis=1)),
            "conv_w_loc": np.ascontiguousarray(conv_w[dsl, 0, :]),
            "conv_b_loc": np.ascontiguousarray(conv_b[dsl].reshape(DL, 1)),
            "w_x_loc": np.ascontiguousarray(w_x[dsl, :]),
            "w_dt_loc": np.ascontiguousarray(w_dt[:, dsl]),
            "b_dt_loc": np.ascontiguousarray(b_dt[dsl].reshape(DL, 1)),
            "a_log_loc": np.ascontiguousarray(a_log[dsl, :]),
            "dp_loc": np.ascontiguousarray(dp[dsl].reshape(DL, 1)),
            "w_out": w_out,
            "w_gT": np.ascontiguousarray(w_g.T),
            "c_fc_loc": np.ascontiguousarray(c_fc[c]),
            "c_proj_loc": np.ascontiguousarray(c_proj[c]),
            "tokid": np.ascontiguousarray(
                np.arange(c * TL, (c + 1) * TL, dtype=np.float32)
                .reshape(TL, 1)),
            "prevmask": np.ascontiguousarray(
                (np.arange(NC) < c).astype(np.float32).reshape(NC, 1)),
            "rankoff": np.array([[float(c)]], dtype=np.float32),
        }
        in_maps.append(m)
    return in_maps


def kernel(**inputs):
    nc = get_program("full")
    in_maps = make_in_maps(inputs)
    res = run_bass_kernel_spmd(nc, in_maps, core_ids=list(range(NC)))
    out = np.concatenate(
        [np.asarray(res.results[c]["out"]) for c in range(NC)], axis=0)
    return out.reshape(1, T, D).astype(np.float32)


# version 12
# speedup vs baseline: 3.5425x; 1.8616x over previous
import numpy as np
from contextlib import ExitStack
import concourse.bass as bass
import concourse.mybir as mybir
from concourse.bass import IndirectOffsetOnAxis
from concourse.bass_utils import run_bass_kernel_spmd

F32 = mybir.dt.float32
BF16 = mybir.dt.bfloat16
U32 = mybir.dt.uint32
AF = mybir.ActivationFunctionType
ALU = mybir.AluOpType
AX = mybir.AxisListType

NC = 8
T = 16384
D = 1024
DIN = 2048
S = 16
DTR = 64
E = 8
CAP = 5120
TL = T // NC
DL = DIN // NC
NITER = 4
TT = 1024
NJ = T // TT
NB = TL // 128
GSL = 1024
NG = CAP // GSL
SQD = float(np.sqrt(1024.0))
OOB = 1.0e9
BIG = 2097152.0

ENGS = ("sync", "vector", "scalar", "gpsimd", "tensor")
KEEP_SAME = [False]
KEEP_ENGS = [set()]


class Sch:
    def __init__(self):
        self.ops = {e: [] for e in ENGS}
        self.cum = {e: 0 for e in ENGS}

    def op(self, eng, fn, inc=1, after=()):
        wd = {}
        if eng == "vector" and self.cum["vector"] > 0:
            wd["vector"] = self.cum["vector"]
        for t in after:
            if t is None:
                continue
            te, tv = t
            if tv <= 0:
                continue
            if te == eng and not (KEEP_SAME[0] or te in KEEP_ENGS[0]):
                continue
            if te == eng and tv >= self.cum[eng] + 1:
                continue
            wd[te] = max(wd.get(te, 0), tv)
        self.ops[eng].append((fn, inc, sorted(wd.items())))
        self.cum[eng] += inc
        return (eng, self.cum[eng])

    def tag(self, eng):
        return (eng, self.cum[eng])


def build_program(mode="full", strict=(), niter=NITER):
    nc = bass.Bass("TRN2", target_bir_lowering=False, debug=False)
    ST = {"cur": "p0", "last": None, "set": set(strict)}

    def ein(name, shape, dtype=F32):
        return nc.dram_tensor(name, list(shape), dtype,
                              kind="ExternalInput").ap()

    X = ein("x_shard", (TL, D))
    WIN = ein("w_in_loc", (D, 2 * DL))
    CW = ein("conv_w_loc", (DL, 4))
    CB = ein("conv_b_loc", (DL, 1))
    WX = ein("w_x_loc", (DL, 96))
    WDT = ein("w_dt_loc", (DTR + 1, DL))
    BDT = ein("b_dt_loc", (DL, 1))
    ALOG = ein("a_log_loc", (DL, S))
    DP = ein("dp_loc", (DL, 1))
    WOUT = ein("w_out", (DIN, D))
    WGT = ein("w_gT", (D, E))
    CFC = ein("c_fc_loc", (D, 4 * D))
    CPR = ein("c_proj_loc", (4 * D, D))
    TOKID = ein("tokid", (TL, 1))
    PREV = ein("prevmask", (NC, 1))
    RKOF = ein("rankoff", (1, 1))

    OUT = nc.dram_tensor("out", [TL, D], F32, kind="ExternalOutput").ap()

    def idr(name, shape, dtype=F32):
        return nc.dram_tensor(name, list(shape), dtype, kind="Internal").ap()

    cur_dram = idr("cur_dram", (TL, D))
    x2_dram = idr("x2_dram", (TL, D))
    z_in = idr("z_in", (D, TL), BF16)
    z_ag = idr("z_ag", (NC * D, TL), BF16)
    dbl_in = idr("dbl_in", (96 * NJ, TT))
    dbl_out = idr("dbl_out", (96 * NJ, TT))
    y_in = idr("y_in", (DL, T), BF16)
    y_ag = idr("y_ag", (DIN, T), BF16)
    ln2_in = idr("ln2_in", (TL, D), BF16)
    ln2_ag = idr("ln2_ag", (T, D), BF16)
    cnt0_in = idr("cnt0_in", (1, E))
    cnt0_ag = idr("cnt0_ag", (NC, E))
    cnt1_in = idr("cnt1_in", (1, E))
    cnt1_ag = idr("cnt1_ag", (NC, E))
    tokmap_in = idr("tokmap_in", (E * CAP, 1))
    tokmap_ar = idr("tokmap_ar", (E * CAP, 1))
    eout_in = idr("eout_in", (CAP, D))
    eout_ag = idr("eout_ag", (E * CAP, D))

    rg = [list(range(NC))]
    P = Sch()
    RB = {}

    def O(eng, fn, inc=1, after=()):
        if "all" in ST["set"] or ST["cur"] in ST["set"]:
            after = list(after) + [ST["last"]]
        t = P.op(eng, fn, inc, after)
        ST["last"] = t
        return t

    def DMA(fn, after=()):
        return O("sync", fn, 16, after)

    def GDMA(fn, after=()):
        return O("gpsimd", fn, 16, after)

    def CC(fn, after=()):
        return O("gpsimd", fn, 1, after)

    with ExitStack() as ctx:
        def sb(name, shape, dtype=F32):
            return ctx.enter_context(nc.sbuf_tensor(name, shape, dtype))

        def psum(name, shape, dtype=F32):
            return ctx.enter_context(nc.psum_tensor(name, shape, dtype))

        big1 = sb("big1", [128, S * TT])
        big1b = big1[:].bitcast(BF16)
        big2 = sb("big2", [128, 8 * TT])
        big2b = big2[:].bitcast(BF16)
        big3 = sb("big3", [128, 4352])
        sc1 = sb("sc1", [128, 4096])
        sc1b = sc1[:].bitcast(BF16)

        w_in_sb = sb("w_in_sb", [128, 8 * 512], BF16)
        w_x_sb = sb("w_x_sb", [128, 2 * 96], BF16)
        w_dt_sb = sb("w_dt_sb", [DTR + 1, DL], BF16)
        wgt_sb = sb("wgt_sb", [128, 8 * E], BF16)
        conv_sb = sb("conv_sb", [128, 8])
        convb_sb = sb("convb_sb", [128, 2])
        bdt_sb = sb("bdt_sb", [128, 2])
        acol_sb = sb("acol_sb", [128, 2 * S])
        dp_sb = sb("dp_sb", [128, 2])
        ones_sb = sb("ones_sb", [1, 128])
        ident_sb = sb("ident_sb", [128, 128])
        identb_sb = sb("identb_sb", [128, 128], BF16)
        iota8_sb = sb("iota8_sb", [128, 8])
        tokid_sb = sb("tokid_sb", [128, NB])
        prev_sb = sb("prev_sb", [NC, 1])
        ones8_sb = sb("ones8_sb", [NC, 1])
        rkof_sb = sb("rkof_sb", [1, 1])
        rank_col = sb("rank_col", [128, 1])
        ebase_col = sb("ebase_col", [128, 1])
        idx2_f = sb("idx2_f", [128, 1])
        idx2_u = sb("idx2_u", [128, 1], U32)
        idx_f = [sb(f"idx_f{q}", [128, 1]) for q in range(2)]
        idx_u = [sb(f"idx_u{q}", [128, 1], U32) for q in range(2)]
        cst_sb = sb("cst_sb", [128, 8])

        xc_b = sb("xc_b", [128, TT])
        a_buf = xc_b
        del_b = [sb(f"del_{d}", [128, TT]) for d in range(2)]
        du_b = [sb(f"du_{d}", [128, TT]) for d in range(2)]
        h_b = sb("h_b", [128, TT])
        ydt_b = [sb(f"ydt_{d}", [128, TT]) for d in range(2)]
        tmp_sb = sb("tmp_sb", [128, TT])
        zdt_sb = sb("zdt_sb", [128, TT])
        dtv_sb = xc_b[0:DTR, :]
        dtb_sb = big1b[0:DTR + 1, 30720:31744]
        ubf_sb = zdt_sb[:].bitcast(BF16)[:, 0:TT]
        carry = sb("carry", [128, 2 * S])
        a_dt = ydt_b
        bcBC = [big1[:, 8192:10240], big1[:, 10240:12288]]
        h2 = [h_b[:], big1[:, 12288:13312]]
        pbf = [big1b[:, 26624:27648], big1b[:, 27648:28672]]
        ybf2 = [big1b[:, 28672:29696], big1b[:, 29696:30720]]
        dpdiag = [big1b[:, 31744:31872], big1b[:, 31872:32000]]
        ubf2 = [sc1b[:, 0:1024], sc1b[:, 1024:2048]]
        gsil = sc1[:, 1024:2048]

        xblk = sb("xblk", [128, D])
        sq_blk = sb("sq_blk", [128, D])
        zt_blk = sb("zt_blk", [128, D])
        ztb_blk = sb("ztb_blk", [128, D], BF16)
        stat = sb("stat", [128, 8])
        ztr_ev = sb("ztr_ev", [128, 128])
        ev_sb = sb("ev_sb", [128, 512])

        lg_sb = sb("lg_sb", [128, E])
        mx_sb = sb("mx_sb", [128, 8])
        ix_sb = sb("ix_sb", [128, 8], U32)
        ixf_sb = sb("ixf_sb", [128, 8])
        ind_t = sb("ind_t", [128, E])
        ind_em = [big1[0:E, k * TL:(k + 1) * TL] for k in range(2)]
        cum_em = [big1[0:E, 2 * TL + k * TL:2 * TL + (k + 1) * TL]
                  for k in range(2)]
        zero_em = big1[0:E, 4 * TL:5 * TL]
        pk_em = big1[0:E, 5 * TL:5 * TL + 128]
        cnts_sb = sb("cnts_sb", [NC, E])
        base_sb = sb("base_sb", [E, 2])
        rank_sb = sb("rank_sb", [128, 2 * NB])
        valid_sb = sb("valid_sb", [128, 2 * NB])
        probs_sb = sb("probs_sb", [128, 2 * NB])
        esel_sb = sb("esel_sb", [128, 2 * NB])
        slot_sb = sb("slot_sb", [128, 2 * NB])
        fill_sb = sb("fill_sb", [128, 320])

        wout_bf = big1b[:, 0:16384]
        hmoe_bf = big1b[:, 0:32768]
        zsb_bf = [big2b[:, q * 8192:(q + 1) * 8192] for q in range(2)]
        ykq_bf = [big2b[:, q * 8192:(q + 1) * 8192] for q in range(2)]
        bufT_bf = big2b[:, 0:8192]
        bufrow_bf = big2b[:, 8192:16384]
        cfc1_bf = [sc1b[:, q * 1024:(q + 1) * 1024] for q in range(2)]
        cfc2_bf = [sc1b[:, q * 4096:(q + 1) * 4096] for q in range(2)]
        ztr8_bf = sc1b[:, 0:1024]

        def x_in_t(dt):
            return big3[:, dt * 1088: dt * 1088 + TT + 32]

        def u_t(dt):
            return x_in_t(dt)[:, 4:4 + TT]

        def res_t(dt):
            return big3[:, 2176 + dt * 1088: 2176 + dt * 1088 + TT]

        PA = psum("pmArena", [128, 4096])
        pmA = PA[:, 0:512]
        pmB = PA[:, 512:1024]
        b_ps = PA[:, 1024:2048]
        py = [PA[:, 2048:3072], PA[:, 3072:4096]]
        pmT1 = PA[:, 2048:2176]
        pmT2 = PA[:, 3072:3200]
        pmT1b = PA[:].bitcast(BF16)[:, 4096:4352]

        NSEG = 4
        sems = {e: [ctx.enter_context(nc.semaphore(f"sm_{e}{i}"))
                    for i in range(NSEG)] for e in ENGS}
        block = ctx.enter_context(nc.Block())

        FREE = {}

        def fr(key):
            return FREE.get(key)

        CONSTS = [1e-5, 1e-30, 0.5, -0.5, -1.0, 1.0, 0.0]
        for ci, cv in enumerate(CONSTS):
            nc.const_aps.register(F32, cv, cst_sb[:, ci:ci + 1])
            O("vector", lambda e, ci=ci, cv=cv: e.memset(
                cst_sb[:, ci:ci + 1], cv))
        t_win = GDMA(lambda e: e.dma_start(
            out=w_in_sb[:], in_=WIN.rearrange("(k p) c -> p k c", p=128)))
        t_wx = GDMA(lambda e: e.dma_start(
            out=w_x_sb[:], in_=WX.rearrange("(d p) c -> p d c", p=128)))
        t_wdt = GDMA(lambda e: e.dma_start(out=w_dt_sb[:], in_=WDT))
        t_wgt = GDMA(lambda e: e.dma_start(
            out=wgt_sb[:], in_=WGT.rearrange("(k p) c -> p k c", p=128)))
        t_wout = GDMA(lambda e: e.dma_start(
            out=wout_bf, in_=WOUT.rearrange("(k p) c -> p k c", p=128)))
        DMA(lambda e: e.dma_start(
            out=conv_sb[:], in_=CW.rearrange("(d p) c -> p d c", p=128)))
        DMA(lambda e: e.dma_start(
            out=convb_sb[:], allow_slow_non_contiguous=True,
            in_=CB.rearrange("(d p) c -> p d c", p=128)[:, :, 0]))
        DMA(lambda e: e.dma_start(
            out=bdt_sb[:], allow_slow_non_contiguous=True,
            in_=BDT.rearrange("(d p) c -> p d c", p=128)[:, :, 0]))
        t_alog = DMA(lambda e: e.dma_start(
            out=acol_sb[:], in_=ALOG.rearrange("(d p) s -> p d s", p=128)))
        t_dp = DMA(lambda e: e.dma_start(
            out=dp_sb[:], allow_slow_non_contiguous=True,
            in_=DP.rearrange("(d p) c -> p d c", p=128)[:, :, 0]))
        DMA(lambda e: e.dma_start(
            out=tokid_sb[:], allow_slow_non_contiguous=True,
            in_=TOKID.rearrange("(b p) c -> p b c", p=128)[:, :, 0]))
        DMA(lambda e: e.dma_start(out=prev_sb[:], in_=PREV))
        t_rk = DMA(lambda e: e.dma_start(out=rkof_sb[:], in_=RKOF))
        tse = O("scalar", lambda e: e.activation(
            out=acol_sb[:], in_=acol_sb[:], func=AF.Exp), after=[t_alog])
        t_acol = O("vector", lambda e: e.tensor_scalar(
            out=acol_sb[:], in0=acol_sb[:], scalar1=-1.0, scalar2=None,
            op0=ALU.mult), after=[tse])
        O("vector", lambda e: e.memset(ones_sb[:], 1.0))
        O("vector", lambda e: e.memset(ones8_sb[:], 1.0))
        t_id0 = O("gpsimd", lambda e: e.iota(
            out=ident_sb[:], pattern=[[1, 128]], base=0,
            channel_multiplier=-1, allow_small_or_imprecise_dtypes=True))
        t_id1 = O("vector", lambda e: e.tensor_scalar(
            out=ident_sb[:], in0=ident_sb[:], scalar1=0.0, scalar2=None,
            op0=ALU.is_equal), after=[t_id0])
        t_idb = O("vector", lambda e: e.tensor_copy(out=identb_sb[:],
                                                    in_=ident_sb[:]))
        O("vector", lambda e: e.memset(
            dtb_sb[DTR:DTR + 1, :], 1.0))
        t_dpd = None
        for dt in range(2):
            O("vector", lambda e, dt=dt: e.tensor_scalar(
                out=ztr_ev[:], in0=ident_sb[:],
                scalar1=dp_sb[:, dt:dt + 1], scalar2=None, op0=ALU.mult),
              after=[t_id1, t_dp])
            t_dpd = O("vector", lambda e, dt=dt: e.tensor_copy(
                out=dpdiag[dt], in_=ztr_ev[:]))
        O("gpsimd", lambda e: e.iota(out=iota8_sb[:], pattern=[[1, 8]],
                                     base=0, channel_multiplier=0,
                                     allow_small_or_imprecise_dtypes=True))
        O("vector", lambda e: e.memset(fill_sb[:], OOB))
        t_rmm = O("tensor", lambda e: e.matmul(
            out=pmT2[:, 0:1], lhsT=ones_sb[:], rhs=rkof_sb[:],
            start=True, stop=True), after=[t_rk, P.tag("vector")])
        tg0 = O("gpsimd", lambda e: e.iota(
            out=rank_col[:], pattern=[[1, 1]], base=0,
            channel_multiplier=32, allow_small_or_imprecise_dtypes=True))
        O("vector", lambda e: e.scalar_tensor_tensor(
            out=rank_col[:], in0=pmT2[:, 0:1], scalar=4.0, in1=rank_col[:],
            op0=ALU.mult, op1=ALU.add), after=[t_rmm, tg0])
        tg1 = O("gpsimd", lambda e: e.iota(
            out=ebase_col[:], pattern=[[1, 1]], base=0,
            channel_multiplier=1, allow_small_or_imprecise_dtypes=True))
        t_ebase = O("vector", lambda e: e.scalar_tensor_tensor(
            out=ebase_col[:], in0=pmT2[:, 0:1], scalar=float(CAP),
            in1=ebase_col[:], op0=ALU.mult, op1=ALU.add),
            after=[t_rmm, tg1])
        FREE["pmT2"] = t_ebase

        serial = {"last": None}

        def SER(eng, fn, inc=1, extra=()):
            t = O(eng, fn, inc,
                  ([serial["last"]] if serial["last"] else []) +
                  list(extra))
            serial["last"] = t
            return t

        def rms_of(src_tile, dst_tile):
            SER("vector", lambda e: e.tensor_mul(sq_blk[:], src_tile[:],
                                                 src_tile[:]))
            SER("vector", lambda e: e.tensor_reduce(
                out=stat[:, 0:1], in_=sq_blk[:], axis=AX.X, op=ALU.add))
            SER("scalar", lambda e: e.activation(
                out=stat[:, 3:4], in_=stat[:, 0:1], func=AF.Ln, bias=1e-30))
            SER("scalar", lambda e: e.activation(
                out=stat[:, 3:4], in_=stat[:, 3:4], func=AF.Exp, scale=0.5))
            SER("vector", lambda e: e.tensor_scalar(
                out=stat[:, 3:4], in0=stat[:, 3:4], scalar1=1e-12,
                scalar2=None, op0=ALU.max))
            SER("vector", lambda e: e.reciprocal(out=stat[:, 4:5],
                                                 in_=stat[:, 3:4]))
            SER("vector", lambda e: e.tensor_scalar(
                out=stat[:, 4:5], in0=stat[:, 4:5], scalar1=SQD,
                scalar2=None, op0=ALU.mult))
            return SER("vector", lambda e: e.tensor_scalar(
                out=dst_tile[:], in0=src_tile[:], scalar1=stat[:, 4:5],
                scalar2=None, op0=ALU.mult))

        def ln_of(src_tile, dst_tile):
            SER("vector", lambda e: e.tensor_mul(sq_blk[:], src_tile[:],
                                                 src_tile[:]))
            SER("vector", lambda e: e.tensor_reduce(
                out=stat[:, 0:1], in_=sq_blk[:], axis=AX.X, op=ALU.add))
            SER("vector", lambda e: e.tensor_reduce(
                out=stat[:, 1:2], in_=src_tile[:], axis=AX.X, op=ALU.add))
            SER("vector", lambda e: e.tensor_scalar(
                out=stat[:, 1:2], in0=stat[:, 1:2], scalar1=1.0 / D,
                scalar2=None, op0=ALU.mult))
            SER("vector", lambda e: e.tensor_scalar(
                out=stat[:, 0:1], in0=stat[:, 0:1], scalar1=1.0 / D,
                scalar2=None, op0=ALU.mult))
            SER("vector", lambda e: e.tensor_mul(stat[:, 2:3], stat[:, 1:2],
                                                 stat[:, 1:2]))
            SER("vector", lambda e: e.tensor_sub(stat[:, 0:1], stat[:, 0:1],
                                                 stat[:, 2:3]))
            SER("scalar", lambda e: e.activation(
                out=stat[:, 3:4], in_=stat[:, 0:1], func=AF.Ln, bias=1e-5))
            SER("scalar", lambda e: e.activation(
                out=stat[:, 3:4], in_=stat[:, 3:4], func=AF.Exp, scale=-0.5))
            SER("vector", lambda e: e.tensor_mul(stat[:, 4:5], stat[:, 1:2],
                                                 stat[:, 3:4]))
            SER("vector", lambda e: e.tensor_scalar(
                out=stat[:, 4:5], in0=stat[:, 4:5], scalar1=-1.0,
                scalar2=None, op0=ALU.mult))
            return SER("vector", lambda e: e.tensor_scalar(
                out=dst_tile[:], in0=src_tile[:], scalar1=stat[:, 3:4],
                scalar2=stat[:, 4:5], op0=ALU.mult, op1=ALU.add))

        ph0 = [P.tag(e) for e in ENGS]
        t_a3_store = None
        for it in range(niter):
            ST["cur"] = "a1"
            serial["last"] = None
            first_extra = ph0 if it == 0 else [t_a3_store]
            for b in range(NB):
                src = X if it == 0 else cur_dram
                SER("sync", lambda e, src=src, b=b: e.dma_start(
                    out=xblk[:], in_=src[b * 128:(b + 1) * 128, :]), 16,
                    extra=(first_extra if b == 0 else ()))
                if it == 0:
                    ln_of(xblk, zt_blk)
                    SER("vector", lambda e: e.tensor_copy(out=xblk[:],
                                                          in_=zt_blk[:]))
                rms_of(xblk, ztb_blk)
                for m in range(8):
                    SER("tensor", lambda e, m=m: e.transpose(
                        out=pmT1b[:, 0:128],
                        in_=ztb_blk[:, m * 128:(m + 1) * 128],
                        identity=identb_sb[:]),
                        extra=([t_idb] if (it == 0 and b == 0 and m == 0)
                               else ()))
                    SER("vector", lambda e, m=m: e.tensor_copy(
                        out=ztr8_bf[:, m * 128:(m + 1) * 128],
                        in_=pmT1b[:, 0:128]))
                SER("sync", lambda e, b=b: e.dma_start(
                    out=z_in[:, b * 128:(b + 1) * 128]
                    .rearrange("(m p) t -> p m t", p=128),
                    in_=ztr8_bf.rearrange("p (m t) -> p m t", t=128)), 16)
            t_zag = CC(lambda e: e.collective_compute(
                "AllGather", ALU.bypass, replica_groups=rg,
                ins=[z_in.opt()], outs=[z_ag.opt()]),
                after=[serial["last"]])
            FREE["pmT1"] = serial["last"]

            ST["cur"] = "a2"
            t_carry = O("vector", lambda e: e.memset(carry[:], 0.0),
                        after=[serial["last"], fr("carrylast")])
            CT = [None] * (2 * S)
            t_scan_dt = [None, None]
            t_exp_dt = [None, None]
            zload = [None, None]
            t_ar = [None] * NJ
            t_dbl = [None] * NJ
            t_del = [[None, None] for _ in range(NJ)]
            t_du = [[None, None] for _ in range(NJ)]
            t_silu = [None, None]
            t_ubf = [None, None]
            t_wxmm = [None] * NJ
            t_ystore = None

            def emit_pA1(j):
                zq = j % 2
                t0 = j * TT
                r = t0 // TL
                lc = t0 % TL
                zload[zq] = DMA(lambda e, r=r, lc=lc, zq=zq: e.dma_start(
                    out=zsb_bf[zq],
                    in_=z_ag[r * D:(r + 1) * D, lc:lc + TT]
                    .rearrange("(k p) t -> p k t", p=128)),
                    after=[t_zag, fr(("zfree", zq)), fr(("ykfree", zq)),
                           t_win])
                for cti in range(2):
                    dt = cti
                    tev = None
                    for ns in range(2):
                        pmk = "pmA" if ns == 0 else "pmB"
                        pm = pmA if ns == 0 else pmB
                        tmm = None
                        for kt in range(8):
                            tmm = O("tensor",
                                    lambda e, cti=cti, ns=ns, kt=kt, pm=pm,
                                    zq=zq: e.matmul(
                                        out=pm[:],
                                        lhsT=w_in_sb[
                                            :, kt * 512 + cti * 128:
                                            kt * 512 + (cti + 1) * 128],
                                        rhs=zsb_bf[zq][
                                            :, kt * TT + ns * 512:
                                            kt * TT + (ns + 1) * 512],
                                        start=(kt == 0), stop=(kt == 7)),
                                    after=([zload[zq], fr(pmk)]
                                           if kt == 0 else ()))
                        FREE[("zfree", zq)] = tmm
                        tev = O("scalar",
                                lambda e, dt=dt, ns=ns, pm=pm: e.activation(
                                    out=x_in_t(dt)[:, 4 + ns * 512:
                                                   4 + (ns + 1) * 512],
                                    in_=pm[:], func=AF.Copy),
                                after=[tmm, fr(("xinfree", dt)),
                                       fr(("ufree", dt))])
                        FREE[pmk] = tev
                    FREE[("xev", dt)] = tev

            def emit_csu(j, dt):
                if True:
                    caft = [fr(("xev", dt)), fr("dtbfree")]
                    if it == 0 and j == 0 and dt == 0:
                        caft = caft + ph0
                    if j == 0:
                        O("vector", lambda e, dt=dt: e.memset(
                            x_in_t(dt)[:, 0:4], 0.0),
                          after=[t_carry] + ([fr(("xinfree", dt))]
                                             if it > 0 else []))
                    O("vector", lambda e, dt=dt: e.tensor_scalar(
                        out=xc_b[:], in0=x_in_t(dt)[:, 1:1 + TT],
                        scalar1=conv_sb[:, dt * 4: dt * 4 + 1],
                        scalar2=convb_sb[:, dt:dt + 1],
                        op0=ALU.mult, op1=ALU.add), after=caft)
                    for k in range(1, 4):
                        O("vector", lambda e, dt=dt, k=k:
                          e.scalar_tensor_tensor(
                              out=xc_b[:],
                              in0=x_in_t(dt)[:, 1 + k:1 + k + TT],
                              scalar=conv_sb[:, dt * 4 + k:
                                             dt * 4 + k + 1],
                              in1=xc_b[:], op0=ALU.mult, op1=ALU.add))
                    t_halo = O("vector", lambda e, dt=dt: e.tensor_copy(
                        out=x_in_t(dt)[:, 0:4],
                        in_=x_in_t(dt)[:, TT:TT + 4]))
                    FREE[("xinfree", dt)] = t_halo
                    t_silu[dt] = O("scalar", lambda e, dt=dt: e.activation(
                        out=u_t(dt)[:], in_=xc_b[:], func=AF.Silu),
                        after=[t_halo, fr(("ufree", dt))])
                    t_ubf[dt] = O("scalar", lambda e, dt=dt: e.activation(
                        out=ubf2[dt], in_=u_t(dt)[:], func=AF.Copy),
                        after=[fr(("ubffree", dt))])

            def emit_wx(j):
                for ns in range(2):
                    pmk = "pmA" if ns == 0 else "pmB"
                    pm = pmA if ns == 0 else pmB
                    tmm = None
                    for dt in range(2):
                        tmm = O("tensor",
                                lambda e, dt=dt, ns=ns, pm=pm: e.matmul(
                                    out=pm[0:96, :],
                                    lhsT=w_x_sb[:, dt * 96:(dt + 1) * 96],
                                    rhs=ubf2[dt][:, ns * 512:(ns + 1) * 512],
                                    start=(dt == 0), stop=(dt == 1)),
                                after=([t_ubf[dt], t_wx, fr(pmk)]
                                       if dt == 0 else [t_ubf[dt]]))
                    tev = O("vector",
                            lambda e, ns=ns, pm=pm: e.tensor_copy(
                                out=tmp_sb[0:96,
                                           ns * 512:(ns + 1) * 512],
                                in_=pm[0:96, :]),
                            after=[tmm, fr("tmpfree")])
                    FREE[pmk] = tev
                t_wxmm[j] = P.tag("tensor")
                FREE[("ubffree", 0)] = t_wxmm[j]
                FREE[("ubffree", 1)] = t_wxmm[j]
                t_dbl[j] = DMA(lambda e, j=j: e.dma_start(
                    out=dbl_in[96 * j:96 * (j + 1), :],
                    in_=tmp_sb[0:96, :]), after=[P.tag("vector")])

            def emit_ar(j):
                t_ar[j] = CC(lambda e, j=j: e.collective_compute(
                    "AllReduce", ALU.add, replica_groups=rg,
                    ins=[dbl_in[96 * j:96 * (j + 1), :].opt()],
                    outs=[dbl_out[96 * j:96 * (j + 1), :].opt()]),
                    after=[t_dbl[j]])

            def emit_bc(j, s):
                bq = s % 2
                base = dbl_out[96 * j + 64 + s:96 * j + 81 + s:16, :]
                bc_in = bass.AP(base.tensor, base.offset,
                                [[0, 128]] + [list(x) for x in base.ap])
                t = O("scalar", lambda e, bq=bq, bc_in=bc_in: e.dma_start(
                    out=bcBC[bq], in_=bc_in), 16,
                    after=[t_ar[j], fr(("bcb", bq)), fr(("bcp", bq))])
                FREE[("bc", j, s)] = t

            def emit_pB(j):
                t_dtv = DMA(lambda e, j=j: e.dma_start(
                    out=dtv_sb[:], in_=dbl_out[96 * j:96 * j + DTR, :]),
                    after=[t_ar[j], t_silu[1]])
                t_dtb = O("vector", lambda e: e.tensor_copy(
                    out=dtb_sb[0:DTR, :], in_=dtv_sb[:]),
                    after=[t_dtv, fr("dtbmm")])
                FREE["dtbfree"] = t_dtb
                for dt in range(2):
                    for ns in range(2):
                        pmk = "pmA" if ns == 0 else "pmB"
                        pm = pmA if ns == 0 else pmB
                        tmm = O("tensor",
                                lambda e, dt=dt, ns=ns, pm=pm: e.matmul(
                                    out=pm[:],
                                    lhsT=w_dt_sb[:,
                                                 dt * 128:(dt + 1) * 128],
                                    rhs=dtb_sb[:, ns * 512:(ns + 1) * 512],
                                    start=True, stop=True),
                                after=[t_dtb, t_wdt, fr(pmk)])
                        FREE["dtbmm"] = tmm
                        tz = O("scalar",
                               lambda e, dt=dt, ns=ns, pm=pm: e.activation(
                                   out=zdt_sb[:, ns * 512:(ns + 1) * 512],
                                   in_=pm[:], func=AF.Copy),
                               after=[tmm, fr("zdtfree")])
                        FREE[pmk] = tz
                    t_zev = P.tag("scalar")
                    O("vector", lambda e: e.scalar_tensor_tensor(
                        out=tmp_sb[:], in0=zdt_sb[:], scalar=-1.0,
                        in1=zdt_sb[:], op0=ALU.mult, op1=ALU.max),
                      after=[t_zev, fr("tmpfree2")])
                    ts1 = O("scalar", lambda e: e.activation(
                        out=tmp_sb[:], in_=tmp_sb[:], func=AF.Exp,
                        scale=-1.0), after=[P.tag("vector")])
                    ts2 = O("scalar", lambda e: e.activation(
                        out=tmp_sb[:], in_=tmp_sb[:], func=AF.Ln, bias=1.0))
                    t_del[j][dt] = O("vector",
                                     lambda e, dt=dt:
                                     e.scalar_tensor_tensor(
                                         out=del_b[dt][:], in0=zdt_sb[:],
                                         scalar=0.0, in1=tmp_sb[:],
                                         op0=ALU.max, op1=ALU.add),
                                     after=[ts2, t_exp_dt[dt]])
                    FREE["zdtfree"] = t_del[j][dt]
                    t_du[j][dt] = O("vector", lambda e, dt=dt: e.tensor_mul(
                        du_b[dt][:], del_b[dt][:], u_t(dt)[:]))
                    FREE[("ufree", dt)] = t_du[j][dt]
                FREE["tmpfree2"] = t_del[j][1]
                FREE["tmpfree"] = t_del[j][1]
                emit_bc(j, 0)
                emit_bc(j, 1)

            def emit_sloop(j):
                for s in range(S):
                    bq = s % 2
                    t_bc = fr(("bc", j, s))
                    for dt in range(2):
                        hq = (s * 2 + dt) % 2
                        cidx = dt * S + s
                        eaft = [t_del[j][dt], t_scan_dt[dt]]
                        if it == 0 and j == 0 and s == 0 and dt == 0:
                            eaft = eaft + [t_acol] + ph0
                        t_exp = O("scalar",
                                  lambda e, dt=dt, s=s: e.activation(
                                      out=a_dt[dt][:], in_=del_b[dt][:],
                                      func=AF.Exp,
                                      scale=acol_sb[:, dt * S + s:
                                                    dt * S + s + 1]),
                                  after=eaft)
                        t_exp_dt[dt] = t_exp
                        t_b = O("vector",
                                lambda e, dt=dt, bq=bq: e.tensor_tensor(
                                    out=b_ps, in0=du_b[dt][:],
                                    in1=bcBC[bq][:, 0:TT], op=ALU.mult),
                                after=[t_bc, t_du[j][dt]])
                        t_scan = O("vector",
                                   lambda e, dt=dt, hq=hq, cidx=cidx:
                                   e.tensor_tensor_scan(
                                       out=h2[hq], data0=a_dt[dt][:],
                                       data1=b_ps,
                                       initial=carry[:, cidx:cidx + 1],
                                       op0=ALU.mult, op1=ALU.add),
                                   after=[t_exp, CT[cidx],
                                          fr(("hp", hq)), fr(("hc", hq))])
                        t_scan_dt[dt] = t_scan
                        t_p = O("gpsimd",
                                lambda e, hq=hq, bq=bq: e.tensor_tensor(
                                    out=pbf[hq], in0=h2[hq],
                                    in1=bcBC[bq][:, TT:2 * TT],
                                    op=ALU.mult),
                                after=[t_scan, fr(("pbffree", hq))])
                        FREE[("hp", hq)] = t_p
                        if dt == 1:
                            FREE[("bcb", bq)] = t_b
                            FREE[("bcp", bq)] = t_p
                        t_pe = None
                        for ns in range(2):
                            t_pe = O("tensor",
                                     lambda e, dt=dt, ns=ns, hq=hq, s=s:
                                     e.matmul(
                                         out=py[dt][:, ns * 512:
                                                     (ns + 1) * 512],
                                         lhsT=identb_sb[:],
                                         rhs=pbf[hq][:, ns * 512:
                                                     (ns + 1) * 512],
                                         start=False, stop=(s == S - 1)),
                                     after=([t_p] if ns == 0 else ()))
                        FREE[("pbffree", hq)] = t_pe
                        if s == S - 1:
                            FREE[("pystop", dt)] = t_pe
                        t_cc = O("scalar",
                                 lambda e, hq=hq, cidx=cidx: e.activation(
                                     out=carry[:, cidx:cidx + 1],
                                     in_=h2[hq][:, TT - 1:TT],
                                     func=AF.Copy),
                                 after=[t_scan])
                        CT[cidx] = t_cc
                        FREE[("hc", hq)] = t_cc
                    if s + 2 < S:
                        emit_bc(j, s + 2)
                    if j + 1 < NJ:
                        if s == 1:
                            emit_csu(j + 1, 0)
                        if s == 2:
                            emit_csu(j + 1, 1)
                        if s == 4:
                            emit_wx(j + 1)
                        if s == 7:
                            emit_ar(j + 1)

            def emit_pydp(j):
                for dt in range(2):
                    tdp = None
                    for ns in range(2):
                        tdp = O("tensor",
                                lambda e, dt=dt, ns=ns: e.matmul(
                                    out=py[dt][:, ns * 512:(ns + 1) * 512],
                                    lhsT=dpdiag[dt],
                                    rhs=ubf2[dt][:, ns * 512:(ns + 1) * 512],
                                    start=True, stop=False),
                                after=([t_ubf[dt], fr(("pyfree", dt)),
                                        t_dpd, t_idb] if ns == 0 else ()))
                    FREE[("ubffree", dt)] = tdp

            def emit_gate(j):
                nonlocal t_ystore
                t0 = j * TT
                for dt in range(2):
                    t_gs = O("scalar", lambda e, dt=dt: e.activation(
                        out=gsil, in_=res_t(dt)[:], func=AF.Silu),
                        after=[fr(("resev", dt)), fr("gsilfree")])
                    t_ybf = O("vector", lambda e, dt=dt: e.tensor_tensor(
                        out=ybf2[dt], in0=py[dt], in1=gsil, op=ALU.mult),
                        after=[t_gs, fr(("pystop", dt)),
                               fr(("ybffree", dt))])
                    FREE[("pyfree", dt)] = t_ybf
                    FREE["gsilfree"] = t_ybf
                    t_ystore = DMA(lambda e, dt=dt, t0=t0: e.dma_start(
                        out=y_in[dt * 128:(dt + 1) * 128, t0:t0 + TT],
                        in_=ybf2[dt]), after=[t_ybf])
                    FREE[("ybffree", dt)] = t_ystore

            def emit_pA2(j):
                zq = j % 2
                for cti in (2, 3):
                    dt = cti % 2
                    tev = None
                    for ns in range(2):
                        pmk = "pmA" if ns == 0 else "pmB"
                        pm = pmA if ns == 0 else pmB
                        tmm = None
                        for kt in range(8):
                            tmm = O("tensor",
                                    lambda e, cti=cti, ns=ns, kt=kt, pm=pm,
                                    zq=zq: e.matmul(
                                        out=pm[:],
                                        lhsT=w_in_sb[
                                            :, kt * 512 + cti * 128:
                                            kt * 512 + (cti + 1) * 128],
                                        rhs=zsb_bf[zq][
                                            :, kt * TT + ns * 512:
                                            kt * TT + (ns + 1) * 512],
                                        start=(kt == 0), stop=(kt == 7)),
                                    after=([zload[zq], fr(pmk)]
                                           if kt == 0 else ()))
                        FREE[("zfree", zq)] = tmm
                        tev = O("scalar",
                                lambda e, dt=dt, ns=ns, pm=pm: e.activation(
                                    out=res_t(dt)[:, ns * 512:
                                                  (ns + 1) * 512],
                                    in_=pm[:], func=AF.Copy),
                                after=[tmm])
                        FREE[pmk] = tev
                    FREE[("resev", dt)] = tev

            emit_pA1(0)
            emit_csu(0, 0)
            emit_csu(0, 1)
            emit_wx(0)
            emit_ar(0)
            emit_pB(0)
            emit_pA2(0)
            for j in range(NJ):
                if j + 1 < NJ:
                    emit_pA1(j + 1)
                emit_pydp(j)
                emit_sloop(j)
                if j + 1 < NJ:
                    emit_pB(j + 1)
                emit_gate(j)
                if j + 1 < NJ:
                    emit_pA2(j + 1)
            FREE["carrylast"] = CT[2 * S - 1]
            t_yag = CC(lambda e: e.collective_compute(
                "AllGather", ALU.bypass, replica_groups=rg,
                ins=[y_in.opt()], outs=[y_ag.opt()]), after=[t_ystore])

            ST["cur"] = "a3"
            last = (it == niter - 1)
            y_view = y_ag.rearrange("d (q t) -> (d q) t", q=32)
            t_big3 = None
            for sub in range(4):
                q = sub % 2
                t_lastgath = None
                ST["cur"] = "a3g"
                for kt in range(16):
                    cst = float(sub + kt * 4096)
                    iq = kt % 2
                    O("vector", lambda e, cst=cst, iq=iq:
                      e.tensor_scalar(
                          out=idx_f[iq][:], in0=rank_col[:],
                          scalar1=cst, scalar2=None, op0=ALU.add),
                      after=[fr(("idxg", iq))])
                    tiu = O("vector", lambda e, iq=iq: e.tensor_copy(
                        out=idx_u[iq][:], in_=idx_f[iq][:]))
                    t_lastgath = GDMA(
                        lambda e, kt=kt, q=q, iq=iq: e.indirect_dma_start(
                            out=ykq_bf[q][:, kt * 512:(kt + 1) * 512],
                            out_offset=None, in_=y_view,
                            in_offset=IndirectOffsetOnAxis(
                                idx_u[iq][:, 0:1], axis=0),
                            bounds_check=RB['y'], oob_is_err=False),
                        after=[tiu, t_yag, fr(("ykfree", q)),
                               fr(("zfree", q)), t_wout,
                               (("tensor", P.cum["tensor"])
                                if kt == 0 else None)])
                    FREE[("idxg", iq)] = t_lastgath
                ST["cur"] = "a3m"
                for m in range(8):
                    pmk = "pmA" if m % 2 == 0 else "pmB"
                    pm = pmA if m % 2 == 0 else pmB
                    tmm = None
                    for kt in range(16):
                        tmm = O("tensor",
                                lambda e, m=m, kt=kt, pm=pm, q=q: e.matmul(
                                    out=pm[:],
                                    lhsT=wout_bf[:, kt * 1024 + m * 128:
                                                 kt * 1024 +
                                                 (m + 1) * 128],
                                    rhs=ykq_bf[q][:,
                                                  kt * 512:
                                                  (kt + 1) * 512],
                                    start=(kt == 0), stop=(kt == 15)),
                                after=([t_lastgath, fr(pmk)]
                                       if kt == 0 else ()))
                    FREE[("ykfree", q)] = tmm
                    tev = O("vector", lambda e, pm=pm: e.tensor_copy(
                        out=ev_sb[:], in_=pm[:]),
                        after=[tmm, fr("evfree")])
                    FREE[pmk] = tev
                    for tb in range(4):
                        ttr = O("tensor", lambda e, tb=tb: e.transpose(
                            out=pmT1[:],
                            in_=ev_sb[:, tb * 128:(tb + 1) * 128],
                            identity=ident_sb[:]),
                            after=[tev, fr("pmT1")])
                        tbw = O("vector", lambda e, tb=tb, m=m:
                                e.tensor_copy(
                                    out=big3[:, tb * 1024 + m * 128:
                                             tb * 1024 + (m + 1) * 128],
                                    in_=pmT1[:]),
                                after=[ttr, t_big3])
                        FREE["pmT1"] = tbw
                    FREE["evfree"] = P.tag("tensor")
                ST["cur"] = "a3s"
                tv_last = P.tag("vector")
                for tb in range(4):
                    row0 = sub * 512 + tb * 128
                    if last:
                        t_xl = DMA(lambda e, row0=row0: e.dma_start(
                            out=xblk[:], in_=X[row0:row0 + 128, :]),
                            after=[fr("xblk")])
                        t_add = O("vector", lambda e, tb=tb: e.tensor_add(
                            big3[:, tb * 1024:(tb + 1) * 1024],
                            big3[:, tb * 1024:(tb + 1) * 1024], xblk[:]),
                            after=[t_xl])
                        FREE["xblk"] = t_add
                        t_a3_store = DMA(
                            lambda e, row0=row0, tb=tb: e.dma_start(
                                out=x2_dram[row0:row0 + 128, :],
                                in_=big3[:, tb * 1024:(tb + 1) * 1024]),
                            after=[t_add])
                    else:
                        t_a3_store = DMA(
                            lambda e, row0=row0, tb=tb: e.dma_start(
                                out=cur_dram[row0:row0 + 128, :],
                                in_=big3[:, tb * 1024:(tb + 1) * 1024]),
                            after=[tv_last])
                t_big3 = t_a3_store

        ST["cur"] = "route"
        serial["last"] = t_a3_store
        for b in range(NB):
            SER("sync", lambda e, b=b: e.dma_start(
                out=xblk[:], in_=x2_dram[b * 128:(b + 1) * 128, :]), 16)
            ln_of(xblk, zt_blk)
            SER("vector", lambda e: e.tensor_copy(out=ztb_blk[:],
                                                  in_=zt_blk[:]))
            SER("sync", lambda e, b=b: e.dma_start(
                out=ln2_in[b * 128:(b + 1) * 128, :], in_=ztb_blk[:]), 16)
            for m in range(8):
                SER("tensor", lambda e, m=m: e.transpose(
                    out=pmT1b[:, 0:128],
                    in_=ztb_blk[:, m * 128:(m + 1) * 128],
                    identity=identb_sb[:]))
                SER("vector", lambda e: e.tensor_copy(
                    out=ztr_ev[:].bitcast(BF16)[:, 0:128],
                    in_=pmT1b[:, 0:128]))
                SER("tensor", lambda e, m=m: e.matmul(
                    out=pmT2[:, 0:8],
                    lhsT=ztr_ev[:].bitcast(BF16)[:, 0:128],
                    rhs=wgt_sb[:, m * 8:(m + 1) * 8],
                    start=(m == 0), stop=(m == 7)),
                    extra=([t_wgt, fr("pmT2")] if (b == 0 and m == 0)
                           else ()))
            SER("vector", lambda e: e.tensor_copy(out=lg_sb[:],
                                                  in_=pmT2[:, 0:8]))
            SER("vector", lambda e: e.max(out=mx_sb[:], in_=lg_sb[:]))
            SER("vector", lambda e: e.max_index(
                out=ix_sb[:], in_max=mx_sb[:], in_values=lg_sb[:]))
            SER("vector", lambda e: e.tensor_copy(out=ixf_sb[:],
                                                  in_=ix_sb[:]))
            SER("vector", lambda e: e.tensor_sub(
                stat[:, 0:1], mx_sb[:, 1:2], mx_sb[:, 0:1]))
            SER("scalar", lambda e: e.activation(
                out=stat[:, 1:2], in_=stat[:, 0:1], func=AF.Exp))
            SER("vector", lambda e: e.tensor_scalar(
                out=stat[:, 2:3], in0=stat[:, 1:2], scalar1=1.0,
                scalar2=None, op0=ALU.add))
            SER("vector", lambda e: e.reciprocal(out=stat[:, 3:4],
                                                 in_=stat[:, 2:3]))
            SER("vector", lambda e, b=b: e.tensor_copy(
                out=probs_sb[:, b:b + 1], in_=stat[:, 3:4]))
            SER("vector", lambda e, b=b: e.tensor_mul(
                probs_sb[:, NB + b:NB + b + 1], stat[:, 1:2],
                stat[:, 3:4]))
            for k in range(2):
                SER("vector", lambda e, k=k, b=b: e.tensor_copy(
                    out=esel_sb[:, k * NB + b:k * NB + b + 1],
                    in_=ixf_sb[:, k:k + 1]))
                SER("vector", lambda e, k=k: e.tensor_scalar(
                    out=ind_t[:], in0=iota8_sb[:],
                    scalar1=ixf_sb[:, k:k + 1], scalar2=None,
                    op0=ALU.is_equal))
                SER("tensor", lambda e: e.transpose(
                    out=pmT1[0:8, :], in_=ind_t[:], identity=ident_sb[:]))
                SER("vector", lambda e, k=k, b=b: e.tensor_copy(
                    out=ind_em[k][:, b * 128:(b + 1) * 128],
                    in_=pmT1[0:8, 0:128]))
        SER("vector", lambda e: e.memset(zero_em, 0.0))
        for k in range(2):
            SER("vector", lambda e, k=k: e.tensor_tensor_scan(
                out=cum_em[k], data0=ind_em[k], data1=zero_em,
                initial=0.0, op0=ALU.add, op1=ALU.add))
        SER("sync", lambda e: e.dma_start(
            out=cnt0_in.rearrange("a e -> e a"),
            in_=cum_em[0][:, TL - 1:TL]), 16)
        SER("sync", lambda e: e.dma_start(
            out=cnt1_in.rearrange("a e -> e a"),
            in_=cum_em[1][:, TL - 1:TL]), 16)
        SER("gpsimd", lambda e: e.collective_compute(
            "AllGather", ALU.bypass, replica_groups=rg,
            ins=[cnt0_in.opt()], outs=[cnt0_ag.opt()]), 1)
        SER("gpsimd", lambda e: e.collective_compute(
            "AllGather", ALU.bypass, replica_groups=rg,
            ins=[cnt1_in.opt()], outs=[cnt1_ag.opt()]), 1)
        SER("sync", lambda e: e.dma_start(out=cnts_sb[:], in_=cnt0_ag), 16)
        SER("tensor", lambda e: e.matmul(
            out=pmT2[0:8, 0:1], lhsT=cnts_sb[:], rhs=prev_sb[:],
            start=True, stop=True))
        SER("tensor", lambda e: e.matmul(
            out=pmT2[0:8, 1:2], lhsT=cnts_sb[:], rhs=ones8_sb[:],
            start=True, stop=False))
        SER("sync", lambda e: e.dma_start(out=cnts_sb[:], in_=cnt1_ag), 16)
        SER("tensor", lambda e: e.matmul(
            out=pmT2[0:8, 1:2], lhsT=cnts_sb[:], rhs=prev_sb[:],
            start=False, stop=True))
        SER("vector", lambda e: e.tensor_copy(out=base_sb[:],
                                              in_=pmT2[0:8, 0:2]))
        for k in range(2):
            for b in range(NB):
                col = k * NB + b
                SER("vector", lambda e, k=k, b=b: e.tensor_mul(
                    pk_em, ind_em[k][:, b * 128:(b + 1) * 128],
                    cum_em[k][:, b * 128:(b + 1) * 128]))
                SER("tensor", lambda e, col=col: e.matmul(
                    out=pmT2[:, col:col + 1], lhsT=pk_em,
                    rhs=ones8_sb[:], start=True, stop=False))
                SER("tensor", lambda e, k=k, b=b, col=col: e.matmul(
                    out=pmT2[:, col:col + 1],
                    lhsT=ind_em[k][:, b * 128:(b + 1) * 128],
                    rhs=base_sb[:, k:k + 1], start=False, stop=True))
        SER("vector", lambda e: e.tensor_copy(out=rank_sb[:],
                                              in_=pmT2[:, 0:32]))
        SER("vector", lambda e: e.tensor_scalar(
            out=valid_sb[:], in0=rank_sb[:], scalar1=float(CAP) + 0.5,
            scalar2=None, op0=ALU.is_lt))
        SER("vector", lambda e: e.tensor_mul(probs_sb[:], probs_sb[:],
                                             valid_sb[:]))
        SER("vector", lambda e: e.scalar_tensor_tensor(
            out=slot_sb[:], in0=esel_sb[:], scalar=float(CAP),
            in1=rank_sb[:], op0=ALU.mult, op1=ALU.add))
        SER("vector", lambda e: e.tensor_scalar(
            out=slot_sb[:], in0=slot_sb[:], scalar1=BIG - 1.0,
            scalar2=None, op0=ALU.add))
        SER("vector", lambda e: e.scalar_tensor_tensor(
            out=slot_sb[:], in0=valid_sb[:], scalar=-BIG,
            in1=slot_sb[:], op0=ALU.mult, op1=ALU.add))
        SER("sync", lambda e: e.dma_start(
            out=tokmap_in.rearrange("(a b) c -> a (b c)", a=128),
            in_=fill_sb[:]), 16)
        for k in range(2):
            for b in range(NB):
                col = k * NB + b
                SER("vector", lambda e, col=col: e.tensor_copy(
                    out=idx_f[0][:], in_=slot_sb[:, col:col + 1]))
                SER("vector", lambda e: e.tensor_copy(out=idx_u[0][:],
                                                      in_=idx_f[0][:]))
                SER("gpsimd", lambda e, b=b: e.indirect_dma_start(
                    out=tokmap_in,
                    out_offset=IndirectOffsetOnAxis(idx_u[0][:, 0:1],
                                                    axis=0),
                    in_=tokid_sb[:, b:b + 1], in_offset=None,
                    bounds_check=RB['tok'], oob_is_err=False), 16)
        SER("gpsimd", lambda e: e.collective_compute(
            "AllReduce", ALU.min, replica_groups=rg,
            ins=[tokmap_in.opt()], outs=[tokmap_ar.opt()]), 1)
        t_ln2ag = SER("gpsimd", lambda e: e.collective_compute(
            "AllGather", ALU.bypass, replica_groups=rg,
            ins=[ln2_in.opt()], outs=[ln2_ag.opt()]), 1)

        ST["cur"] = "moe"
        t_mm1_done = None
        t_mm2_done = None
        t_eo_store = t_a3_store
        for g in range(NG):
            t_bufT = None
            ST["cur"] = "moeg"
            for st in range(8):
                srow = g * GSL + st * 128
                O("vector", lambda e, srow=srow: e.tensor_scalar(
                    out=idx2_f[:], in0=ebase_col[:], scalar1=float(srow),
                    scalar2=None, op0=ALU.add),
                  after=[fr("idx2"), t_ebase, serial["last"]])
                t2 = O("vector", lambda e: e.tensor_copy(out=idx2_u[:],
                                                         in_=idx2_f[:]))
                t3 = GDMA(lambda e: e.indirect_dma_start(
                    out=idx_f[0][:], out_offset=None, in_=tokmap_ar,
                    in_offset=IndirectOffsetOnAxis(idx2_u[:, 0:1], axis=0),
                    bounds_check=RB['tok'], oob_is_err=False),
                    after=[t2, fr("idxm"), t_ln2ag])
                FREE["idx2"] = t3
                t4 = O("vector", lambda e: e.tensor_copy(out=idx_u[0][:],
                                                         in_=idx_f[0][:]),
                       after=[t3])
                t5 = O("vector", lambda e, st=st: e.memset(
                    bufrow_bf[:, st * D:(st + 1) * D], 0.0),
                    after=[fr(("brow", st)), t_ln2ag])
                t6 = GDMA(lambda e, st=st: e.indirect_dma_start(
                    out=bufrow_bf[:, st * D:(st + 1) * D],
                    out_offset=None, in_=ln2_ag,
                    in_offset=IndirectOffsetOnAxis(idx_u[0][:, 0:1],
                                                   axis=0),
                    bounds_check=RB['ln'], oob_is_err=False),
                    after=[t4, t5])
                FREE["idxm"] = t6
                for m in range(8):
                    ttr = O("tensor", lambda e, st=st, m=m: e.transpose(
                        out=pmT1b[:, 0:128],
                        in_=bufrow_bf[:, st * D + m * 128:
                                      st * D + (m + 1) * 128],
                        identity=identb_sb[:]),
                        after=[t6, fr("pmT1")])
                    t_bufT = O("vector", lambda e, st=st, m=m:
                               e.tensor_copy(
                                   out=bufT_bf[:, m * GSL + st * 128:
                                               m * GSL + (st + 1) * 128],
                                   in_=pmT1b[:, 0:128]), after=[ttr])
                    FREE["pmT1"] = t_bufT
                FREE[("brow", st)] = P.tag("tensor")
            ST["cur"] = "moe1"
            for m in range(32):
                cq = m % 2
                tcf = GDMA(lambda e, m=m, cq=cq: e.dma_start(
                    out=cfc1_bf[cq][:],
                    in_=CFC[:, m * 128:(m + 1) * 128]
                    .rearrange("(k p) c -> p k c", p=128)),
                    after=[fr(("cfc1", cq)), fr(("cfc2", 0))])
                for nsb in range(2):
                    pmk = "pmA" if nsb == 0 else "pmB"
                    pm = pmA if nsb == 0 else pmB
                    tmm = None
                    for kt in range(8):
                        tmm = O("tensor",
                                lambda e, kt=kt, nsb=nsb, pm=pm, cq=cq:
                                e.matmul(
                                    out=pm[:],
                                    lhsT=cfc1_bf[cq][
                                        :, kt * 128:(kt + 1) * 128],
                                    rhs=bufT_bf[:, kt * GSL + nsb * 512:
                                                kt * GSL +
                                                (nsb + 1) * 512],
                                    start=(kt == 0), stop=(kt == 7)),
                                after=([tcf, t_bufT, fr(pmk)]
                                       if kt == 0 else ()))
                    FREE[("cfc1", cq)] = tmm
                    trl = O("vector", lambda e, pm=pm: e.tensor_scalar(
                        out=ev_sb[:], in0=pm[:], scalar1=0.0,
                        scalar2=None, op0=ALU.max),
                        after=[tmm, fr("evh")])
                    FREE[pmk] = trl
                    t_mm1_done = O("scalar",
                                   lambda e, m=m, nsb=nsb: e.activation(
                                       out=hmoe_bf[
                                           :, m * GSL + nsb * 512:
                                           m * GSL + (nsb + 1) * 512],
                                       in_=ev_sb[:], func=AF.Square),
                                   after=[trl, t_mm2_done])
                    FREE["evh"] = t_mm1_done
            ST["cur"] = "moe2"
            for nsb in range(2):
                for m in range(8):
                    cq = m % 2
                    tcp = GDMA(lambda e, m=m, cq=cq: e.dma_start(
                        out=cfc2_bf[cq][:],
                        in_=CPR[:, m * 128:(m + 1) * 128]
                        .rearrange("(k p) c -> p k c", p=128)),
                        after=[fr(("cfc2", cq))] +
                              ([fr(("cfc1", 0)), fr(("cfc1", 1))]
                               if cq == 0 else []))
                    tmm = None
                    for kt in range(32):
                        tmm = O("tensor",
                                lambda e, kt=kt, nsb=nsb, cq=cq: e.matmul(
                                    out=pmB[:],
                                    lhsT=cfc2_bf[cq][
                                        :, kt * 128:(kt + 1) * 128],
                                    rhs=hmoe_bf[:, kt * GSL + nsb * 512:
                                                kt * GSL +
                                                (nsb + 1) * 512],
                                    start=(kt == 0), stop=(kt == 31)),
                                after=([tcp, t_mm1_done, fr("pmB")]
                                       if kt == 0 else ()))
                    FREE[("cfc2", cq)] = tmm
                    t_mm2_done = tmm
                    tev = O("vector", lambda e: e.tensor_copy(
                        out=ev_sb[:], in_=pmB[:]),
                        after=[tmm, fr("evh")])
                    FREE["pmB"] = tev
                    for st in range(4):
                        ttr = O("tensor", lambda e, st=st: e.transpose(
                            out=pmT1[:],
                            in_=ev_sb[:, st * 128:(st + 1) * 128],
                            identity=ident_sb[:]),
                            after=[tev, fr("pmT1")])
                        tbw = O("vector", lambda e, st=st, m=m:
                                e.tensor_copy(
                                    out=big3[:, st * 1024 + m * 128:
                                             st * 1024 + (m + 1) * 128],
                                    in_=pmT1[:]),
                                after=[ttr, t_eo_store])
                        FREE["pmT1"] = tbw
                    FREE["evh"] = P.tag("tensor")
                tvl = P.tag("vector")
                for st in range(4):
                    srow = g * GSL + nsb * 512 + st * 128
                    t_eo_store = DMA(
                        lambda e, srow=srow, st=st: e.dma_start(
                            out=eout_in[srow:srow + 128, :],
                            in_=big3[:, st * 1024:(st + 1) * 1024]),
                        after=[tvl])
        t_eoag = CC(lambda e: e.collective_compute(
            "AllGather", ALU.bypass, replica_groups=rg,
            ins=[eout_in.opt()], outs=[eout_ag.opt()]),
            after=[t_eo_store])

        ST["cur"] = "comb"
        serial["last"] = t_eoag
        for b in range(NB):
            for k in range(2):
                col = k * NB + b
                SER("vector", lambda e, col=col: e.tensor_scalar(
                    out=idx_f[0][:], in0=rank_sb[:, col:col + 1],
                    scalar1=-1.0, scalar2=0.0, op0=ALU.add, op1=ALU.max))
                SER("vector", lambda e: e.tensor_scalar(
                    out=idx_f[0][:], in0=idx_f[0][:],
                    scalar1=float(CAP - 1), scalar2=None, op0=ALU.min))
                SER("vector", lambda e, col=col: e.scalar_tensor_tensor(
                    out=idx_f[0][:], in0=esel_sb[:, col:col + 1],
                    scalar=float(CAP), in1=idx_f[0][:],
                    op0=ALU.mult, op1=ALU.add))
                SER("vector", lambda e: e.tensor_copy(out=idx_u[0][:],
                                                      in_=idx_f[0][:]))
                dstt = sq_blk if k == 0 else zt_blk
                SER("gpsimd", lambda e, dstt=dstt: e.indirect_dma_start(
                    out=dstt[:], out_offset=None, in_=eout_ag,
                    in_offset=IndirectOffsetOnAxis(idx_u[0][:, 0:1],
                                                   axis=0),
                    bounds_check=RB['tok'], oob_is_err=False), 16)
            SER("vector", lambda e, b=b: e.tensor_scalar(
                out=xc_b[:, 0:D], in0=sq_blk[:],
                scalar1=probs_sb[:, b:b + 1], scalar2=None, op0=ALU.mult))
            SER("vector", lambda e, b=b: e.scalar_tensor_tensor(
                out=xc_b[:, 0:D], in0=zt_blk[:],
                scalar=probs_sb[:, NB + b:NB + b + 1], in1=xc_b[:, 0:D],
                op0=ALU.mult, op1=ALU.add))
            SER("sync", lambda e, b=b: e.dma_start(
                out=xblk[:], in_=x2_dram[b * 128:(b + 1) * 128, :]), 16)
            SER("vector", lambda e: e.tensor_add(xc_b[:, 0:D],
                                                 xc_b[:, 0:D], xblk[:]))
            SER("sync", lambda e, b=b: e.dma_start(
                out=OUT[b * 128:(b + 1) * 128, :], in_=xc_b[:, 0:D]), 16)

        SEGV = 30000
        segs = {}
        for e2 in ENGS:
            base = [0]
            cum = 0
            bases = []
            for (fn, inc, waits) in P.ops[e2]:
                if cum - base[-1] >= SEGV:
                    base.append(cum)
                bases.append((len(base) - 1, base[-1]))
                cum += inc
            totals = []
            for i2 in range(len(base)):
                end = base[i2 + 1] if i2 + 1 < len(base) else cum
                totals.append(end - base[i2])
            assert len(base) <= NSEG, (e2, len(base))
            segs[e2] = (base, bases, totals, cum)

        def wait_on(engine, te, tv):
            base = segs[te][0]
            si = 0
            for i2 in range(len(base)):
                if tv > base[i2]:
                    si = i2
            engine.wait_ge(sems[te][si], tv - base[si])

        def run_engine(engine, name):
            bases = segs[name][1]
            for i2, (fn, inc, waits) in enumerate(P.ops[name]):
                for te, tv in waits:
                    wait_on(engine, te, tv)
                si, b0 = bases[i2]
                fn(engine).then_inc(sems[name][si], inc)
            for te in ENGS:
                b2, _, t2, c2 = segs[te]
                for i3 in range(len(b2)):
                    engine.wait_ge(sems[te][i3], t2[i3])

        @block.sync
        def _(sync):
            run_engine(sync, "sync")

        @block.vector
        def _(vector):
            run_engine(vector, "vector")

        @block.scalar
        def _(scalar):
            run_engine(scalar, "scalar")

        @block.gpsimd
        def _(gpsimd):
            RB['y'] = gpsimd.alloc_register("rb_y")
            RB['tok'] = gpsimd.alloc_register("rb_tok")
            RB['ln'] = gpsimd.alloc_register("rb_ln")
            gpsimd.reg_mov(RB['y'], DIN * 32 - 1)
            gpsimd.reg_mov(RB['tok'], E * CAP - 1)
            gpsimd.reg_mov(RB['ln'], T - 1)
            run_engine(gpsimd, "gpsimd")

        @block.tensor
        def _(tensor):
            run_engine(tensor, "tensor")

    return nc


_cache = {}


def get_program(mode="full", strict=(), niter=NITER):
    key = (mode, tuple(sorted(strict)), niter)
    if key not in _cache:
        _cache[key] = build_program(mode, strict, niter)
    return _cache[key]


def make_in_maps(inputs):
    x = np.ascontiguousarray(
        np.asarray(inputs["L0_attn_x"], dtype=np.float32).reshape(T, D))
    w_in = np.asarray(inputs["L0_attn_W_in"], dtype=np.float32)
    conv_w = np.asarray(inputs["L0_attn_conv_w"], dtype=np.float32)
    conv_b = np.asarray(inputs["L0_attn_conv_b"], dtype=np.float32)
    w_x = np.asarray(inputs["L0_attn_W_x"], dtype=np.float32)
    w_dt = np.asarray(inputs["L0_attn_W_dt"], dtype=np.float32)
    b_dt = np.asarray(inputs["L0_attn_b_dt"], dtype=np.float32)
    a_log = np.asarray(inputs["L0_attn_A_log"], dtype=np.float32)
    dp = np.asarray(inputs["L0_attn_Dp"], dtype=np.float32)
    w_out = np.ascontiguousarray(
        np.asarray(inputs["L0_attn_W_out"], dtype=np.float32))
    w_g = np.asarray(inputs["L0_ffn_w_g"], dtype=np.float32)
    c_fc = np.asarray(inputs["L0_ffn_c_fc"], dtype=np.float32)
    c_proj = np.asarray(inputs["L0_ffn_c_proj"], dtype=np.float32)

    in_maps = []
    for c in range(NC):
        dsl = slice(c * DL, (c + 1) * DL)
        m = {
            "x_shard": np.ascontiguousarray(x[c * TL:(c + 1) * TL, :]),
            "w_in_loc": np.ascontiguousarray(
                np.concatenate([w_in[:, dsl], w_in[:, DIN + c * DL:
                                                  DIN + (c + 1) * DL]],
                               axis=1)),
            "conv_w_loc": np.ascontiguousarray(conv_w[dsl, 0, :]),
            "conv_b_loc": np.ascontiguousarray(conv_b[dsl].reshape(DL, 1)),
            "w_x_loc": np.ascontiguousarray(w_x[dsl, :]),
            "w_dt_loc": np.ascontiguousarray(np.concatenate(
                [w_dt[:, dsl], b_dt[dsl].reshape(1, DL)], axis=0)),
            "b_dt_loc": np.ascontiguousarray(b_dt[dsl].reshape(DL, 1)),
            "a_log_loc": np.ascontiguousarray(a_log[dsl, :]),
            "dp_loc": np.ascontiguousarray(dp[dsl].reshape(DL, 1)),
            "w_out": w_out,
            "w_gT": np.ascontiguousarray(w_g.T),
            "c_fc_loc": np.ascontiguousarray(c_fc[c]),
            "c_proj_loc": np.ascontiguousarray(c_proj[c]),
            "tokid": np.ascontiguousarray(
                np.arange(c * TL, (c + 1) * TL, dtype=np.float32)
                .reshape(TL, 1)),
            "prevmask": np.ascontiguousarray(
                (np.arange(NC) < c).astype(np.float32).reshape(NC, 1)),
            "rankoff": np.array([[float(c)]], dtype=np.float32),
        }
        in_maps.append(m)
    return in_maps


def kernel(**inputs):
    nc = get_program("full", ("a3g", "a3m", "a3s"))
    in_maps = make_in_maps(inputs)
    res = run_bass_kernel_spmd(nc, in_maps, core_ids=list(range(NC)))
    out = np.concatenate(
        [np.asarray(res.results[c]["out"]) for c in range(NC)], axis=0)
    return out.reshape(1, T, D).astype(np.float32)


# version 13 (final)
# speedup vs baseline: 3.9705x; 1.1208x over previous
import numpy as np
from contextlib import ExitStack
import concourse.bass as bass
import concourse.mybir as mybir
from concourse.bass import IndirectOffsetOnAxis
from concourse.bass_utils import run_bass_kernel_spmd

F32 = mybir.dt.float32
BF16 = mybir.dt.bfloat16
U32 = mybir.dt.uint32
AF = mybir.ActivationFunctionType
ALU = mybir.AluOpType
AX = mybir.AxisListType

NC = 8
T = 16384
D = 1024
DIN = 2048
S = 16
DTR = 64
E = 8
CAP = 5120
TL = T // NC
DL = DIN // NC
NITER = 4
TT = 1024
NJ = T // TT
NB = TL // 128
GSL = 1024
NG = CAP // GSL
SQD = float(np.sqrt(1024.0))
OOB = 1.0e9
BIG = 2097152.0

ENGS = ("sync", "vector", "scalar", "gpsimd", "tensor")
KEEP_SAME = [False]
KEEP_ENGS = [set()]


class Sch:
    def __init__(self):
        self.ops = {e: [] for e in ENGS}
        self.cum = {e: 0 for e in ENGS}

    def op(self, eng, fn, inc=1, after=()):
        wd = {}
        if eng == "vector" and self.cum["vector"] > 0:
            wd["vector"] = self.cum["vector"]
        for t in after:
            if t is None:
                continue
            te, tv = t
            if tv <= 0:
                continue
            if te == eng and not (KEEP_SAME[0] or te in KEEP_ENGS[0]):
                continue
            if te == eng and tv >= self.cum[eng] + 1:
                continue
            wd[te] = max(wd.get(te, 0), tv)
        self.ops[eng].append((fn, inc, sorted(wd.items())))
        self.cum[eng] += inc
        return (eng, self.cum[eng])

    def tag(self, eng):
        return (eng, self.cum[eng])


def build_program(mode="full", strict=(), niter=NITER):
    nc = bass.Bass("TRN2", target_bir_lowering=False, debug=False)
    ST = {"cur": "p0", "last": None, "set": set(strict)}

    def ein(name, shape, dtype=F32):
        return nc.dram_tensor(name, list(shape), dtype,
                              kind="ExternalInput").ap()

    X = ein("x_shard", (TL, D))
    WIN = ein("w_in_loc", (D, 2 * DL))
    CW = ein("conv_w_loc", (DL, 4))
    CB = ein("conv_b_loc", (DL, 1))
    WX = ein("w_x_loc", (DL, 96))
    WDT = ein("w_dt_loc", (DTR + 1, DL))
    BDT = ein("b_dt_loc", (DL, 1))
    ALOG = ein("a_log_loc", (DL, S))
    DP = ein("dp_loc", (DL, 1))
    WOUT = ein("w_out", (DIN, D))
    WGT = ein("w_gT", (D, E))
    CFC = ein("c_fc_loc", (D, 4 * D))
    CPR = ein("c_proj_loc", (4 * D, D))
    TOKID = ein("tokid", (TL, 1))
    PREV = ein("prevmask", (NC, 1))
    RKOF = ein("rankoff", (1, 1))

    OUT = nc.dram_tensor("out", [TL, D], F32, kind="ExternalOutput").ap()

    def idr(name, shape, dtype=F32):
        return nc.dram_tensor(name, list(shape), dtype, kind="Internal").ap()

    cur_dram = idr("cur_dram", (TL, D))
    x2_dram = idr("x2_dram", (TL, D))
    z_in = idr("z_in", (D, TL), BF16)
    z_ag = idr("z_ag", (NC * D, TL), BF16)
    dbl_in = idr("dbl_in", (96 * NJ, TT))
    dbl_out = idr("dbl_out", (96 * NJ, TT))
    y_in = idr("y_in", (DL, T), BF16)
    y_ag = idr("y_ag", (DIN, T), BF16)
    ln2_in = idr("ln2_in", (TL, D), BF16)
    ln2_ag = idr("ln2_ag", (T, D), BF16)
    cnt0_in = idr("cnt0_in", (1, E))
    cnt0_ag = idr("cnt0_ag", (NC, E))
    cnt1_in = idr("cnt1_in", (1, E))
    cnt1_ag = idr("cnt1_ag", (NC, E))
    tokmap_in = idr("tokmap_in", (E * CAP, 1))
    tokmap_ar = idr("tokmap_ar", (E * CAP, 1))
    eout_in = idr("eout_in", (CAP, D))
    eout_ag = idr("eout_ag", (E * CAP, D))

    rg = [list(range(NC))]
    P = Sch()
    RB = {}

    def O(eng, fn, inc=1, after=()):
        if "all" in ST["set"] or ST["cur"] in ST["set"]:
            after = list(after) + [ST["last"]]
        t = P.op(eng, fn, inc, after)
        ST["last"] = t
        return t

    def DMA(fn, after=()):
        return O("sync", fn, 16, after)

    def GDMA(fn, after=()):
        return O("gpsimd", fn, 16, after)

    def CC(fn, after=()):
        return O("gpsimd", fn, 1, after)

    with ExitStack() as ctx:
        def sb(name, shape, dtype=F32):
            return ctx.enter_context(nc.sbuf_tensor(name, shape, dtype))

        def psum(name, shape, dtype=F32):
            return ctx.enter_context(nc.psum_tensor(name, shape, dtype))

        big1 = sb("big1", [128, S * TT])
        big1b = big1[:].bitcast(BF16)
        big2 = sb("big2", [128, 8 * TT])
        big2b = big2[:].bitcast(BF16)
        big3 = sb("big3", [128, 4352])
        sc1 = sb("sc1", [128, 4096])
        sc1b = sc1[:].bitcast(BF16)

        w_in_sb = sb("w_in_sb", [128, 8 * 512], BF16)
        w_x_sb = sb("w_x_sb", [128, 2 * 96], BF16)
        w_dt_sb = sb("w_dt_sb", [DTR + 1, DL], BF16)
        wgt_sb = sb("wgt_sb", [128, 8 * E], BF16)
        conv_sb = sb("conv_sb", [128, 8])
        convb_sb = sb("convb_sb", [128, 2])
        bdt_sb = sb("bdt_sb", [128, 2])
        acol_sb = sb("acol_sb", [128, 2 * S])
        dp_sb = sb("dp_sb", [128, 2])
        ones_sb = sb("ones_sb", [1, 128])
        ident_sb = sb("ident_sb", [128, 128])
        identb_sb = sb("identb_sb", [128, 128], BF16)
        iota8_sb = sb("iota8_sb", [128, 8])
        tokid_sb = sb("tokid_sb", [128, NB])
        prev_sb = sb("prev_sb", [NC, 1])
        ones8_sb = sb("ones8_sb", [NC, 1])
        rkof_sb = sb("rkof_sb", [1, 1])
        rank_col = sb("rank_col", [128, 1])
        ebase_col = sb("ebase_col", [128, 1])
        idx2_f = sb("idx2_f", [128, 1])
        idx2_u = sb("idx2_u", [128, 1], U32)
        idx_f = [sb(f"idx_f{q}", [128, 1]) for q in range(2)]
        idx_u = [sb(f"idx_u{q}", [128, 1], U32) for q in range(2)]
        cst_sb = sb("cst_sb", [128, 8])

        xc_b = sb("xc_b", [128, TT])
        a_buf = xc_b
        del_b = [sb(f"del_{d}", [128, TT]) for d in range(2)]
        du_b = [sb(f"du_{d}", [128, TT]) for d in range(2)]
        h_b = sb("h_b", [128, TT])
        ydt_b = [sb(f"ydt_{d}", [128, TT]) for d in range(2)]
        tmp_sb = sb("tmp_sb", [128, TT])
        zdt_sb = sb("zdt_sb", [128, TT])
        dtv_sb = xc_b[0:DTR, :]
        dtb_sb = big1b[0:DTR + 1, 30720:31744]
        ubf_sb = zdt_sb[:].bitcast(BF16)[:, 0:TT]
        carry = sb("carry", [128, 2 * S])
        a_dt = ydt_b
        bcBC = [big1[:, 8192:10240], big1[:, 10240:12288]]
        h2 = [h_b[:], big1[:, 12288:13312]]
        pbf = [big1b[:, 26624:27648], big1b[:, 27648:28672]]
        ybf2 = [big1b[:, 28672:29696], big1b[:, 29696:30720]]
        dpdiag = [big1b[:, 31744:31872], big1b[:, 31872:32000]]
        ubf2 = [sc1b[:, 0:1024], sc1b[:, 1024:2048]]
        gsil = sc1[:, 1024:2048]

        xblk = sb("xblk", [128, D])
        sq_blk = sb("sq_blk", [128, D])
        zt_blk = sb("zt_blk", [128, D])
        ztb_blk = sb("ztb_blk", [128, D], BF16)
        stat = sb("stat", [128, 8])
        ztr_ev = sb("ztr_ev", [128, 128])
        ev_sb = sb("ev_sb", [128, 512])

        lg_sb = sb("lg_sb", [128, E])
        mx_sb = sb("mx_sb", [128, 8])
        ix_sb = sb("ix_sb", [128, 8], U32)
        ixf_sb = sb("ixf_sb", [128, 8])
        ind_t = sb("ind_t", [128, E])
        ind_em = [big1[0:E, k * TL:(k + 1) * TL] for k in range(2)]
        cum_em = [big1[0:E, 2 * TL + k * TL:2 * TL + (k + 1) * TL]
                  for k in range(2)]
        zero_em = big1[0:E, 4 * TL:5 * TL]
        pk_em = big1[0:E, 5 * TL:5 * TL + 128]
        cnts_sb = sb("cnts_sb", [NC, E])
        base_sb = sb("base_sb", [E, 2])
        rank_sb = sb("rank_sb", [128, 2 * NB])
        valid_sb = sb("valid_sb", [128, 2 * NB])
        probs_sb = sb("probs_sb", [128, 2 * NB])
        esel_sb = sb("esel_sb", [128, 2 * NB])
        slot_sb = sb("slot_sb", [128, 2 * NB])
        fill_sb = sb("fill_sb", [128, 320])

        wout_bf = big1b[:, 0:16384]
        hmoe_bf = big1b[:, 0:32768]
        zsb_bf = [big2b[:, q * 8192:(q + 1) * 8192] for q in range(2)]
        ykq_bf = [big2b[:, q * 8192:(q + 1) * 8192] for q in range(2)]
        bufT_bf = big2b[:, 0:8192]
        bufrow_bf = big2b[:, 8192:16384]
        cfc1_bf = [sc1b[:, q * 1024:(q + 1) * 1024] for q in range(2)]
        cfc2_bf = [sc1b[:, q * 4096:(q + 1) * 4096] for q in range(2)]
        ztr8_bf = sc1b[:, 0:1024]

        def x_in_t(dt):
            return big3[:, dt * 1088: dt * 1088 + TT + 32]

        def u_t(dt):
            return x_in_t(dt)[:, 4:4 + TT]

        def res_t(dt):
            return big3[:, 2176 + dt * 1088: 2176 + dt * 1088 + TT]

        PA = psum("pmArena", [128, 4096])
        pmA = PA[:, 0:512]
        pmB = PA[:, 512:1024]
        b_ps = PA[:, 1024:2048]
        py = [PA[:, 2048:3072], PA[:, 3072:4096]]
        pmT1 = PA[:, 2048:2176]
        pmT2 = PA[:, 3072:3200]
        pmT1b = PA[:].bitcast(BF16)[:, 4096:4352]

        NSEG = 4
        sems = {e: [ctx.enter_context(nc.semaphore(f"sm_{e}{i}"))
                    for i in range(NSEG)] for e in ENGS}
        block = ctx.enter_context(nc.Block())

        FREE = {}

        def fr(key):
            return FREE.get(key)

        CONSTS = [1e-5, 1e-30, 0.5, -0.5, -1.0, 1.0, 0.0]
        for ci, cv in enumerate(CONSTS):
            nc.const_aps.register(F32, cv, cst_sb[:, ci:ci + 1])
            O("vector", lambda e, ci=ci, cv=cv: e.memset(
                cst_sb[:, ci:ci + 1], cv))
        t_win = GDMA(lambda e: e.dma_start(
            out=w_in_sb[:], in_=WIN.rearrange("(k p) c -> p k c", p=128)))
        t_wx = GDMA(lambda e: e.dma_start(
            out=w_x_sb[:], in_=WX.rearrange("(d p) c -> p d c", p=128)))
        t_wdt = GDMA(lambda e: e.dma_start(out=w_dt_sb[:], in_=WDT))
        t_wgt = GDMA(lambda e: e.dma_start(
            out=wgt_sb[:], in_=WGT.rearrange("(k p) c -> p k c", p=128)))
        t_wout = GDMA(lambda e: e.dma_start(
            out=wout_bf, in_=WOUT.rearrange("(k p) c -> p k c", p=128)))
        DMA(lambda e: e.dma_start(
            out=conv_sb[:], in_=CW.rearrange("(d p) c -> p d c", p=128)))
        DMA(lambda e: e.dma_start(
            out=convb_sb[:], allow_slow_non_contiguous=True,
            in_=CB.rearrange("(d p) c -> p d c", p=128)[:, :, 0]))
        DMA(lambda e: e.dma_start(
            out=bdt_sb[:], allow_slow_non_contiguous=True,
            in_=BDT.rearrange("(d p) c -> p d c", p=128)[:, :, 0]))
        t_alog = DMA(lambda e: e.dma_start(
            out=acol_sb[:], in_=ALOG.rearrange("(d p) s -> p d s", p=128)))
        t_dp = DMA(lambda e: e.dma_start(
            out=dp_sb[:], allow_slow_non_contiguous=True,
            in_=DP.rearrange("(d p) c -> p d c", p=128)[:, :, 0]))
        DMA(lambda e: e.dma_start(
            out=tokid_sb[:], allow_slow_non_contiguous=True,
            in_=TOKID.rearrange("(b p) c -> p b c", p=128)[:, :, 0]))
        DMA(lambda e: e.dma_start(out=prev_sb[:], in_=PREV))
        t_rk = DMA(lambda e: e.dma_start(out=rkof_sb[:], in_=RKOF))
        tse = O("scalar", lambda e: e.activation(
            out=acol_sb[:], in_=acol_sb[:], func=AF.Exp), after=[t_alog])
        t_acol = O("vector", lambda e: e.tensor_scalar(
            out=acol_sb[:], in0=acol_sb[:], scalar1=-1.0, scalar2=None,
            op0=ALU.mult), after=[tse])
        O("vector", lambda e: e.memset(ones_sb[:], 1.0))
        O("vector", lambda e: e.memset(ones8_sb[:], 1.0))
        t_id0 = O("gpsimd", lambda e: e.iota(
            out=ident_sb[:], pattern=[[1, 128]], base=0,
            channel_multiplier=-1, allow_small_or_imprecise_dtypes=True))
        t_id1 = O("vector", lambda e: e.tensor_scalar(
            out=ident_sb[:], in0=ident_sb[:], scalar1=0.0, scalar2=None,
            op0=ALU.is_equal), after=[t_id0])
        t_idb = O("vector", lambda e: e.tensor_copy(out=identb_sb[:],
                                                    in_=ident_sb[:]))
        O("vector", lambda e: e.memset(
            dtb_sb[DTR:DTR + 1, :], 1.0))
        t_dpd = None
        for dt in range(2):
            O("vector", lambda e, dt=dt: e.tensor_scalar(
                out=ztr_ev[:], in0=ident_sb[:],
                scalar1=dp_sb[:, dt:dt + 1], scalar2=None, op0=ALU.mult),
              after=[t_id1, t_dp])
            t_dpd = O("vector", lambda e, dt=dt: e.tensor_copy(
                out=dpdiag[dt], in_=ztr_ev[:]))
        O("gpsimd", lambda e: e.iota(out=iota8_sb[:], pattern=[[1, 8]],
                                     base=0, channel_multiplier=0,
                                     allow_small_or_imprecise_dtypes=True))
        O("vector", lambda e: e.memset(fill_sb[:], OOB))
        t_rmm = O("tensor", lambda e: e.matmul(
            out=pmT2[:, 0:1], lhsT=ones_sb[:], rhs=rkof_sb[:],
            start=True, stop=True), after=[t_rk, P.tag("vector")])
        tg0 = O("gpsimd", lambda e: e.iota(
            out=rank_col[:], pattern=[[1, 1]], base=0,
            channel_multiplier=32, allow_small_or_imprecise_dtypes=True))
        O("vector", lambda e: e.scalar_tensor_tensor(
            out=rank_col[:], in0=pmT2[:, 0:1], scalar=4.0, in1=rank_col[:],
            op0=ALU.mult, op1=ALU.add), after=[t_rmm, tg0])
        tg1 = O("gpsimd", lambda e: e.iota(
            out=ebase_col[:], pattern=[[1, 1]], base=0,
            channel_multiplier=1, allow_small_or_imprecise_dtypes=True))
        t_ebase = O("vector", lambda e: e.scalar_tensor_tensor(
            out=ebase_col[:], in0=pmT2[:, 0:1], scalar=float(CAP),
            in1=ebase_col[:], op0=ALU.mult, op1=ALU.add),
            after=[t_rmm, tg1])
        FREE["pmT2"] = t_ebase

        serial = {"last": None}

        def SER(eng, fn, inc=1, extra=()):
            t = O(eng, fn, inc,
                  ([serial["last"]] if serial["last"] else []) +
                  list(extra))
            serial["last"] = t
            return t

        def rms_of(src_tile, dst_tile):
            SER("vector", lambda e: e.tensor_mul(sq_blk[:], src_tile[:],
                                                 src_tile[:]))
            SER("vector", lambda e: e.tensor_reduce(
                out=stat[:, 0:1], in_=sq_blk[:], axis=AX.X, op=ALU.add))
            SER("scalar", lambda e: e.activation(
                out=stat[:, 3:4], in_=stat[:, 0:1], func=AF.Ln, bias=1e-30))
            SER("scalar", lambda e: e.activation(
                out=stat[:, 3:4], in_=stat[:, 3:4], func=AF.Exp, scale=0.5))
            SER("vector", lambda e: e.tensor_scalar(
                out=stat[:, 3:4], in0=stat[:, 3:4], scalar1=1e-12,
                scalar2=None, op0=ALU.max))
            SER("vector", lambda e: e.reciprocal(out=stat[:, 4:5],
                                                 in_=stat[:, 3:4]))
            SER("vector", lambda e: e.tensor_scalar(
                out=stat[:, 4:5], in0=stat[:, 4:5], scalar1=SQD,
                scalar2=None, op0=ALU.mult))
            return SER("vector", lambda e: e.tensor_scalar(
                out=dst_tile[:], in0=src_tile[:], scalar1=stat[:, 4:5],
                scalar2=None, op0=ALU.mult))

        def ln_of(src_tile, dst_tile):
            SER("vector", lambda e: e.tensor_mul(sq_blk[:], src_tile[:],
                                                 src_tile[:]))
            SER("vector", lambda e: e.tensor_reduce(
                out=stat[:, 0:1], in_=sq_blk[:], axis=AX.X, op=ALU.add))
            SER("vector", lambda e: e.tensor_reduce(
                out=stat[:, 1:2], in_=src_tile[:], axis=AX.X, op=ALU.add))
            SER("vector", lambda e: e.tensor_scalar(
                out=stat[:, 1:2], in0=stat[:, 1:2], scalar1=1.0 / D,
                scalar2=None, op0=ALU.mult))
            SER("vector", lambda e: e.tensor_scalar(
                out=stat[:, 0:1], in0=stat[:, 0:1], scalar1=1.0 / D,
                scalar2=None, op0=ALU.mult))
            SER("vector", lambda e: e.tensor_mul(stat[:, 2:3], stat[:, 1:2],
                                                 stat[:, 1:2]))
            SER("vector", lambda e: e.tensor_sub(stat[:, 0:1], stat[:, 0:1],
                                                 stat[:, 2:3]))
            SER("scalar", lambda e: e.activation(
                out=stat[:, 3:4], in_=stat[:, 0:1], func=AF.Ln, bias=1e-5))
            SER("scalar", lambda e: e.activation(
                out=stat[:, 3:4], in_=stat[:, 3:4], func=AF.Exp, scale=-0.5))
            SER("vector", lambda e: e.tensor_mul(stat[:, 4:5], stat[:, 1:2],
                                                 stat[:, 3:4]))
            SER("vector", lambda e: e.tensor_scalar(
                out=stat[:, 4:5], in0=stat[:, 4:5], scalar1=-1.0,
                scalar2=None, op0=ALU.mult))
            return SER("vector", lambda e: e.tensor_scalar(
                out=dst_tile[:], in0=src_tile[:], scalar1=stat[:, 3:4],
                scalar2=stat[:, 4:5], op0=ALU.mult, op1=ALU.add))

        ph0 = [P.tag(e) for e in ENGS]
        t_a3_store = None
        for it in range(niter):
            ST["cur"] = "a1"
            serial["last"] = None
            first_extra = ph0 if it == 0 else [t_a3_store]
            for b in range(NB):
                src = X if it == 0 else cur_dram
                SER("sync", lambda e, src=src, b=b: e.dma_start(
                    out=xblk[:], in_=src[b * 128:(b + 1) * 128, :]), 16,
                    extra=(first_extra if b == 0 else ()))
                if it == 0:
                    ln_of(xblk, zt_blk)
                    SER("vector", lambda e: e.tensor_copy(out=xblk[:],
                                                          in_=zt_blk[:]))
                rms_of(xblk, ztb_blk)
                for m in range(8):
                    SER("tensor", lambda e, m=m: e.transpose(
                        out=pmT1b[:, 0:128],
                        in_=ztb_blk[:, m * 128:(m + 1) * 128],
                        identity=identb_sb[:]),
                        extra=([t_idb] if (it == 0 and b == 0 and m == 0)
                               else ()))
                    SER("vector", lambda e, m=m: e.tensor_copy(
                        out=ztr8_bf[:, m * 128:(m + 1) * 128],
                        in_=pmT1b[:, 0:128]))
                SER("sync", lambda e, b=b: e.dma_start(
                    out=z_in[:, b * 128:(b + 1) * 128]
                    .rearrange("(m p) t -> p m t", p=128),
                    in_=ztr8_bf.rearrange("p (m t) -> p m t", t=128)), 16)
            t_zag = CC(lambda e: e.collective_compute(
                "AllGather", ALU.bypass, replica_groups=rg,
                ins=[z_in.opt()], outs=[z_ag.opt()]),
                after=[serial["last"]])
            FREE["pmT1"] = serial["last"]

            ST["cur"] = "a2"
            t_carry = O("vector", lambda e: e.memset(carry[:], 0.0),
                        after=[serial["last"], fr("carrylast")])
            CT = [None] * (2 * S)
            t_scan_dt = [None, None]
            t_exp_dt = [None, None]
            zload = [None, None]
            t_ar = [None] * NJ
            t_dbl = [None] * NJ
            t_del = [[None, None] for _ in range(NJ)]
            t_du = [[None, None] for _ in range(NJ)]
            t_silu = [None, None]
            t_ubf = [None, None]
            t_wxmm = [None] * NJ
            t_ystore = None

            def emit_pA1(j):
                zq = j % 2
                t0 = j * TT
                r = t0 // TL
                lc = t0 % TL
                zload[zq] = DMA(lambda e, r=r, lc=lc, zq=zq: e.dma_start(
                    out=zsb_bf[zq],
                    in_=z_ag[r * D:(r + 1) * D, lc:lc + TT]
                    .rearrange("(k p) t -> p k t", p=128)),
                    after=[t_zag, fr(("zfree", zq)), fr(("ykfree", zq)),
                           t_win])
                for cti in range(2):
                    dt = cti
                    tev = None
                    for ns in range(2):
                        pmk = "pmA" if ns == 0 else "pmB"
                        pm = pmA if ns == 0 else pmB
                        tmm = None
                        for kt in range(8):
                            tmm = O("tensor",
                                    lambda e, cti=cti, ns=ns, kt=kt, pm=pm,
                                    zq=zq: e.matmul(
                                        out=pm[:],
                                        lhsT=w_in_sb[
                                            :, kt * 512 + cti * 128:
                                            kt * 512 + (cti + 1) * 128],
                                        rhs=zsb_bf[zq][
                                            :, kt * TT + ns * 512:
                                            kt * TT + (ns + 1) * 512],
                                        start=(kt == 0), stop=(kt == 7)),
                                    after=([zload[zq], fr(pmk)]
                                           if kt == 0 else ()))
                        FREE[("zfree", zq)] = tmm
                        tev = O("scalar",
                                lambda e, dt=dt, ns=ns, pm=pm: e.activation(
                                    out=x_in_t(dt)[:, 4 + ns * 512:
                                                   4 + (ns + 1) * 512],
                                    in_=pm[:], func=AF.Copy),
                                after=[tmm, fr(("xinfree", dt)),
                                       fr(("ufree", dt))])
                        FREE[pmk] = tev
                    FREE[("xev", dt)] = tev

            def emit_csu(j, dt):
                if True:
                    caft = [fr(("xev", dt)), fr("dtbfree")]
                    if it == 0 and j == 0 and dt == 0:
                        caft = caft + ph0
                    if j == 0:
                        O("vector", lambda e, dt=dt: e.memset(
                            x_in_t(dt)[:, 0:4], 0.0),
                          after=[t_carry] + ([fr(("xinfree", dt))]
                                             if it > 0 else []))
                    O("vector", lambda e, dt=dt: e.tensor_scalar(
                        out=xc_b[:], in0=x_in_t(dt)[:, 1:1 + TT],
                        scalar1=conv_sb[:, dt * 4: dt * 4 + 1],
                        scalar2=convb_sb[:, dt:dt + 1],
                        op0=ALU.mult, op1=ALU.add), after=caft)
                    for k in range(1, 4):
                        O("vector", lambda e, dt=dt, k=k:
                          e.scalar_tensor_tensor(
                              out=xc_b[:],
                              in0=x_in_t(dt)[:, 1 + k:1 + k + TT],
                              scalar=conv_sb[:, dt * 4 + k:
                                             dt * 4 + k + 1],
                              in1=xc_b[:], op0=ALU.mult, op1=ALU.add))
                    t_halo = O("vector", lambda e, dt=dt: e.tensor_copy(
                        out=x_in_t(dt)[:, 0:4],
                        in_=x_in_t(dt)[:, TT:TT + 4]))
                    FREE[("xinfree", dt)] = t_halo
                    t_silu[dt] = O("scalar", lambda e, dt=dt: e.activation(
                        out=u_t(dt)[:], in_=xc_b[:], func=AF.Silu),
                        after=[t_halo, fr(("ufree", dt))])
                    t_ubf[dt] = O("scalar", lambda e, dt=dt: e.activation(
                        out=ubf2[dt], in_=u_t(dt)[:], func=AF.Copy),
                        after=[fr(("ubffree", dt))])

            def emit_wx(j):
                for ns in range(2):
                    pmk = "pmA" if ns == 0 else "pmB"
                    pm = pmA if ns == 0 else pmB
                    tmm = None
                    for dt in range(2):
                        tmm = O("tensor",
                                lambda e, dt=dt, ns=ns, pm=pm: e.matmul(
                                    out=pm[0:96, :],
                                    lhsT=w_x_sb[:, dt * 96:(dt + 1) * 96],
                                    rhs=ubf2[dt][:, ns * 512:(ns + 1) * 512],
                                    start=(dt == 0), stop=(dt == 1)),
                                after=([t_ubf[dt], t_wx, fr(pmk)]
                                       if dt == 0 else [t_ubf[dt]]))
                    tev = O("vector",
                            lambda e, ns=ns, pm=pm: e.tensor_copy(
                                out=tmp_sb[0:96,
                                           ns * 512:(ns + 1) * 512],
                                in_=pm[0:96, :]),
                            after=[tmm, fr("tmpfree")])
                    FREE[pmk] = tev
                t_wxmm[j] = P.tag("tensor")
                FREE[("ubffree", 0)] = t_wxmm[j]
                FREE[("ubffree", 1)] = t_wxmm[j]
                t_dbl[j] = DMA(lambda e, j=j: e.dma_start(
                    out=dbl_in[96 * j:96 * (j + 1), :],
                    in_=tmp_sb[0:96, :]), after=[P.tag("vector")])

            def emit_ar(j):
                t_ar[j] = CC(lambda e, j=j: e.collective_compute(
                    "AllReduce", ALU.add, replica_groups=rg,
                    ins=[dbl_in[96 * j:96 * (j + 1), :].opt()],
                    outs=[dbl_out[96 * j:96 * (j + 1), :].opt()]),
                    after=[t_dbl[j]])

            def emit_bc(j, s):
                bq = s % 2
                base = dbl_out[96 * j + 64 + s:96 * j + 81 + s:16, :]
                bc_in = bass.AP(base.tensor, base.offset,
                                [[0, 128]] + [list(x) for x in base.ap])
                t = DMA(lambda e, bq=bq, bc_in=bc_in: e.dma_start(
                    out=bcBC[bq], in_=bc_in),
                    after=[t_ar[j], fr(("bcb", bq)), fr(("bcp", bq))])
                FREE[("bc", j, s)] = t

            def emit_pB(j):
                t_dtv = DMA(lambda e, j=j: e.dma_start(
                    out=dtv_sb[:], in_=dbl_out[96 * j:96 * j + DTR, :]),
                    after=[t_ar[j], t_silu[1]])
                t_dtb = O("vector", lambda e: e.tensor_copy(
                    out=dtb_sb[0:DTR, :], in_=dtv_sb[:]),
                    after=[t_dtv, fr("dtbmm")])
                FREE["dtbfree"] = t_dtb
                for dt in range(2):
                    for ns in range(2):
                        pmk = "pmA" if ns == 0 else "pmB"
                        pm = pmA if ns == 0 else pmB
                        tmm = O("tensor",
                                lambda e, dt=dt, ns=ns, pm=pm: e.matmul(
                                    out=pm[:],
                                    lhsT=w_dt_sb[:,
                                                 dt * 128:(dt + 1) * 128],
                                    rhs=dtb_sb[:, ns * 512:(ns + 1) * 512],
                                    start=True, stop=True),
                                after=[t_dtb, t_wdt, fr(pmk)])
                        FREE["dtbmm"] = tmm
                        tz = O("scalar",
                               lambda e, dt=dt, ns=ns, pm=pm: e.activation(
                                   out=zdt_sb[:, ns * 512:(ns + 1) * 512],
                                   in_=pm[:], func=AF.Copy),
                               after=[tmm, fr("zdtfree")])
                        FREE[pmk] = tz
                    t_zev = P.tag("scalar")
                    O("vector", lambda e: e.scalar_tensor_tensor(
                        out=tmp_sb[:], in0=zdt_sb[:], scalar=-1.0,
                        in1=zdt_sb[:], op0=ALU.mult, op1=ALU.max),
                      after=[t_zev, fr("tmpfree2")])
                    ts1 = O("scalar", lambda e: e.activation(
                        out=tmp_sb[:], in_=tmp_sb[:], func=AF.Exp,
                        scale=-1.0), after=[P.tag("vector")])
                    ts2 = O("scalar", lambda e: e.activation(
                        out=tmp_sb[:], in_=tmp_sb[:], func=AF.Ln, bias=1.0))
                    t_del[j][dt] = O("vector",
                                     lambda e, dt=dt:
                                     e.scalar_tensor_tensor(
                                         out=del_b[dt][:], in0=zdt_sb[:],
                                         scalar=0.0, in1=tmp_sb[:],
                                         op0=ALU.max, op1=ALU.add),
                                     after=[ts2, t_exp_dt[dt]])
                    FREE["zdtfree"] = t_del[j][dt]
                    t_du[j][dt] = O("vector", lambda e, dt=dt: e.tensor_mul(
                        du_b[dt][:], del_b[dt][:], u_t(dt)[:]))
                    FREE[("ufree", dt)] = t_du[j][dt]
                FREE["tmpfree2"] = t_del[j][1]
                FREE["tmpfree"] = t_del[j][1]
                emit_bc(j, 0)
                emit_bc(j, 1)

            def emit_sloop(j):
                for s in range(S):
                    bq = s % 2
                    t_bc = fr(("bc", j, s))
                    for dt in range(2):
                        hq = (s * 2 + dt) % 2
                        cidx = dt * S + s
                        eaft = [t_del[j][dt], t_scan_dt[dt]]
                        if it == 0 and j == 0 and s == 0 and dt == 0:
                            eaft = eaft + [t_acol] + ph0
                        t_exp = O("scalar",
                                  lambda e, dt=dt, s=s: e.activation(
                                      out=a_dt[dt][:], in_=del_b[dt][:],
                                      func=AF.Exp,
                                      scale=acol_sb[:, dt * S + s:
                                                    dt * S + s + 1]),
                                  after=eaft)
                        t_exp_dt[dt] = t_exp
                        t_b = O("vector",
                                lambda e, dt=dt, bq=bq: e.tensor_tensor(
                                    out=b_ps, in0=du_b[dt][:],
                                    in1=bcBC[bq][:, 0:TT], op=ALU.mult),
                                after=[t_bc, t_du[j][dt]])
                        t_scan = O("vector",
                                   lambda e, dt=dt, hq=hq, cidx=cidx:
                                   e.tensor_tensor_scan(
                                       out=h2[hq], data0=a_dt[dt][:],
                                       data1=b_ps,
                                       initial=carry[:, cidx:cidx + 1],
                                       op0=ALU.mult, op1=ALU.add),
                                   after=[t_exp, CT[cidx],
                                          fr(("hp", hq)), fr(("hc", hq))])
                        t_scan_dt[dt] = t_scan
                        t_p = O("gpsimd",
                                lambda e, hq=hq, bq=bq: e.tensor_tensor(
                                    out=pbf[hq], in0=h2[hq],
                                    in1=bcBC[bq][:, TT:2 * TT],
                                    op=ALU.mult),
                                after=[t_scan, fr(("pbffree", hq))])
                        FREE[("hp", hq)] = t_p
                        if dt == 1:
                            FREE[("bcb", bq)] = t_b
                            FREE[("bcp", bq)] = t_p
                        t_pe = None
                        for ns in range(2):
                            t_pe = O("tensor",
                                     lambda e, dt=dt, ns=ns, hq=hq, s=s:
                                     e.matmul(
                                         out=py[dt][:, ns * 512:
                                                     (ns + 1) * 512],
                                         lhsT=identb_sb[:],
                                         rhs=pbf[hq][:, ns * 512:
                                                     (ns + 1) * 512],
                                         start=False, stop=(s == S - 1)),
                                     after=([t_p] if ns == 0 else ()))
                        FREE[("pbffree", hq)] = t_pe
                        if s == S - 1:
                            FREE[("pystop", dt)] = t_pe
                        t_cc = O("scalar",
                                 lambda e, hq=hq, cidx=cidx: e.activation(
                                     out=carry[:, cidx:cidx + 1],
                                     in_=h2[hq][:, TT - 1:TT],
                                     func=AF.Copy),
                                 after=[t_scan])
                        CT[cidx] = t_cc
                        FREE[("hc", hq)] = t_cc
                    if s + 2 < S:
                        emit_bc(j, s + 2)
                    if j + 1 < NJ:
                        if s == 1:
                            emit_csu(j + 1, 0)
                        if s == 2:
                            emit_csu(j + 1, 1)
                        if s == 4:
                            emit_wx(j + 1)
                        if s == 7:
                            emit_ar(j + 1)

            def emit_pydp(j):
                for dt in range(2):
                    tdp = None
                    for ns in range(2):
                        tdp = O("tensor",
                                lambda e, dt=dt, ns=ns: e.matmul(
                                    out=py[dt][:, ns * 512:(ns + 1) * 512],
                                    lhsT=dpdiag[dt],
                                    rhs=ubf2[dt][:, ns * 512:(ns + 1) * 512],
                                    start=True, stop=False),
                                after=([t_ubf[dt], fr(("pyfree", dt)),
                                        t_dpd, t_idb] if ns == 0 else ()))
                    FREE[("ubffree", dt)] = tdp

            def emit_gate(j):
                nonlocal t_ystore
                t0 = j * TT
                for dt in range(2):
                    t_gs = O("scalar", lambda e, dt=dt: e.activation(
                        out=gsil, in_=res_t(dt)[:], func=AF.Silu),
                        after=[fr(("resev", dt)), fr("gsilfree")])
                    t_ybf = O("vector", lambda e, dt=dt: e.tensor_tensor(
                        out=ybf2[dt], in0=py[dt], in1=gsil, op=ALU.mult),
                        after=[t_gs, fr(("pystop", dt)),
                               fr(("ybffree", dt))])
                    FREE[("pyfree", dt)] = t_ybf
                    FREE["gsilfree"] = t_ybf
                    t_ystore = DMA(lambda e, dt=dt, t0=t0: e.dma_start(
                        out=y_in[dt * 128:(dt + 1) * 128, t0:t0 + TT],
                        in_=ybf2[dt]), after=[t_ybf])
                    FREE[("ybffree", dt)] = t_ystore

            def emit_pA2(j):
                zq = j % 2
                for cti in (2, 3):
                    dt = cti % 2
                    tev = None
                    for ns in range(2):
                        pmk = "pmA" if ns == 0 else "pmB"
                        pm = pmA if ns == 0 else pmB
                        tmm = None
                        for kt in range(8):
                            tmm = O("tensor",
                                    lambda e, cti=cti, ns=ns, kt=kt, pm=pm,
                                    zq=zq: e.matmul(
                                        out=pm[:],
                                        lhsT=w_in_sb[
                                            :, kt * 512 + cti * 128:
                                            kt * 512 + (cti + 1) * 128],
                                        rhs=zsb_bf[zq][
                                            :, kt * TT + ns * 512:
                                            kt * TT + (ns + 1) * 512],
                                        start=(kt == 0), stop=(kt == 7)),
                                    after=([zload[zq], fr(pmk)]
                                           if kt == 0 else ()))
                        FREE[("zfree", zq)] = tmm
                        tev = O("scalar",
                                lambda e, dt=dt, ns=ns, pm=pm: e.activation(
                                    out=res_t(dt)[:, ns * 512:
                                                  (ns + 1) * 512],
                                    in_=pm[:], func=AF.Copy),
                                after=[tmm])
                        FREE[pmk] = tev
                    FREE[("resev", dt)] = tev

            emit_pA1(0)
            emit_csu(0, 0)
            emit_csu(0, 1)
            emit_wx(0)
            emit_ar(0)
            emit_pB(0)
            emit_pA2(0)
            for j in range(NJ):
                if j + 1 < NJ:
                    emit_pA1(j + 1)
                emit_pydp(j)
                emit_sloop(j)
                if j + 1 < NJ:
                    emit_pB(j + 1)
                emit_gate(j)
                if j + 1 < NJ:
                    emit_pA2(j + 1)
            FREE["carrylast"] = CT[2 * S - 1]
            t_yag = CC(lambda e: e.collective_compute(
                "AllGather", ALU.bypass, replica_groups=rg,
                ins=[y_in.opt()], outs=[y_ag.opt()]), after=[t_ystore])

            ST["cur"] = "a3"
            last = (it == niter - 1)
            y_view = y_ag.rearrange("d (q t) -> (d q) t", q=32)
            t_big3 = None
            for sub in range(4):
                q = sub % 2
                t_lastgath = None
                ST["cur"] = "a3g"
                for kt in range(16):
                    cst = float(sub + kt * 4096)
                    iq = kt % 2
                    O("vector", lambda e, cst=cst, iq=iq:
                      e.tensor_scalar(
                          out=idx_f[iq][:], in0=rank_col[:],
                          scalar1=cst, scalar2=None, op0=ALU.add),
                      after=[fr(("idxg", iq))])
                    tiu = O("vector", lambda e, iq=iq: e.tensor_copy(
                        out=idx_u[iq][:], in_=idx_f[iq][:]))
                    t_lastgath = GDMA(
                        lambda e, kt=kt, q=q, iq=iq: e.indirect_dma_start(
                            out=ykq_bf[q][:, kt * 512:(kt + 1) * 512],
                            out_offset=None, in_=y_view,
                            in_offset=IndirectOffsetOnAxis(
                                idx_u[iq][:, 0:1], axis=0),
                            bounds_check=RB['y'], oob_is_err=False),
                        after=[tiu, t_yag, fr(("ykfree", q)),
                               fr(("zfree", q)), t_wout,
                               (("tensor", P.cum["tensor"])
                                if kt == 0 else None)])
                    FREE[("idxg", iq)] = t_lastgath
                ST["cur"] = "a3m"
                for m in range(8):
                    pmk = "pmA" if m % 2 == 0 else "pmB"
                    pm = pmA if m % 2 == 0 else pmB
                    tmm = None
                    for kt in range(16):
                        tmm = O("tensor",
                                lambda e, m=m, kt=kt, pm=pm, q=q: e.matmul(
                                    out=pm[:],
                                    lhsT=wout_bf[:, kt * 1024 + m * 128:
                                                 kt * 1024 +
                                                 (m + 1) * 128],
                                    rhs=ykq_bf[q][:,
                                                  kt * 512:
                                                  (kt + 1) * 512],
                                    start=(kt == 0), stop=(kt == 15)),
                                after=([t_lastgath, fr(pmk)]
                                       if kt == 0 else ()))
                    FREE[("ykfree", q)] = tmm
                    tev = O("vector", lambda e, pm=pm: e.tensor_copy(
                        out=ev_sb[:], in_=pm[:]),
                        after=[tmm, fr("evfree")])
                    FREE[pmk] = tev
                    for tb in range(4):
                        ttr = O("tensor", lambda e, tb=tb: e.transpose(
                            out=pmT1[:],
                            in_=ev_sb[:, tb * 128:(tb + 1) * 128],
                            identity=ident_sb[:]),
                            after=[tev, fr("pmT1")])
                        tbw = O("vector", lambda e, tb=tb, m=m:
                                e.tensor_copy(
                                    out=big3[:, tb * 1024 + m * 128:
                                             tb * 1024 + (m + 1) * 128],
                                    in_=pmT1[:]),
                                after=[ttr, t_big3])
                        FREE["pmT1"] = tbw
                    FREE["evfree"] = P.tag("tensor")
                ST["cur"] = "a3s"
                tv_last = P.tag("vector")
                for tb in range(4):
                    row0 = sub * 512 + tb * 128
                    if last:
                        t_xl = DMA(lambda e, row0=row0: e.dma_start(
                            out=xblk[:], in_=X[row0:row0 + 128, :]),
                            after=[fr("xblk")])
                        t_add = O("vector", lambda e, tb=tb: e.tensor_add(
                            big3[:, tb * 1024:(tb + 1) * 1024],
                            big3[:, tb * 1024:(tb + 1) * 1024], xblk[:]),
                            after=[t_xl])
                        FREE["xblk"] = t_add
                        t_a3_store = DMA(
                            lambda e, row0=row0, tb=tb: e.dma_start(
                                out=x2_dram[row0:row0 + 128, :],
                                in_=big3[:, tb * 1024:(tb + 1) * 1024]),
                            after=[t_add])
                    else:
                        t_a3_store = DMA(
                            lambda e, row0=row0, tb=tb: e.dma_start(
                                out=cur_dram[row0:row0 + 128, :],
                                in_=big3[:, tb * 1024:(tb + 1) * 1024]),
                            after=[tv_last])
                t_big3 = t_a3_store

        ST["cur"] = "route"
        serial["last"] = t_a3_store
        for b in range(NB):
            SER("sync", lambda e, b=b: e.dma_start(
                out=xblk[:], in_=x2_dram[b * 128:(b + 1) * 128, :]), 16)
            ln_of(xblk, zt_blk)
            SER("vector", lambda e: e.tensor_copy(out=ztb_blk[:],
                                                  in_=zt_blk[:]))
            SER("sync", lambda e, b=b: e.dma_start(
                out=ln2_in[b * 128:(b + 1) * 128, :], in_=ztb_blk[:]), 16)
            for m in range(8):
                SER("tensor", lambda e, m=m: e.transpose(
                    out=pmT1b[:, 0:128],
                    in_=ztb_blk[:, m * 128:(m + 1) * 128],
                    identity=identb_sb[:]))
                SER("vector", lambda e: e.tensor_copy(
                    out=ztr_ev[:].bitcast(BF16)[:, 0:128],
                    in_=pmT1b[:, 0:128]))
                SER("tensor", lambda e, m=m: e.matmul(
                    out=pmT2[:, 0:8],
                    lhsT=ztr_ev[:].bitcast(BF16)[:, 0:128],
                    rhs=wgt_sb[:, m * 8:(m + 1) * 8],
                    start=(m == 0), stop=(m == 7)),
                    extra=([t_wgt, fr("pmT2")] if (b == 0 and m == 0)
                           else ()))
            SER("vector", lambda e: e.tensor_copy(out=lg_sb[:],
                                                  in_=pmT2[:, 0:8]))
            SER("vector", lambda e: e.max(out=mx_sb[:], in_=lg_sb[:]))
            SER("vector", lambda e: e.max_index(
                out=ix_sb[:], in_max=mx_sb[:], in_values=lg_sb[:]))
            SER("vector", lambda e: e.tensor_copy(out=ixf_sb[:],
                                                  in_=ix_sb[:]))
            SER("vector", lambda e: e.tensor_sub(
                stat[:, 0:1], mx_sb[:, 1:2], mx_sb[:, 0:1]))
            SER("scalar", lambda e: e.activation(
                out=stat[:, 1:2], in_=stat[:, 0:1], func=AF.Exp))
            SER("vector", lambda e: e.tensor_scalar(
                out=stat[:, 2:3], in0=stat[:, 1:2], scalar1=1.0,
                scalar2=None, op0=ALU.add))
            SER("vector", lambda e: e.reciprocal(out=stat[:, 3:4],
                                                 in_=stat[:, 2:3]))
            SER("vector", lambda e, b=b: e.tensor_copy(
                out=probs_sb[:, b:b + 1], in_=stat[:, 3:4]))
            SER("vector", lambda e, b=b: e.tensor_mul(
                probs_sb[:, NB + b:NB + b + 1], stat[:, 1:2],
                stat[:, 3:4]))
            for k in range(2):
                SER("vector", lambda e, k=k, b=b: e.tensor_copy(
                    out=esel_sb[:, k * NB + b:k * NB + b + 1],
                    in_=ixf_sb[:, k:k + 1]))
                SER("vector", lambda e, k=k: e.tensor_scalar(
                    out=ind_t[:], in0=iota8_sb[:],
                    scalar1=ixf_sb[:, k:k + 1], scalar2=None,
                    op0=ALU.is_equal))
                SER("tensor", lambda e: e.transpose(
                    out=pmT1[0:8, :], in_=ind_t[:], identity=ident_sb[:]))
                SER("vector", lambda e, k=k, b=b: e.tensor_copy(
                    out=ind_em[k][:, b * 128:(b + 1) * 128],
                    in_=pmT1[0:8, 0:128]))
        SER("vector", lambda e: e.memset(zero_em, 0.0))
        for k in range(2):
            SER("vector", lambda e, k=k: e.tensor_tensor_scan(
                out=cum_em[k], data0=ind_em[k], data1=zero_em,
                initial=0.0, op0=ALU.add, op1=ALU.add))
        SER("sync", lambda e: e.dma_start(
            out=cnt0_in.rearrange("a e -> e a"),
            in_=cum_em[0][:, TL - 1:TL]), 16)
        SER("sync", lambda e: e.dma_start(
            out=cnt1_in.rearrange("a e -> e a"),
            in_=cum_em[1][:, TL - 1:TL]), 16)
        SER("gpsimd", lambda e: e.collective_compute(
            "AllGather", ALU.bypass, replica_groups=rg,
            ins=[cnt0_in.opt()], outs=[cnt0_ag.opt()]), 1)
        SER("gpsimd", lambda e: e.collective_compute(
            "AllGather", ALU.bypass, replica_groups=rg,
            ins=[cnt1_in.opt()], outs=[cnt1_ag.opt()]), 1)
        SER("sync", lambda e: e.dma_start(out=cnts_sb[:], in_=cnt0_ag), 16)
        SER("tensor", lambda e: e.matmul(
            out=pmT2[0:8, 0:1], lhsT=cnts_sb[:], rhs=prev_sb[:],
            start=True, stop=True))
        SER("tensor", lambda e: e.matmul(
            out=pmT2[0:8, 1:2], lhsT=cnts_sb[:], rhs=ones8_sb[:],
            start=True, stop=False))
        SER("sync", lambda e: e.dma_start(out=cnts_sb[:], in_=cnt1_ag), 16)
        SER("tensor", lambda e: e.matmul(
            out=pmT2[0:8, 1:2], lhsT=cnts_sb[:], rhs=prev_sb[:],
            start=False, stop=True))
        SER("vector", lambda e: e.tensor_copy(out=base_sb[:],
                                              in_=pmT2[0:8, 0:2]))
        for k in range(2):
            for b in range(NB):
                col = k * NB + b
                SER("vector", lambda e, k=k, b=b: e.tensor_mul(
                    pk_em, ind_em[k][:, b * 128:(b + 1) * 128],
                    cum_em[k][:, b * 128:(b + 1) * 128]))
                SER("tensor", lambda e, col=col: e.matmul(
                    out=pmT2[:, col:col + 1], lhsT=pk_em,
                    rhs=ones8_sb[:], start=True, stop=False))
                SER("tensor", lambda e, k=k, b=b, col=col: e.matmul(
                    out=pmT2[:, col:col + 1],
                    lhsT=ind_em[k][:, b * 128:(b + 1) * 128],
                    rhs=base_sb[:, k:k + 1], start=False, stop=True))
        SER("vector", lambda e: e.tensor_copy(out=rank_sb[:],
                                              in_=pmT2[:, 0:32]))
        SER("vector", lambda e: e.tensor_scalar(
            out=valid_sb[:], in0=rank_sb[:], scalar1=float(CAP) + 0.5,
            scalar2=None, op0=ALU.is_lt))
        SER("vector", lambda e: e.tensor_mul(probs_sb[:], probs_sb[:],
                                             valid_sb[:]))
        SER("vector", lambda e: e.scalar_tensor_tensor(
            out=slot_sb[:], in0=esel_sb[:], scalar=float(CAP),
            in1=rank_sb[:], op0=ALU.mult, op1=ALU.add))
        SER("vector", lambda e: e.tensor_scalar(
            out=slot_sb[:], in0=slot_sb[:], scalar1=BIG - 1.0,
            scalar2=None, op0=ALU.add))
        SER("vector", lambda e: e.scalar_tensor_tensor(
            out=slot_sb[:], in0=valid_sb[:], scalar=-BIG,
            in1=slot_sb[:], op0=ALU.mult, op1=ALU.add))
        SER("sync", lambda e: e.dma_start(
            out=tokmap_in.rearrange("(a b) c -> a (b c)", a=128),
            in_=fill_sb[:]), 16)
        for k in range(2):
            for b in range(NB):
                col = k * NB + b
                SER("vector", lambda e, col=col: e.tensor_copy(
                    out=idx_f[0][:], in_=slot_sb[:, col:col + 1]))
                SER("vector", lambda e: e.tensor_copy(out=idx_u[0][:],
                                                      in_=idx_f[0][:]))
                SER("gpsimd", lambda e, b=b: e.indirect_dma_start(
                    out=tokmap_in,
                    out_offset=IndirectOffsetOnAxis(idx_u[0][:, 0:1],
                                                    axis=0),
                    in_=tokid_sb[:, b:b + 1], in_offset=None,
                    bounds_check=RB['tok'], oob_is_err=False), 16)
        SER("gpsimd", lambda e: e.collective_compute(
            "AllReduce", ALU.min, replica_groups=rg,
            ins=[tokmap_in.opt()], outs=[tokmap_ar.opt()]), 1)
        t_ln2ag = SER("gpsimd", lambda e: e.collective_compute(
            "AllGather", ALU.bypass, replica_groups=rg,
            ins=[ln2_in.opt()], outs=[ln2_ag.opt()]), 1)

        ST["cur"] = "moe"
        t_mm1_done = None
        t_mm2_done = None
        t_eo_store = t_a3_store
        for g in range(NG):
            t_bufT = None
            ST["cur"] = "moeg"
            for st in range(8):
                srow = g * GSL + st * 128
                O("vector", lambda e, srow=srow: e.tensor_scalar(
                    out=idx2_f[:], in0=ebase_col[:], scalar1=float(srow),
                    scalar2=None, op0=ALU.add),
                  after=[fr("idx2"), t_ebase, serial["last"]])
                t2 = O("vector", lambda e: e.tensor_copy(out=idx2_u[:],
                                                         in_=idx2_f[:]))
                t3 = GDMA(lambda e: e.indirect_dma_start(
                    out=idx_f[0][:], out_offset=None, in_=tokmap_ar,
                    in_offset=IndirectOffsetOnAxis(idx2_u[:, 0:1], axis=0),
                    bounds_check=RB['tok'], oob_is_err=False),
                    after=[t2, fr("idxm"), t_ln2ag])
                FREE["idx2"] = t3
                t4 = O("vector", lambda e: e.tensor_copy(out=idx_u[0][:],
                                                         in_=idx_f[0][:]),
                       after=[t3])
                t5 = O("vector", lambda e, st=st: e.memset(
                    bufrow_bf[:, st * D:(st + 1) * D], 0.0),
                    after=[fr(("brow", st)), t_ln2ag])
                t6 = GDMA(lambda e, st=st: e.indirect_dma_start(
                    out=bufrow_bf[:, st * D:(st + 1) * D],
                    out_offset=None, in_=ln2_ag,
                    in_offset=IndirectOffsetOnAxis(idx_u[0][:, 0:1],
                                                   axis=0),
                    bounds_check=RB['ln'], oob_is_err=False),
                    after=[t4, t5])
                FREE["idxm"] = t6
                for m in range(8):
                    ttr = O("tensor", lambda e, st=st, m=m: e.transpose(
                        out=pmT1b[:, 0:128],
                        in_=bufrow_bf[:, st * D + m * 128:
                                      st * D + (m + 1) * 128],
                        identity=identb_sb[:]),
                        after=[t6, fr("pmT1")])
                    t_bufT = O("vector", lambda e, st=st, m=m:
                               e.tensor_copy(
                                   out=bufT_bf[:, m * GSL + st * 128:
                                               m * GSL + (st + 1) * 128],
                                   in_=pmT1b[:, 0:128]), after=[ttr])
                    FREE["pmT1"] = t_bufT
                FREE[("brow", st)] = P.tag("tensor")
            ST["cur"] = "moe1"
            for m in range(32):
                cq = m % 2
                tcf = GDMA(lambda e, m=m, cq=cq: e.dma_start(
                    out=cfc1_bf[cq][:],
                    in_=CFC[:, m * 128:(m + 1) * 128]
                    .rearrange("(k p) c -> p k c", p=128)),
                    after=[fr(("cfc1", cq)), fr(("cfc2", 0))])
                for nsb in range(2):
                    pmk = "pmA" if nsb == 0 else "pmB"
                    pm = pmA if nsb == 0 else pmB
                    tmm = None
                    for kt in range(8):
                        tmm = O("tensor",
                                lambda e, kt=kt, nsb=nsb, pm=pm, cq=cq:
                                e.matmul(
                                    out=pm[:],
                                    lhsT=cfc1_bf[cq][
                                        :, kt * 128:(kt + 1) * 128],
                                    rhs=bufT_bf[:, kt * GSL + nsb * 512:
                                                kt * GSL +
                                                (nsb + 1) * 512],
                                    start=(kt == 0), stop=(kt == 7)),
                                after=([tcf, t_bufT, fr(pmk)]
                                       if kt == 0 else ()))
                    FREE[("cfc1", cq)] = tmm
                    trl = O("vector", lambda e, pm=pm: e.tensor_scalar(
                        out=ev_sb[:], in0=pm[:], scalar1=0.0,
                        scalar2=None, op0=ALU.max),
                        after=[tmm, fr("evh")])
                    FREE[pmk] = trl
                    t_mm1_done = O("scalar",
                                   lambda e, m=m, nsb=nsb: e.activation(
                                       out=hmoe_bf[
                                           :, m * GSL + nsb * 512:
                                           m * GSL + (nsb + 1) * 512],
                                       in_=ev_sb[:], func=AF.Square),
                                   after=[trl, t_mm2_done])
                    FREE["evh"] = t_mm1_done
            ST["cur"] = "moe2"
            for nsb in range(2):
                for m in range(8):
                    cq = m % 2
                    tcp = GDMA(lambda e, m=m, cq=cq: e.dma_start(
                        out=cfc2_bf[cq][:],
                        in_=CPR[:, m * 128:(m + 1) * 128]
                        .rearrange("(k p) c -> p k c", p=128)),
                        after=[fr(("cfc2", cq))] +
                              ([fr(("cfc1", 0)), fr(("cfc1", 1))]
                               if cq == 0 else []))
                    tmm = None
                    for kt in range(32):
                        tmm = O("tensor",
                                lambda e, kt=kt, nsb=nsb, cq=cq: e.matmul(
                                    out=pmB[:],
                                    lhsT=cfc2_bf[cq][
                                        :, kt * 128:(kt + 1) * 128],
                                    rhs=hmoe_bf[:, kt * GSL + nsb * 512:
                                                kt * GSL +
                                                (nsb + 1) * 512],
                                    start=(kt == 0), stop=(kt == 31)),
                                after=([tcp, t_mm1_done, fr("pmB")]
                                       if kt == 0 else ()))
                    FREE[("cfc2", cq)] = tmm
                    t_mm2_done = tmm
                    tev = O("vector", lambda e: e.tensor_copy(
                        out=ev_sb[:], in_=pmB[:]),
                        after=[tmm, fr("evh")])
                    FREE["pmB"] = tev
                    for st in range(4):
                        ttr = O("tensor", lambda e, st=st: e.transpose(
                            out=pmT1[:],
                            in_=ev_sb[:, st * 128:(st + 1) * 128],
                            identity=ident_sb[:]),
                            after=[tev, fr("pmT1")])
                        tbw = O("vector", lambda e, st=st, m=m:
                                e.tensor_copy(
                                    out=big3[:, st * 1024 + m * 128:
                                             st * 1024 + (m + 1) * 128],
                                    in_=pmT1[:]),
                                after=[ttr, t_eo_store])
                        FREE["pmT1"] = tbw
                    FREE["evh"] = P.tag("tensor")
                tvl = P.tag("vector")
                for st in range(4):
                    srow = g * GSL + nsb * 512 + st * 128
                    t_eo_store = DMA(
                        lambda e, srow=srow, st=st: e.dma_start(
                            out=eout_in[srow:srow + 128, :],
                            in_=big3[:, st * 1024:(st + 1) * 1024]),
                        after=[tvl])
        t_eoag = CC(lambda e: e.collective_compute(
            "AllGather", ALU.bypass, replica_groups=rg,
            ins=[eout_in.opt()], outs=[eout_ag.opt()]),
            after=[t_eo_store])

        ST["cur"] = "comb"
        serial["last"] = t_eoag
        for b in range(NB):
            for k in range(2):
                col = k * NB + b
                SER("vector", lambda e, col=col: e.tensor_scalar(
                    out=idx_f[0][:], in0=rank_sb[:, col:col + 1],
                    scalar1=-1.0, scalar2=0.0, op0=ALU.add, op1=ALU.max))
                SER("vector", lambda e: e.tensor_scalar(
                    out=idx_f[0][:], in0=idx_f[0][:],
                    scalar1=float(CAP - 1), scalar2=None, op0=ALU.min))
                SER("vector", lambda e, col=col: e.scalar_tensor_tensor(
                    out=idx_f[0][:], in0=esel_sb[:, col:col + 1],
                    scalar=float(CAP), in1=idx_f[0][:],
                    op0=ALU.mult, op1=ALU.add))
                SER("vector", lambda e: e.tensor_copy(out=idx_u[0][:],
                                                      in_=idx_f[0][:]))
                dstt = sq_blk if k == 0 else zt_blk
                SER("gpsimd", lambda e, dstt=dstt: e.indirect_dma_start(
                    out=dstt[:], out_offset=None, in_=eout_ag,
                    in_offset=IndirectOffsetOnAxis(idx_u[0][:, 0:1],
                                                   axis=0),
                    bounds_check=RB['tok'], oob_is_err=False), 16)
            SER("vector", lambda e, b=b: e.tensor_scalar(
                out=xc_b[:, 0:D], in0=sq_blk[:],
                scalar1=probs_sb[:, b:b + 1], scalar2=None, op0=ALU.mult))
            SER("vector", lambda e, b=b: e.scalar_tensor_tensor(
                out=xc_b[:, 0:D], in0=zt_blk[:],
                scalar=probs_sb[:, NB + b:NB + b + 1], in1=xc_b[:, 0:D],
                op0=ALU.mult, op1=ALU.add))
            SER("sync", lambda e, b=b: e.dma_start(
                out=xblk[:], in_=x2_dram[b * 128:(b + 1) * 128, :]), 16)
            SER("vector", lambda e: e.tensor_add(xc_b[:, 0:D],
                                                 xc_b[:, 0:D], xblk[:]))
            SER("sync", lambda e, b=b: e.dma_start(
                out=OUT[b * 128:(b + 1) * 128, :], in_=xc_b[:, 0:D]), 16)

        SEGV = 30000
        segs = {}
        for e2 in ENGS:
            base = [0]
            cum = 0
            bases = []
            for (fn, inc, waits) in P.ops[e2]:
                if cum - base[-1] >= SEGV:
                    base.append(cum)
                bases.append((len(base) - 1, base[-1]))
                cum += inc
            totals = []
            for i2 in range(len(base)):
                end = base[i2 + 1] if i2 + 1 < len(base) else cum
                totals.append(end - base[i2])
            assert len(base) <= NSEG, (e2, len(base))
            segs[e2] = (base, bases, totals, cum)

        def wait_on(engine, te, tv):
            base = segs[te][0]
            si = 0
            for i2 in range(len(base)):
                if tv > base[i2]:
                    si = i2
            engine.wait_ge(sems[te][si], tv - base[si])

        def run_engine(engine, name):
            bases = segs[name][1]
            for i2, (fn, inc, waits) in enumerate(P.ops[name]):
                for te, tv in waits:
                    wait_on(engine, te, tv)
                si, b0 = bases[i2]
                fn(engine).then_inc(sems[name][si], inc)
            for te in ENGS:
                b2, _, t2, c2 = segs[te]
                for i3 in range(len(b2)):
                    engine.wait_ge(sems[te][i3], t2[i3])

        @block.sync
        def _(sync):
            run_engine(sync, "sync")

        @block.vector
        def _(vector):
            run_engine(vector, "vector")

        @block.scalar
        def _(scalar):
            run_engine(scalar, "scalar")

        @block.gpsimd
        def _(gpsimd):
            RB['y'] = gpsimd.alloc_register("rb_y")
            RB['tok'] = gpsimd.alloc_register("rb_tok")
            RB['ln'] = gpsimd.alloc_register("rb_ln")
            gpsimd.reg_mov(RB['y'], DIN * 32 - 1)
            gpsimd.reg_mov(RB['tok'], E * CAP - 1)
            gpsimd.reg_mov(RB['ln'], T - 1)
            run_engine(gpsimd, "gpsimd")

        @block.tensor
        def _(tensor):
            run_engine(tensor, "tensor")

    return nc


_cache = {}


def get_program(mode="full", strict=(), niter=NITER):
    key = (mode, tuple(sorted(strict)), niter)
    if key not in _cache:
        _cache[key] = build_program(mode, strict, niter)
    return _cache[key]


def make_in_maps(inputs):
    x = np.ascontiguousarray(
        np.asarray(inputs["L0_attn_x"], dtype=np.float32).reshape(T, D))
    w_in = np.asarray(inputs["L0_attn_W_in"], dtype=np.float32)
    conv_w = np.asarray(inputs["L0_attn_conv_w"], dtype=np.float32)
    conv_b = np.asarray(inputs["L0_attn_conv_b"], dtype=np.float32)
    w_x = np.asarray(inputs["L0_attn_W_x"], dtype=np.float32)
    w_dt = np.asarray(inputs["L0_attn_W_dt"], dtype=np.float32)
    b_dt = np.asarray(inputs["L0_attn_b_dt"], dtype=np.float32)
    a_log = np.asarray(inputs["L0_attn_A_log"], dtype=np.float32)
    dp = np.asarray(inputs["L0_attn_Dp"], dtype=np.float32)
    w_out = np.ascontiguousarray(
        np.asarray(inputs["L0_attn_W_out"], dtype=np.float32))
    w_g = np.asarray(inputs["L0_ffn_w_g"], dtype=np.float32)
    c_fc = np.asarray(inputs["L0_ffn_c_fc"], dtype=np.float32)
    c_proj = np.asarray(inputs["L0_ffn_c_proj"], dtype=np.float32)

    in_maps = []
    for c in range(NC):
        dsl = slice(c * DL, (c + 1) * DL)
        m = {
            "x_shard": np.ascontiguousarray(x[c * TL:(c + 1) * TL, :]),
            "w_in_loc": np.ascontiguousarray(
                np.concatenate([w_in[:, dsl], w_in[:, DIN + c * DL:
                                                  DIN + (c + 1) * DL]],
                               axis=1)),
            "conv_w_loc": np.ascontiguousarray(conv_w[dsl, 0, :]),
            "conv_b_loc": np.ascontiguousarray(conv_b[dsl].reshape(DL, 1)),
            "w_x_loc": np.ascontiguousarray(w_x[dsl, :]),
            "w_dt_loc": np.ascontiguousarray(np.concatenate(
                [w_dt[:, dsl], b_dt[dsl].reshape(1, DL)], axis=0)),
            "b_dt_loc": np.ascontiguousarray(b_dt[dsl].reshape(DL, 1)),
            "a_log_loc": np.ascontiguousarray(a_log[dsl, :]),
            "dp_loc": np.ascontiguousarray(dp[dsl].reshape(DL, 1)),
            "w_out": w_out,
            "w_gT": np.ascontiguousarray(w_g.T),
            "c_fc_loc": np.ascontiguousarray(c_fc[c]),
            "c_proj_loc": np.ascontiguousarray(c_proj[c]),
            "tokid": np.ascontiguousarray(
                np.arange(c * TL, (c + 1) * TL, dtype=np.float32)
                .reshape(TL, 1)),
            "prevmask": np.ascontiguousarray(
                (np.arange(NC) < c).astype(np.float32).reshape(NC, 1)),
            "rankoff": np.array([[float(c)]], dtype=np.float32),
        }
        in_maps.append(m)
    return in_maps


def kernel(**inputs):
    nc = get_program("full", ("a3g", "a3m", "a3s"))
    in_maps = make_in_maps(inputs)
    res = run_bass_kernel_spmd(nc, in_maps, core_ids=list(range(NC)))
    out = np.concatenate(
        [np.asarray(res.results[c]["out"]) for c in range(NC)], axis=0)
    return out.reshape(1, T, D).astype(np.float32)
